# version 07
# baseline (speedup 1.0000x reference)
import numpy as np
import ml_dtypes
from contextlib import ExitStack

import concourse.bass as bass
import concourse.mybir as mybir
from concourse.bass_utils import run_bass_kernel_spmd
from concourse.alu_op_type import AluOpType as OP

F32 = mybir.dt.float32
BF16 = mybir.dt.bfloat16
AF = mybir.ActivationFunctionType
AX = mybir.AxisListType.X

D = 1024
KC = 8


class Buf:
    __slots__ = ("name", "writer", "readers")

    def __init__(self, name):
        self.name = name
        self.writer = None
        self.readers = {}


class KB:
    COMPUTE = {"t": 1, "v": 1, "s": 1, "g": 1}
    DMA_SEM = {"q": "dq", "s": "ds", "g": "dg", "v": "dv", "t": "dt"}

    def __init__(self, nc, T):
        self.nc = nc
        self.T = T
        self.TC = T // 128
        self.TT = T // 512
        self.st = ExitStack()
        self.eng = None
        self.sems = {}
        self.cnt = {}
        self.obs = {}
        self.bufs = {}
        self.q = {e: [] for e in ("t", "v", "s", "g", "q")}

    def sem(self, name):
        if name not in self.sems:
            self.sems[name] = self.st.enter_context(self.nc.semaphore(name))
            self.cnt[name] = 0
        return self.sems[name]

    def S(self, name, shape, dtype):
        h = self.st.enter_context(self.nc.sbuf_tensor(name, shape, dtype))
        self.bufs[name] = Buf(name)
        return h

    def PS(self, name, shape, dtype):
        h = self.st.enter_context(self.nc.psum_tensor(name, shape, dtype))
        self.bufs[name] = Buf(name)
        return h

    def buf(self, name):
        return self.bufs[name]

    def _wait(self, e, sem_name, val):
        if val <= 0:
            return
        key = (e, sem_name)
        if self.obs.get(key, 0) >= val:
            return
        self.q[e].append(("w", sem_name, val))
        self.obs[key] = val

    SELF_SYNC = True

    def _pre(self, e, outs, ins, is_compute):
        skip_own = is_compute and (e == "t" or not self.SELF_SYNC)
        own = e if skip_own else None
        for b in ins:
            if b.writer:
                ws, wc = b.writer
                if ws != own:
                    self._wait(e, ws, wc)
        for b in outs:
            if b.writer:
                ws, wc = b.writer
                if ws != own:
                    self._wait(e, ws, wc)
            for rs, rc in b.readers.items():
                if rs != own:
                    self._wait(e, rs, rc)

    def _post(self, sem_name, outs, ins):
        c = self.cnt[sem_name]
        for b in ins:
            if b.readers.get(sem_name, 0) < c:
                b.readers[sem_name] = c
        for b in outs:
            b.writer = (sem_name, c)
            b.readers = {}

    def op(self, e, emit, outs=(), ins=(), inc=True):
        self._pre(e, outs, ins, True)
        if inc:
            self.cnt[e] += 1
            self.q[e].append(("i", emit, e, 1))
            self._post(e, outs, ins)
        else:
            self.q[e].append(("i", emit, None, 0))

    def group(self, e, emits, outs=(), ins_per=None, inc=True):
        all_ins = []
        last = len(emits) - 1
        for i, emit in enumerate(emits):
            ins = ins_per[i] if ins_per else ()
            all_ins += list(ins)
            self._pre(e, outs if i == 0 else (), ins, True)
            if i < last:
                self.q[e].append(("i", emit, None, 0))
        self.cnt[e] += 1
        self.q[e].append(("i", emits[last], e, 1))
        self._post(e, outs, all_ins)

    def dma(self, q, out_ap, in_ap, outs=(), ins=(), **kw):
        assert outs, "dma needs a destination buf"
        sem_name = "D" + outs[0].name
        self.sem(sem_name)
        self._pre(q, outs, ins, False)
        self.cnt[sem_name] += 16
        self.q[q].append(("i", lambda e, o=out_ap, i_=in_ap, kw=kw: e.dma_start(out=o, in_=i_, **kw),
                          sem_name, 16))
        self._post(sem_name, outs, ins)

    def dmat(self, q, out_ap, in_ap, outs=(), ins=()):
        assert outs, "dmat needs a destination buf"
        sem_name = "D" + outs[0].name
        self.sem(sem_name)
        self._pre(q, outs, ins, False)
        self.cnt[sem_name] += 16
        self.q[q].append(("i", lambda e, o=out_ap, i_=in_ap: e.dma_start_transpose(o, i_),
                          sem_name, 16))
        self._post(sem_name, outs, ins)

    def barrier_engine(self, e):
        for o, c in self.cnt.items():
            if o != e and c > 0:
                self._wait(e, o, c)


def ts(i, w=128):
    return slice(i * w, (i + 1) * w)


class Prog:

    def __init__(self, nc, T, blocks, io_in, io_out):
        self.nc = nc
        self.T = T
        self.TC = T // 128
        self.TT = T // 512
        self.blocks = blocks
        self.io_in = io_in
        self.io_out = io_out
        self.kb = KB(nc, T)

    def declare(self):
        nc, T = self.nc, self.T
        kb = self.kb
        bl = self.blocks
        self.dram = {}

        def din(name, shape, dtype=F32):
            self.dram[name] = nc.dram_tensor(name, shape, dtype, kind="ExternalInput").ap()

        def dscr(name, shape, dtype=BF16):
            self.dram[name] = nc.dram_tensor(name, shape, dtype, kind="Internal").ap()
            self.kb.bufs[name] = Buf(name)

        if self.io_in == "tm":
            din("x_in", [T, D])
        else:
            din("x_in_fm", [KC * 128, T])
        if self.io_out == "tm":
            self.dram["y_out"] = nc.dram_tensor("y_out", [T, D], F32, kind="ExternalOutput").ap()
            self.kb.bufs["y_out"] = Buf("y_out")
        else:
            self.dram["y_out_fm"] = nc.dram_tensor("y_out_fm", [KC * 128, T], F32, kind="ExternalOutput").ap()
            self.kb.bufs["y_out_fm"] = Buf("y_out_fm")

        din("ident_f", [128, 128])
        din("ident_b", [128, 128], BF16)
        din("ones_f", [128, 128])
        din("ones_b", [128, 128], BF16)
        din("ut_f", [128, 128])
        din("R2", [128, 128], BF16)
        din("cos_t", [128, T])
        din("sin_t", [128, T])
        din("t_bc", [128, T])
        din("constc", [128, 8])
        din("sel8", [8, 1024], BF16)
        if "b0" in bl:
            din("w_qkv", [3 * D, D])
            din("bqc", [128, 24])
            din("w_out", [D, D])
            din("boc", [128, 8])
            dscr("wqkv_bf", [3 * D, D])
            dscr("wout_bf", [D, D])
        if "b1" in bl:
            din("wq1", [D, D]); din("wk1", [D, D]); din("wv1", [D, D]); din("wo1", [D, D])
            din("w_router", [D, 8])
            din("wg", [8, D, 2752]); din("wu", [8, D, 2752]); din("wd", [8, 2752, D])
            din("g1c", [128, 8]); din("g2c", [128, 8])
            dscr("wd_bf", [8, 2752, D])
            dscr("wgu_t", [704 * 128, 1024])

            dscr("wq1_bf", [D, D]); dscr("wk1_bf", [D, D])
            dscr("wv1_bf", [D, D]); dscr("wo1_bf", [D, D])
        if "b2" in bl:
            din("Wr", [D, D]); din("Wk", [D, D]); din("Wv", [D, D]); din("Wo", [D, D])
            din("decc", [128, 8]); din("lnwc", [128, 8])
            dscr("Wr_bf", [D, D]); dscr("Wk_bf", [D, D]); dscr("Wv_bf", [D, D]); dscr("Wo_bf", [D, D])
        if "b3" in bl:
            din("w1w", [4096, D]); din("w2w", [4096, D])
            din("h1w", [4096, 4096]); din("h2w", [4096, 4096])
            din("w3w", [D, 4096])
            din("s1c", [128, 32]); din("s2c", [128, 32])
            din("hs1c", [128, 32]); din("hs2c", [128, 32]); din("s3c", [128, 8])
            dscr("w1_bf", [4096, D]); dscr("w2_bf", [4096, D])
            dscr("h1_bf", [4096, 4096]); dscr("h2_bf", [4096, 4096])
            dscr("w3_bf", [D, 4096])
        for n in ("s1_d", "xn2_d", "hin_d", "s2_d", "s3_d"):
            dscr(n, [KC * 128, T], F32)

        S = kb.S
        self.xs = [S(f"xs{k}", [128, T], F32) for k in range(KC)]
        NPOOL = 12
        self.pool = [S(f"pl{i}", [128, T], BF16) for i in range(NPOOL)]
        self.scr = [S(f"scr{i}", [128, T], F32) for i in range(4)]
        self.stg = [S(f"stg{i}", [128, max(2 * T, 4096)], BF16) for i in range(3)]
        self.vs2 = [S(f"vs{i}", [128, 2080 * T // 2048], BF16) for i in range(2)]
        self.cid = {}
        for n in ("ident_f", "ones_f", "ut_f"):
            self.cid[n] = S(n + "_s", [128, 128], F32)
        for n in ("ident_b", "ones_b", "R2"):
            self.cid[n] = S(n + "_s", [128, 128], BF16)
        self.sel_s = S("sel_s", [8, 1024], BF16)
        self.vecs = S("vecs", [128, 360], F32)
        self.vecmap = {}
        self.vec_used = 0
        self.rows = {n: S("row_" + n, [1, T], F32)
                     for n in ("mean", "rstd")}
        self.rr = {n: kb.buf("row_" + n) for n in self.rows}
        self.pb = [kb.PS(f"pb{i}", [128, 512], F32) for i in range(6)]
        self.pt = [kb.PS(f"pt{i}", [128, 1024], BF16) for i in range(2)]
        self.pbb = [kb.buf(f"pb{i}") for i in range(6)]
        self.ptb = [kb.buf(f"pt{i}") for i in range(2)]
        for i in range(4):
            kb.bufs[f"prepslot{i}"] = Buf(f"prepslot{i}")
        for e in range(8):
            kb.bufs[f"wgu_t_e{e}"] = Buf(f"wgu_t_e{e}")
        self.Ssf = S("Ssf", [128, 65], F32)
        self.Ssb = S("Ssb", [128, 65], BF16)
        self.small = [S(f"sm{i}", [128, 16], F32) for i in range(4)]
        self.pa = S("pa", [128, 256], BF16)
        if "b1" in bl:
            self.rt_s = S("rt_s", [128, 64], BF16)
            self.lg8 = S("lg8", [8, 512], F32)
            self.cfm = S("cfm", [8, T], BF16)
            self.ctm = S("ctm", [128, 128], F32)

    def vslot(self, name, ncols):
        if name not in self.vecmap:
            self.vecmap[name] = (self.vec_used, ncols)
            self.vec_used += ncols
            assert self.vec_used <= 360
        off, n = self.vecmap[name]
        return self.vecs[:, off:off + ncols]

    def B(self, name):
        return self.kb.buf(name)

    def load_vec(self, q, name, src_ap, ncols):
        ap = self.vslot(name, ncols)
        self.kb.dma(q, ap, src_ap, outs=(self.B("vecs"),))
        return ap

    def vec(self, name, col=None):
        off, n = self.vecmap[name]
        if col is None:
            return self.vecs[:, off:off + n]
        return self.vecs[:, off + col:off + col + 1]

    def ln_stats(self, tag, eps, rms=False, src=None, src_bufs=None, sq_eng="g"):
        kb, TT, T = self.kb, self.TT, self.T
        src = src or self.xs
        src_bufs = src_bufs or [self.B(f"xs{k}") for k in range(KC)]
        xc = self.vs2[0]
        xcB = self.B("vs0")
        sq = self.scr[3].bitcast(BF16)
        sqB = self.B("scr3")
        onesb = self.cid["ones_b"]
        onesbB = self.B("ones_b_s")
        per = 2 if TT >= 2 else 1
        nh = TT // per if TT >= 2 else 1
        for h in range(nh):
            for k in range(KC):
                c0 = (h * per) * 512
                w = per * 512
                sl = slice(c0, c0 + w)
                lsl = slice(0, w)
                kb.op("v", lambda e, k=k, sl=sl, lsl=lsl: e.tensor_copy(xc[:, lsl], src[k][:, sl]),
                      outs=(xcB,), ins=(src_bufs[k],))
                kb.op(sq_eng, lambda e, lsl=lsl: e.tensor_tensor(sq[:, lsl], xc[:, lsl], xc[:, lsl], OP.mult),
                      outs=(sqB,), ins=(xcB,))
                for j in range(per):
                    tt = h * per + j
                    if not rms:
                        kb.op("t", lambda e, k=k, j=j: e.matmul(
                            self.pb[j][0:1, :], onesb[:, 0:1], xc[:, ts(j, 512)],
                            start=(k == 0), stop=(k == KC - 1)),
                            outs=(self.pbb[j],), ins=(xcB, onesbB))
                    kb.op("t", lambda e, k=k, j=j: e.matmul(
                        self.pb[2 + j][0:1, :], onesb[:, 0:1], sq[:, ts(j, 512)],
                        start=(k == 0), stop=(k == KC - 1)),
                        outs=(self.pbb[2 + j],), ins=(sqB, onesbB))
            for j in range(per):
                tt = h * per + j
                R = self.rows
                RR = self.rr
                epscol = 0 if eps == 1e-5 else 1
                epsap = self.vec("cst", epscol)[0:1, :]
                bsum, bsq = self.pb[j], self.pb[2 + j]
                bsumB, bsqB = self.pbb[j], self.pbb[2 + j]
                if rms:
                    kb.op("v", lambda e, bsq=bsq: e.tensor_scalar(
                        bsq[0:1, :], bsq[0:1, :], 1.0 / D, None, OP.mult),
                        outs=(bsqB,), ins=(bsqB,))
                    kb.op("s", lambda e, bsq=bsq: e.activation(
                        bsq[0:1, :], bsq[0:1, :], AF.Ln, bias=epsap, scale=1.0),
                        outs=(bsqB,), ins=(bsqB, self.B("vecs")))
                    kb.op("s", lambda e, tt=tt, bsq=bsq: e.activation(
                        R["rstd"][:, ts(tt, 512)], bsq[0:1, :], AF.Exp, scale=-0.5),
                        outs=(RR["rstd"],), ins=(bsqB,))
                else:
                    kb.op("v", lambda e, tt=tt, bsum=bsum: e.tensor_scalar(
                        R["mean"][:, ts(tt, 512)], bsum[0:1, :], 1.0 / D, None, OP.mult),
                        outs=(RR["mean"],), ins=(bsumB,))
                    kb.op("v", lambda e, bsq=bsq: e.tensor_scalar(
                        bsq[0:1, :], bsq[0:1, :], 1.0 / D, None, OP.mult),
                        outs=(bsqB,), ins=(bsqB,))
                    m2 = self.scr[2]
                    m2B = self.B("scr2")
                    kb.op("v", lambda e, tt=tt, m2=m2: e.tensor_tensor(
                        m2[0:1, 0:512], R["mean"][:, ts(tt, 512)], R["mean"][:, ts(tt, 512)], OP.mult),
                        outs=(m2B,), ins=(RR["mean"],))
                    kb.op("v", lambda e, m2=m2, bsq=bsq: e.tensor_tensor(
                        bsq[0:1, :], bsq[0:1, :], m2[0:1, 0:512], OP.subtract),
                        outs=(bsqB,), ins=(bsqB, m2B))
                    kb.op("s", lambda e, bsq=bsq: e.activation(
                        bsq[0:1, :], bsq[0:1, :], AF.Ln, bias=epsap, scale=1.0),
                        outs=(bsqB,), ins=(bsqB, self.B("vecs")))
                    kb.op("s", lambda e, tt=tt, bsq=bsq: e.activation(
                        R["rstd"][:, ts(tt, 512)], bsq[0:1, :], AF.Exp, scale=-0.5),
                        outs=(RR["rstd"],), ins=(bsqB,))

    def bc_rows(self, tt, rows=("mean", "rstd"), banks=(4, 5)):
        kb = self.kb
        onesf = self.cid["ones_f"]
        onesfB = self.B("ones_f_s")
        out = []
        for rn, bi in zip(rows, banks):
            kb.op("t", lambda e, rn=rn, bi=bi: e.matmul(
                self.pb[bi][:, :], onesf[0:1, :], self.rows[rn][:, ts(tt, 512)],
                start=True, stop=True),
                outs=(self.pbb[bi],), ins=(self.rr[rn], onesfB))
            out.append(self.pb[bi])
        return out

    def emit_input(self):
        kb, T, TC = self.kb, self.T, self.TC
        d = self.dram
        for n in ("ident_f", "ones_f", "ut_f", "ident_b", "ones_b", "R2"):
            kb.dma("q", self.cid[n][:], d[n], outs=(self.B(n + "_s"),))
        kb.dma("q", self.sel_s[:], d["sel8"], outs=(self.B("sel_s"),))
        self.load_vec("q", "cst", d["constc"], 8)
        if self.io_in == "fm":
            for k in range(KC):
                kb.dma("q", self.xs[k][:], d["x_in_fm"][ts(k), :], outs=(self.B(f"xs{k}"),))
            return
        identf = self.cid["ident_f"]
        idB = self.B("ident_f_s")
        for tc in range(TC):
            stgv = self.stg[tc % 2].bitcast(F32)
            stgB = self.B(f"stg{tc % 2}")
            kb.dma("q", stgv[:, 0:D], d["x_in"][ts(tc), :], outs=(stgB,))
            for k in range(KC):
                b = 4 + (k % 2)
                kb.op("t", lambda e, k=k, b=b, stgv=stgv: e.transpose(
                    self.pb[b][:, 0:128], stgv[:, ts(k)], identf[:]),
                    outs=(self.pbb[b],), ins=(stgB, idB))
                kb.op("v", lambda e, k=k, tc=tc, b=b: e.tensor_copy(
                    self.xs[k][:, ts(tc)], self.pb[b][:, 0:128]),
                    outs=(self.B(f"xs{k}"),), ins=(self.pbb[b],))

    def emit_output(self):
        kb, T, TC = self.kb, self.T, self.TC
        d = self.dram
        if self.io_out == "fm":
            for k in range(KC):
                kb.dma("s", d["y_out_fm"][ts(k), :], self.xs[k][:],
                       outs=(self.B("y_out_fm"),), ins=(self.B(f"xs{k}"),))
            return
        identf = self.cid["ident_f"]
        idB = self.B("ident_f_s")
        for tc in range(TC):
            stgv = self.stg[tc % 2].bitcast(F32)
            stgB = self.B(f"stg{tc % 2}")
            for k in range(KC):
                b = 4 + (k % 2)
                kb.op("t", lambda e, k=k, tc=tc, b=b: e.transpose(
                    self.pb[b][:, 0:128], self.xs[k][:, ts(tc)], identf[:]),
                    outs=(self.pbb[b],), ins=(self.B(f"xs{k}"), idB))
                kb.op("v", lambda e, k=k, b=b, stgv=stgv: e.tensor_copy(
                    stgv[:, ts(k)], self.pb[b][:, 0:128]),
                    outs=(stgB,), ins=(self.pbb[b],))
            kb.dma("s", d["y_out"][ts(tc), :], stgv[:, 0:D],
                   outs=(self.B("y_out"),), ins=(stgB,))

    def emit_b0(self):
        kb, T, TC, TT = self.kb, self.T, self.TC, self.TT
        d = self.dram
        for i in range(3):
            kb.dma("g", d["wqkv_bf"][ts(i, D), :], d["w_qkv"][ts(i, D), :],
                   outs=(self.B("wqkv_bf"),))
        kb.dma("g", d["wout_bf"], d["w_out"], outs=(self.B("wout_bf"),))
        self.load_vec("q", "bqc", d["bqc"], 24)
        self.load_vec("q", "boc", d["boc"], 8)

        self.ln_stats("ln1", 1e-5)
        xn1 = self.pool[0:8]
        xn1B = [self.B(f"pl{i}") for i in range(8)]
        for tt in range(TT):
            self.bc_rows(tt)
            mb = self.scr[0]; mbB = self.B("scr0")
            rb = self.scr[1]; rbB = self.B("scr1")
            kb.op("v", lambda e, tt=tt: e.tensor_copy(mb[:, 0:512], self.pb[4][:, :]),
                  outs=(mbB,), ins=(self.pbb[4],))
            kb.op("v", lambda e, tt=tt: e.tensor_copy(rb[:, 0:512], self.pb[5][:, :]),
                  outs=(rbB,), ins=(self.pbb[5],))
            for k in range(KC):
                tmp = self.scr[2]
                tmpB = self.B("scr2")
                kb.op("g", lambda e, k=k, tt=tt: e.tensor_tensor(
                    tmp[:, ts(tt, 512)], self.xs[k][:, ts(tt, 512)], mb[:, 0:512], OP.subtract),
                    outs=(tmpB,), ins=(self.B(f"xs{k}"), mbB))
                kb.op("v", lambda e, k=k, tt=tt: e.tensor_tensor(
                    xn1[k][:, ts(tt, 512)], tmp[:, ts(tt, 512)], rb[:, 0:512], OP.mult),
                    outs=(xn1B[k],), ins=(tmpB, rbB))

        qp, kp, ktm, vfm = self.pool[8], self.pool[9], self.pool[10], self.pool[11]
        qpB, kpB, ktmB, vfmB = (self.B("pl8"), self.B("pl9"), self.B("pl10"), self.B("pl11"))
        atm = self.stg[2]
        atmB = self.B("stg2")
        identb = self.cid["ident_b"]
        idbB = self.B("ident_b_s")
        utf = self.cid["ut_f"]
        utfB = self.B("ut_f_s")
        SsfB, SsbB = self.B("Ssf"), self.B("Ssb")
        paB = self.B("pa")
        vsB = [self.B(f"vs{i}") for i in range(2)]

        for hp in range(8):
            stgw = self.stg[hp % 2]
            stgwB = self.B(f"stg{hp % 2}")
            for part in range(3):
                base = part * D + hp * 128
                for kc2 in range(KC):
                    col = part * D + kc2 * 128
                    kb.dmat("q", stgw[:, col:col + 128],
                            d["wqkv_bf"][base:base + 128, ts(kc2)],
                            outs=(stgwB,), ins=(self.B("wqkv_bf"),))
            for part, dst, dstB in ((0, qp, qpB), (1, kp, kpB), (2, vfm, vfmB)):
                bias = self.vec("bqc", part * 8 + hp)
                for tt in range(TT):
                    bank = tt % 4
                    emits = []
                    ins_per = []
                    for kc2 in range(KC):
                        emits.append(lambda e, kc2=kc2, tt=tt, bank=bank, part=part, stgw=stgw: e.matmul(
                            self.pb[bank][:, :],
                            stgw[:, slice(part * D + kc2 * 128, part * D + kc2 * 128 + 128)],
                            xn1[kc2][:, ts(tt, 512)],
                            start=(kc2 == 0), stop=(kc2 == KC - 1)))
                        ins_per.append((stgwB, xn1B[kc2]))
                    kb.group("t", emits, outs=(self.pbb[bank],), ins_per=ins_per)
                    if part < 2:
                        sc12 = self.stg[2].bitcast(F32)
                        sc12B = self.B("stg2")
                        co = self.T // 2
                        ee = sc12[:, co:co + 512]
                        rr2 = sc12[:, co + 512:co + 1024]
                        kb.op("s", lambda e, bank=bank, bias=bias, ee=ee: e.activation(
                            ee, self.pb[bank][:, :], AF.Exp, bias=bias, scale=1.0),
                            outs=(sc12B,), ins=(self.pbb[bank], self.B("vecs")))
                        kb.op("s", lambda e, bank=bank, bias=bias, rr2=rr2: e.activation(
                            rr2, self.pb[bank][:, :], AF.Relu, bias=bias, scale=1.0),
                            outs=(sc12B,), ins=(self.pbb[bank], self.B("vecs")))
                        kb.op("v", lambda e, tt=tt, dst=dst, ee=ee, rr2=rr2: e.scalar_tensor_tensor(
                            dst[:, ts(tt, 512)], ee, 1.0, rr2,
                            OP.min, OP.add),
                            outs=(dstB,), ins=(sc12B,))
                    else:
                        kb.op("v", lambda e, tt=tt, bank=bank, dst=dst, bias=bias: e.tensor_scalar(
                            dst[:, ts(tt, 512)], self.pb[bank][:, :], bias, None, OP.add),
                            outs=(dstB,), ins=(self.pbb[bank], self.B("vecs")))
            vs = self.vs2[hp % 2]
            vsb = vsB[hp % 2]
            kb.op("v", lambda e, vs=vs: e.memset(vs[:], 1.0), outs=(vsb,))
            for tc in range(TC):
                pt = self.pt[tc % 2]
                ptB = self.ptb[tc % 2]
                kb.op("t", lambda e, tc=tc, pt=pt: e.transpose(
                    pt[:, 0:128], kp[:, ts(tc)], identb[:]),
                    outs=(ptB,), ins=(kpB, idbB))
                kb.op("v", lambda e, tc=tc, pt=pt: e.tensor_copy(ktm[:, ts(tc)], pt[:, 0:128]),
                      outs=(ktmB,), ins=(ptB,))
                kb.op("t", lambda e, tc=tc, pt=pt: e.transpose(
                    pt[:, 128:256], vfm[:, ts(tc)], identb[:]),
                    outs=(ptB,), ins=(vfmB, idbB))
                kb.op("v", lambda e, tc=tc, pt=pt, vs=vs: e.tensor_copy(
                    vs[:, 130 * tc:130 * tc + 64], pt[:, 128:192]),
                    outs=(vsb,), ins=(ptB,))
                kb.op("v", lambda e, tc=tc, pt=pt, vs=vs: e.tensor_copy(
                    vs[:, 130 * tc + 65:130 * tc + 129], pt[:, 192:256]),
                    outs=(vsb,), ins=(ptB,))
            kb.op("v", lambda e: e.memset(self.Ssf[:], 0.0), outs=(SsfB,))
            kb.op("v", lambda e: e.memset(self.Ssb[:], 0.0), outs=(SsbB,))
            for tc in range(TC):
                for hl in range(2):
                    hs = slice(64 * hl, 64 * hl + 64)
                    bA = 0 if hl == 0 else 1
                    bO = 2 if hl == 0 else 3
                    kb.op("t", lambda e, tc=tc, hs=hs, bA=bA: e.matmul(
                        self.pb[bA][:, 0:128], kp[hs, ts(tc)], qp[hs, ts(tc)],
                        start=True, stop=True),
                        outs=(self.pbb[bA],), ins=(kpB, qpB))
                    pasl = slice(128 * hl, 128 * hl + 128)
                    kb.op("v", lambda e, bA=bA, pasl=pasl: e.tensor_tensor(
                        self.pa[:, pasl], self.pb[bA][:, 0:128], utf[:], OP.mult),
                        outs=(paB,), ins=(self.pbb[bA], utfB))
                    e1 = lambda e, tc=tc, hl=hl, bO=bO, pasl=pasl, vs=vs: e.matmul(
                        self.pb[bO][:, 0:65], self.pa[:, pasl],
                        vs[:, 130 * tc + 65 * hl: 130 * tc + 65 * hl + 65],
                        start=True, stop=False)
                    e2 = lambda e, tc=tc, hs=hs, bO=bO: e.matmul(
                        self.pb[bO][:, 0:65], qp[hs, ts(tc)], self.Ssb[hs, :],
                        start=False, stop=True)
                    kb.group("t", [e1, e2], outs=(self.pbb[bO],),
                             ins_per=[(paB, vsb), (qpB, SsbB)])
                    bS = 4 + hl
                    kb.op("t", lambda e, tc=tc, hl=hl, hs=hs, bS=bS, vs=vs: e.matmul(
                        self.pb[bS][hs, 0:65],
                        ktm[:, 128 * tc + 64 * hl:128 * tc + 64 * hl + 64],
                        vs[:, 130 * tc + 65 * hl: 130 * tc + 65 * hl + 65],
                        start=True, stop=True),
                        outs=(self.pbb[bS],), ins=(ktmB, vsb))
                    kb.op("v", lambda e, hs=hs, bS=bS: e.tensor_tensor(
                        self.Ssf[hs, :], self.Ssf[hs, :], self.pb[bS][hs, 0:65], OP.add),
                        outs=(SsfB,), ins=(SsfB, self.pbb[bS]))
                    kb.op("v", lambda e, hs=hs: e.tensor_copy(self.Ssb[hs, :], self.Ssf[hs, :]),
                          outs=(SsbB,), ins=(SsfB,))
                    nrm = self.small[0]
                    nrmB = self.B("sm0")
                    kb.op("v", lambda e, bO=bO: e.tensor_scalar(
                        nrm[:, 0:1], self.pb[bO][:, 64:65], 1e-6, None, OP.add),
                        outs=(nrmB,), ins=(self.pbb[bO],))
                    kb.op("v", lambda e: e.reciprocal(nrm[:, 1:2], nrm[:, 0:1]),
                          outs=(nrmB,), ins=(nrmB,))
                    kb.op("v", lambda e, tc=tc, hl=hl, bO=bO: e.tensor_scalar(
                        atm[:, 128 * tc + 64 * hl:128 * tc + 64 * hl + 64],
                        self.pb[bO][:, 0:64], nrm[:, 1:2], None, OP.mult),
                        outs=(atmB,), ins=(self.pbb[bO], nrmB))
            for tc in range(TC):
                pt = self.pt[tc % 2]
                ptB = self.ptb[tc % 2]
                kb.op("t", lambda e, tc=tc, pt=pt: e.transpose(
                    pt[:, 0:128], atm[:, ts(tc)], identb[:]),
                    outs=(ptB,), ins=(atmB, idbB))
                sc = self.scr[hp // 2].bitcast(BF16)
                scB = self.B(f"scr{hp // 2}")
                kb.op("v", lambda e, tc=tc, pt=pt, sc=sc, hp=hp: e.tensor_copy(
                    sc[:, (hp % 2) * T + 128 * tc:(hp % 2) * T + 128 * tc + 128], pt[:, 0:128]),
                    outs=(scB,), ins=(ptB,))

        for kc2 in range(KC):
            dst = self.stg[kc2 // 4]
            dstB = self.B(f"stg{kc2 // 4}")
            kb.dmat("q", dst[:, (kc2 % 4) * D:(kc2 % 4) * D + D],
                    d["wout_bf"][:, ts(kc2)], outs=(dstB,), ins=(self.B("wout_bf"),))
        for oc in range(KC):
            for tt in range(TT):
                bank = (oc * TT + tt) % 4
                emits, ins_per = [], []
                for kc2 in range(KC):
                    wv = self.stg[kc2 // 4]
                    wvB = self.B(f"stg{kc2 // 4}")
                    afm = self.scr[kc2 // 2].bitcast(BF16)
                    afmB = self.B(f"scr{kc2 // 2}")
                    emits.append(lambda e, kc2=kc2, oc=oc, tt=tt, bank=bank, wv=wv, afm=afm: e.matmul(
                        self.pb[bank][:, :],
                        wv[:, (kc2 % 4) * D + oc * 128:(kc2 % 4) * D + oc * 128 + 128],
                        afm[:, (kc2 % 2) * T + tt * 512:(kc2 % 2) * T + tt * 512 + 512],
                        start=(kc2 == 0), stop=(kc2 == KC - 1)))
                    ins_per.append((wvB, afmB))
                kb.group("t", emits, outs=(self.pbb[bank],), ins_per=ins_per)
                kb.op("v", lambda e, oc=oc, tt=tt, bank=bank: e.scalar_tensor_tensor(
                    self.xs[oc][:, ts(tt, 512)], self.pb[bank][:, :],
                    self.vec("boc", oc), self.xs[oc][:, ts(tt, 512)], OP.add, OP.add),
                    outs=(self.B(f"xs{oc}"),),
                    ins=(self.pbb[bank], self.B("vecs"), self.B(f"xs{oc}")))


    def emit_prep_wgu(self):
        kb, T = self.kb, self.T
        d = self.dram
        wg_r = d["wg"].rearrange("x (c p) h -> p (x c) h", c=8)
        wu_r = d["wu"].rearrange("x (c p) h -> p (x c) h", c=8)
        wgt = d["wgu_t"]
        wgtB = self.B("wgu_t")
        pslots = []
        for i in range(4):
            t = self.pool[9 + i // 2]
            ap = t[:, (i % 2) * 1024:(i % 2) * 1024 + 1024]
            pslots.append((ap, kb.buf(f"prepslot{i}"), self.B(f"pl{9 + i // 2}")))
        si = 0
        for e8 in range(8):
            for hz in range(22):
                w = 128 if hz < 21 else 64
                for gi, wr in ((0, wg_r), (1, wu_r)):
                    ap, sb, rb = pslots[si % 4]
                    si += 1
                    kb.dma("g", ap[:, 0:8 * w].rearrange("p (c h) -> p c h", c=8),
                           wr[:, 8 * e8:8 * e8 + 8, 128 * hz:128 * hz + w],
                           outs=(sb, rb), ins=())
                    row = ((e8 * 22 + hz) * 2 + gi) * 128
                    kb.dma("s", wgt[row:row + 128, 0:8 * w], ap[:, 0:8 * w],
                           outs=(kb.buf(f"wgu_t_e{e8}"),), ins=(sb,))

    def rms_rows_from_banks(self):
        kb = self.kb
        epsap = self.vec("cst", 1)[0:1, :]
        for tt in range(self.TT):
            b = self.pb[2 + tt]
            bB = self.pbb[2 + tt]
            kb.op("v", lambda e, b=b: e.tensor_scalar(b[0:1, :], b[0:1, :], 1.0 / D, None, OP.mult),
                  outs=(bB,), ins=(bB,))
            kb.op("s", lambda e, b=b: e.activation(b[0:1, :], b[0:1, :], AF.Ln, bias=epsap, scale=1.0),
                  outs=(bB,), ins=(bB, self.B("vecs")))
            kb.op("s", lambda e, tt=tt, b=b: e.activation(
                self.rows["rstd"][:, ts(tt, 512)], b[0:1, :], AF.Exp, scale=-0.5),
                outs=(self.rr["rstd"],), ins=(bB,))

    def emit_router(self, b2, poolB):
        kb, T, TC, TT = self.kb, self.T, self.TC, self.TT
        d = self.dram
        PB, PBB = self.pb, self.pbb
        smB = [self.B(f"sm{i}") for i in range(4)]
        rtB, lgB, ctmB, cfmB = self.B("rt_s"), self.B("lg8"), self.B("ctm"), self.B("cfm")
        sm0, sm1, sm2, sm3 = self.small
        identf = self.cid["ident_f"]
        idB = self.B("ident_f_s")
        for kc2 in range(KC):
            kb.dma("g", self.rt_s[:, 8 * kc2:8 * kc2 + 8], d["w_router"][ts(kc2), :], outs=(rtB,))
        for tt in range(TT):
            emits, ins_per = [], []
            for kc2 in range(KC):
                emits.append(lambda e, kc2=kc2, tt=tt: e.matmul(
                    PB[0][0:8, :], self.rt_s[:, 8 * kc2:8 * kc2 + 8], b2[kc2][:, ts(tt, 512)],
                    start=(kc2 == 0), stop=(kc2 == KC - 1)))
                ins_per.append((rtB, poolB[kc2]))
            kb.group("t", emits, outs=(PBB[0],), ins_per=ins_per)
            kb.op("v", lambda e: e.tensor_copy(self.lg8[:, :], PB[0][0:8, :]),
                  outs=(lgB,), ins=(PBB[0],))
            for tcl in range(4 if TT > 0 else 0):
                tc = tt * 4 + tcl
                if tc >= TC:
                    break
                kb.op("t", lambda e, tcl=tcl: e.transpose(
                    PB[1][:, 0:8], self.lg8[:, ts(tcl)], identf[0:8, 0:8]),
                    outs=(PBB[1],), ins=(lgB, idB))
                kb.op("v", lambda e: e.reduce_max(sm0[:, 0:1], PB[1][:, 0:8], axis=AX, negate=True),
                      outs=(smB[0],), ins=(PBB[1],))
                kb.op("s", lambda e: e.activation(sm1[:, 0:8], PB[1][:, 0:8], AF.Exp,
                                                  bias=sm0[:, 0:1], scale=1.0),
                      outs=(smB[1],), ins=(PBB[1], smB[0]))
                kb.op("v", lambda e: e.reduce_sum(sm0[:, 1:2], sm1[:, 0:8], axis=AX),
                      outs=(smB[0],), ins=(smB[1],))
                kb.op("v", lambda e: e.reciprocal(sm0[:, 2:3], sm0[:, 1:2]),
                      outs=(smB[0],), ins=(smB[0],))
                kb.op("v", lambda e: e.tensor_scalar(sm1[:, 8:16], sm1[:, 0:8], sm0[:, 2:3], None, OP.mult),
                      outs=(smB[1],), ins=(smB[1], smB[0]))
                kb.op("v", lambda e: e.reduce_max(sm0[:, 3:4], sm1[:, 8:16], axis=AX),
                      outs=(smB[0],), ins=(smB[1],))
                kb.op("v", lambda e: e.tensor_scalar(sm2[:, 0:8], sm1[:, 8:16], sm0[:, 3:4], None, OP.is_ge),
                      outs=(smB[2],), ins=(smB[1], smB[0]))
                kb.op("v", lambda e: e.scalar_tensor_tensor(sm2[:, 8:16], sm2[:, 0:8], -1e30,
                                                            sm1[:, 8:16], OP.mult, OP.add),
                      outs=(smB[2],), ins=(smB[2], smB[1]))
                kb.op("v", lambda e: e.reduce_max(sm0[:, 4:5], sm2[:, 8:16], axis=AX),
                      outs=(smB[0],), ins=(smB[2],))
                kb.op("v", lambda e: e.tensor_scalar(sm2[:, 0:8], sm1[:, 8:16], sm0[:, 4:5], None, OP.is_ge),
                      outs=(smB[2],), ins=(smB[1], smB[0]))
                kb.op("v", lambda e: e.tensor_tensor(sm0[:, 5:6], sm0[:, 3:4], sm0[:, 4:5], OP.add),
                      outs=(smB[0],), ins=(smB[0],))
                kb.op("v", lambda e: e.reciprocal(sm0[:, 6:7], sm0[:, 5:6]),
                      outs=(smB[0],), ins=(smB[0],))
                kb.op("v", lambda e: e.tensor_tensor(sm3[:, 0:8], sm1[:, 8:16], sm2[:, 0:8], OP.mult),
                      outs=(smB[3],), ins=(smB[1], smB[2]))
                kb.op("v", lambda e, tc=tc: e.tensor_scalar(
                    self.ctm[:, 8 * tc:8 * tc + 8], sm3[:, 0:8], sm0[:, 6:7], None, OP.mult),
                    outs=(ctmB,), ins=(smB[3], smB[0]))
        for tc in range(TC):
            kb.op("t", lambda e, tc=tc: e.transpose(
                PB[1][0:8, 0:128], self.ctm[:, 8 * tc:8 * tc + 8], identf[:]),
                outs=(PBB[1],), ins=(ctmB, idB))
            kb.op("v", lambda e, tc=tc: e.tensor_copy(
                self.cfm[0:8, ts(tc)], PB[1][0:8, 0:128]),
                outs=(cfmB,), ins=(PBB[1],))

    def emit_moe(self, b2, poolB):
        kb, T, TC, TT = self.kb, self.T, self.TC, self.TT
        assert T == 2048, "MoE gu layout assumes T=2048"
        d = self.dram
        PB, PBB = self.pb, self.pbb
        xsB = [self.B(f"xs{k}") for k in range(KC)]
        scrB = [self.B(f"scr{i}") for i in range(4)]
        stgB = [self.B(f"stg{i}") for i in range(3)]
        vsB = [self.B("vs0"), self.B("vs1")]
        cfmB = self.B("cfm")
        selB = self.B("sel_s")
        wd_r = d["wd_bf"].rearrange("x h o -> (x h) o")
        wgt = d["wgu_t"]
        wgtB = self.B("wgu_t")
        NHZ = 22

        def hzw(hz):
            return 128 if hz < 21 else 64

        def gu_tile(t):
            if t < 16:
                xb = self.xs[t // 2].bitcast(BF16)
                c0 = T + (t % 2) * 1024
                return xb[:, c0:c0 + 1024], xsB[t // 2]
            u = t - 16
            return self.pool[8 + u // 2][:, (u % 2) * 1024:(u % 2) * 1024 + 1024], self.B(f"pl{8 + u // 2}")

        for half in range(2):
            h0 = half * 1024
            for e8 in range(8):
                for tt in range(2):
                    kb.op("t", lambda e, e8=e8, tt=tt, h0=h0: e.matmul(
                        PB[0][:, :], self.sel_s[0:8, ts(e8)], self.cfm[0:8, h0 + tt * 512:h0 + tt * 512 + 512],
                        start=True, stop=True),
                        outs=(PBB[0],), ins=(selB, cfmB))
                    kb.op("v", lambda e, tt=tt: e.tensor_copy(
                        self.vs2[1][:, ts(tt, 512)], PB[0][:, :]),
                        outs=(vsB[1],), ins=(PBB[0],))
                for hz in range(NHZ):
                    w = hzw(hz)
                    stg = self.stg[hz % 2]
                    sB = stgB[hz % 2]
                    row = (e8 * 22 + hz) * 2 * 128
                    eb = kb.buf(f"wgu_t_e{e8}")
                    kb.dma("q", stg[:, 0:8 * w], wgt[row:row + 128, 0:8 * w],
                           outs=(sB,), ins=(eb,))
                    kb.dma("s", stg[:, 1024:1024 + 8 * w], wgt[row + 128:row + 256, 0:8 * w],
                           outs=(sB,), ins=(eb,))
                    gt, gtB = gu_tile(hz)
                    for tt in range(2):
                        bk = 0 if (hz * 2 + tt) % 2 == 0 else 2
                        for which, bank, cb in ((0, bk, 0), (1, bk + 1, 1024)):
                            emits, ins_per = [], []
                            for kc2 in range(KC):
                                emits.append(lambda e, kc2=kc2, tt=tt, bank=bank, cb=cb, stg=stg, w=w, h0=h0: e.matmul(
                                    PB[bank][0:w, :], stg[:, cb + w * kc2:cb + w * kc2 + w],
                                    b2[kc2][:, h0 + tt * 512:h0 + tt * 512 + 512],
                                    start=(kc2 == 0), stop=(kc2 == KC - 1)))
                                ins_per.append((sB, poolB[kc2]))
                            kb.group("t", emits, outs=(PBB[bank],), ins_per=ins_per)
                        sg = self.vs2[0]
                        sgsl = slice(0, 512) if bk == 0 else slice(2080 - 512, 2080)
                        kb.op("s", lambda e, sg=sg, w=w, bk=bk, sgsl=sgsl: e.activation(
                            sg[0:w, sgsl], PB[bk][0:w, :], AF.Silu),
                            outs=(vsB[0],), ins=(PBB[bk],))
                        kb.op("v", lambda e, tt=tt, gt=gt, sg=sg, w=w, bk=bk, sgsl=sgsl: e.tensor_tensor(
                            gt[0:w, ts(tt, 512)], sg[0:w, sgsl], PB[bk + 1][0:w, :], OP.mult),
                            outs=(gtB,), ins=(vsB[0], PBB[bk + 1]))
                first_e = (e8 == 0)
                for ocg in range(2):
                    for tt in range(2):
                        for hz in range(NHZ):
                            w = hzw(hz)
                            wdt = self.stg[2]
                            wdtB = stgB[2]
                            po = 512 * (hz % 2)
                            kb.dma("q", wdt[0:w, po + 1024 * ocg:po + 1024 * ocg + 512],
                                   wd_r[2752 * e8 + 128 * hz:2752 * e8 + 128 * hz + w,
                                        512 * ocg:512 * ocg + 512],
                                   outs=(wdtB,), ins=(self.B("wd_bf"),))
                            gt, gtB = gu_tile(hz)
                            for oci in range(4):
                                kb.op("t", lambda e, hz=hz, oci=oci, tt=tt, po=po, ocg=ocg, gt=gt, wdt=wdt, w=w: e.matmul(
                                    PB[2 + oci][:, :],
                                    wdt[0:w, po + 1024 * ocg + 128 * oci:po + 1024 * ocg + 128 * oci + 128],
                                    gt[0:w, ts(tt, 512)],
                                    start=(hz == 0), stop=(hz == NHZ - 1)),
                                    outs=(PBB[2 + oci],), ins=(wdtB, gtB))
                        t1 = self.vs2[0].bitcast(F32)
                        for oci in range(4):
                            oc = ocg * 4 + oci
                            ysl = self.scr[oc // 2][:, (oc % 2) * 1024 + 512 * tt:(oc % 2) * 1024 + 512 * tt + 512]
                            kb.op("v", lambda e, oci=oci, tt=tt, t1=t1: e.tensor_tensor(
                                t1[:, 256:768], PB[2 + oci][:, :], self.vs2[1][:, ts(tt, 512)], OP.mult),
                                outs=(vsB[0],), ins=(PBB[2 + oci], vsB[1]))
                            if first_e:
                                kb.op("v", lambda e, ysl=ysl, t1=t1: e.tensor_copy(ysl, t1[:, 256:768]),
                                      outs=(scrB[oc // 2],), ins=(vsB[0],))
                            else:
                                kb.op("v", lambda e, ysl=ysl, t1=t1: e.tensor_tensor(ysl, ysl, t1[:, 256:768], OP.add),
                                      outs=(scrB[oc // 2],),
                                      ins=(scrB[oc // 2], vsB[0]))
            p8 = self.pool[8].bitcast(F32)
            p9 = self.pool[9].bitcast(F32)
            p10 = self.pool[10].bitcast(F32)
            for k in range(KC):
                kb.dma("q", p8[:, 0:1024], d["s1_d"][ts(k), h0:h0 + 1024],
                       outs=(self.B("pl8"),), ins=(self.B("s1_d"),))
                kb.dma("q", p9[:, 0:1024], d["hin_d"][ts(k), h0:h0 + 1024],
                       outs=(self.B("pl9"),), ins=(self.B("hin_d"),))
                kb.op("v", lambda e: e.tensor_tensor(p10[:, 0:1024], p8[:, 0:1024], p9[:, 0:1024], OP.add),
                      outs=(self.B("pl10"),), ins=(self.B("pl8"), self.B("pl9")))
                ysl = self.scr[k // 2][:, (k % 2) * 1024:(k % 2) * 1024 + 1024]
                kb.op("v", lambda e, k=k, ysl=ysl, h0=h0: e.tensor_tensor(
                    self.xs[k][:, h0:h0 + 1024], p10[:, 0:1024], ysl, OP.add),
                    outs=(xsB[k],), ins=(self.B("pl10"), scrB[k // 2]))

    def emit_b1(self):
        kb, T, TC, TT = self.kb, self.T, self.TC, self.TT
        d = self.dram
        for wn in ("wq1", "wk1", "wv1", "wo1"):
            kb.dma("g", d[wn + "_bf"], d[wn], outs=(self.B(wn + "_bf"),))
        kb.dma("g", d["wd_bf"].rearrange("x h o -> (x h) o"),
               d["wd"].rearrange("x h o -> (x h) o"), outs=(self.B("wd_bf"),))
        self.emit_prep_wgu()
        self.load_vec("q", "g1c", d["g1c"], 8)
        self.load_vec("q", "g2c", d["g2c"], 8)
        xsB = [self.B(f"xs{k}") for k in range(KC)]
        scrB = [self.B(f"scr{i}") for i in range(4)]
        stgB = [self.B(f"stg{i}") for i in range(3)]
        poolB = [self.B(f"pl{i}") for i in range(12)]
        vsB = [self.B("vs0"), self.B("vs1")]
        onesfB = self.B("ones_f_s")
        onesbB = self.B("ones_b_s")
        smB = [self.B(f"sm{i}") for i in range(4)]

        self.ln_stats("ln2", 1e-5, sq_eng="v")
        for k in range(KC):
            kb.dma("s", d["s1_d"][ts(k), :], self.xs[k][:], outs=(self.B("s1_d"),), ins=(xsB[k],))
        b1 = self.pool[0:8]
        for tt in range(TT):
            self.bc_rows(tt)
            mb, rb = self.scr[0], self.scr[1]
            kb.op("v", lambda e: e.tensor_copy(mb[:, 0:512], self.pb[4][:, :]),
                  outs=(scrB[0],), ins=(self.pbb[4],))
            kb.op("v", lambda e: e.tensor_copy(rb[:, 0:512], self.pb[5][:, :]),
                  outs=(scrB[1],), ins=(self.pbb[5],))
            for k in range(KC):
                tmp = self.scr[2]
                kb.op("v", lambda e, k=k, tt=tt: e.tensor_tensor(
                    tmp[:, ts(tt, 512)], self.xs[k][:, ts(tt, 512)], mb[:, 0:512], OP.subtract),
                    outs=(scrB[2],), ins=(xsB[k], scrB[0]))
                kb.op("v", lambda e, k=k, tt=tt: e.tensor_tensor(
                    self.xs[k][:, ts(tt, 512)], tmp[:, ts(tt, 512)], rb[:, 0:512], OP.mult),
                    outs=(xsB[k],), ins=(scrB[2], scrB[1]))
        for k in range(KC):
            kb.dma("s", d["xn2_d"][ts(k), :], self.xs[k][:], outs=(self.B("xn2_d"),), ins=(xsB[k],))
            kb.op("v", lambda e, k=k: e.tensor_scalar(
                b1[k][:], self.xs[k][:], self.vec("g1c", k), None, OP.mult),
                outs=(poolB[k],), ins=(xsB[k], self.B("vecs")))
        self.ln_stats("rms1", 1e-6, rms=True, src=self.xs, src_bufs=xsB, sq_eng="v")

        qx = [self.xs[h].bitcast(BF16) for h in range(KC)]
        PB, PBB = self.pb, self.pbb

        def load_strips(wname):
            for kc2 in range(KC):
                dst = self.stg[kc2 // 4]
                kb.dma("q", dst[:, (kc2 % 4) * D:(kc2 % 4) * D + D], d[wname + "_bf"][ts(kc2), :],
                       outs=(stgB[kc2 // 4],), ins=(self.B(wname + "_bf"),))

        def gemm_1024(dst_fn, rhs_list, rhs_bufs, epi, n_oc=8, banks=(0, 1, 2, 3)):
            for oc in range(n_oc):
                for tt in range(TT):
                    bank = banks[(oc * TT + tt) % len(banks)]
                    emits, ins_per = [], []
                    for kc2 in range(KC):
                        wv = self.stg[kc2 // 4]
                        emits.append(lambda e, kc2=kc2, oc=oc, tt=tt, bank=bank, wv=wv: e.matmul(
                            PB[bank][:, :],
                            wv[:, (kc2 % 4) * D + oc * 128:(kc2 % 4) * D + oc * 128 + 128],
                            rhs_list[kc2][:, ts(tt, 512)],
                            start=(kc2 == 0), stop=(kc2 == KC - 1)))
                        ins_per.append((stgB[kc2 // 4], rhs_bufs[kc2]))
                    kb.group("t", emits, outs=(PBB[bank],), ins_per=ins_per)
                    epi(oc, tt, bank)

        b1B = poolB[0:8]
        for proj, wname, colbase in (("q", "wq1", 0), ("k", "wk1", T)):
            load_strips(wname)

            def epi(oc, tt, bank, colbase=colbase):
                kb.op("v", lambda e, oc=oc, tt=tt, bank=bank: e.tensor_copy(
                    qx[oc][:, colbase + tt * 512: colbase + tt * 512 + 512], PB[bank][:, :]),
                    outs=(xsB[oc],), ins=(PBB[bank],))
            gemm_1024(None, b1, b1B, epi)

        cqs, sqs = self.scr[0], self.scr[1]
        for tt in range(TT):
            self.bc_rows(tt, rows=("rstd",), banks=(4,))
            ct = self.scr[2]
            kb.dma("q", ct[:, 0:512], d["cos_t"][:, ts(tt, 512)], outs=(scrB[2],))
            kb.dma("q", ct[:, 512:1024], d["sin_t"][:, ts(tt, 512)], outs=(scrB[2],))
            kb.op("v", lambda e, tt=tt: e.tensor_tensor(
                cqs[:, ts(tt, 512)], ct[:, 0:512], self.pb[4][:, :], OP.mult),
                outs=(scrB[0],), ins=(scrB[2], self.pbb[4]))
            kb.op("v", lambda e, tt=tt: e.tensor_tensor(
                sqs[:, ts(tt, 512)], ct[:, 512:1024], self.pb[4][:, :], OP.mult),
                outs=(scrB[1],), ins=(scrB[2], self.pbb[4]))
        R2 = self.cid["R2"]
        R2B = self.B("R2_s")
        kb.op("v", lambda e: e.memset(self.small[1][:], 0.0), outs=(smB[1],))
        for colbase in (0, T):
            for hp in range(KC):
                for tt in range(TT):
                    sl = slice(colbase + tt * 512, colbase + tt * 512 + 512)
                    bank = tt % 2
                    kb.op("t", lambda e, hp=hp, sl=sl, bank=bank: e.matmul(
                        PB[bank][:, :], R2[:], qx[hp][:, sl], start=True, stop=True),
                        outs=(PBB[bank],), ins=(xsB[hp], R2B))
                    t1 = self.scr[3]
                    kb.op("v", lambda e, hp=hp, sl=sl, tt=tt: e.tensor_tensor(
                        t1[:, 0:512], qx[hp][:, sl], cqs[:, ts(tt, 512)], OP.mult),
                        outs=(scrB[3],), ins=(xsB[hp], scrB[0]))
                    kb.op("v", lambda e, tt=tt, bank=bank: e.tensor_tensor(
                        t1[:, 512:1024], PB[bank][:, :], sqs[:, ts(tt, 512)], OP.mult),
                        outs=(scrB[3],), ins=(PBB[bank], scrB[1]))
                    kb.op("v", lambda e, hp=hp, sl=sl: e.tensor_tensor(
                        qx[hp][:, sl], t1[:, 0:512], t1[:, 512:1024], OP.add),
                        outs=(xsB[hp],), ins=(scrB[3],))
                    sqt = self.vs2[1]
                    kb.op("v", lambda e, hp=hp, sl=sl: e.tensor_tensor(
                        sqt[:, 0:512], qx[hp][:, sl], qx[hp][:, sl], OP.mult),
                        outs=(vsB[1],), ins=(xsB[hp],))
                    kb.op("t", lambda e: e.matmul(
                        PB[5][0:1, 0:512], self.cid["ones_b"][:, 0:1], sqt[:, 0:512],
                        start=True, stop=True),
                        outs=(PBB[5],), ins=(vsB[1], onesbB))
                    gi = 0 if colbase == 0 else 1
                    kb.op("v", lambda e: e.reduce_max(
                        self.small[1][0:1, 4:5], PB[5][0:1, 0:512], axis=AX),
                        outs=(smB[1],), ins=(PBB[5], smB[1]))
                    kb.op("v", lambda e, gi=gi: e.tensor_tensor(
                        self.small[1][0:1, gi:gi + 1], self.small[1][0:1, gi:gi + 1],
                        self.small[1][0:1, 4:5], OP.max),
                        outs=(smB[1],), ins=(smB[1],))
        kb.op("s", lambda e: e.activation(self.small[1][0:1, 2:4], self.small[1][0:1, 0:2], AF.Sqrt),
              outs=(smB[1],), ins=(smB[1],))
        kb.op("v", lambda e: e.tensor_tensor(
            self.small[1][0:1, 4:5], self.small[1][0:1, 2:3], self.small[1][0:1, 3:4], OP.mult),
            outs=(smB[1],), ins=(smB[1],))
        kb.op("v", lambda e: e.tensor_scalar(
            self.small[1][0:1, 5:6], self.small[1][0:1, 4:5], -0.125, None, OP.mult),
            outs=(smB[1],), ins=(smB[1],))
        kb.op("t", lambda e: e.matmul(
            PB[5][:, 0:1], self.cid["ones_f"][0:1, :], self.small[1][0:1, 5:6],
            start=True, stop=True),
            outs=(PBB[5],), ins=(smB[1], onesfB))
        kb.op("v", lambda e: e.tensor_copy(self.small[2][:, 0:1], PB[5][:, 0:1]),
              outs=(smB[2],), ins=(PBB[5],))
        negm = self.small[2][:, 0:1]

        for tt in range(TT):
            self.bc_rows(tt, rows=("rstd",), banks=(4,))
            kb.op("v", lambda e, tt=tt: e.tensor_copy(self.scr[0][:, ts(tt, 512)], self.pb[4][:, :]),
                  outs=(scrB[0],), ins=(self.pbb[4],))

        load_strips("wv1")
        identb = self.cid["ident_b"]
        idbB = self.B("ident_b_s")
        utf = self.cid["ut_f"]
        utfB = self.B("ut_f_s")
        recrow = self.rows["mean"]
        recrowB = self.rr["mean"]
        for hp in range(KC):
            vfm = self.pool[8]
            for tt in range(TT):
                bank = tt % 2
                emits, ins_per = [], []
                for kc2 in range(KC):
                    wv = self.stg[kc2 // 4]
                    emits.append(lambda e, kc2=kc2, hp=hp, tt=tt, bank=bank, wv=wv: e.matmul(
                        PB[bank][:, :],
                        wv[:, (kc2 % 4) * D + hp * 128:(kc2 % 4) * D + hp * 128 + 128],
                        b1[kc2][:, ts(tt, 512)],
                        start=(kc2 == 0), stop=(kc2 == KC - 1)))
                    ins_per.append((stgB[kc2 // 4], b1B[kc2]))
                kb.group("t", emits, outs=(PBB[bank],), ins_per=ins_per)
                kb.op("v", lambda e, tt=tt, bank=bank: e.tensor_tensor(
                    vfm[:, ts(tt, 512)], PB[bank][:, :], self.scr[0][:, ts(tt, 512)], OP.mult),
                    outs=(poolB[8],), ins=(PBB[bank], scrB[0]))
            vs = self.vs2[0]
            vsb = vsB[0]
            kb.op("v", lambda e: e.memset(vs[:], 1.0), outs=(vsb,))
            for tc in range(TC):
                pt = self.pt[tc % 2]
                ptB = self.ptb[tc % 2]
                kb.op("t", lambda e, tc=tc, pt=pt: e.transpose(
                    pt[:, 0:128], vfm[:, ts(tc)], identb[:]),
                    outs=(ptB,), ins=(poolB[8], idbB))
                kb.op("v", lambda e, tc=tc, pt=pt: e.tensor_copy(
                    vs[:, 130 * tc:130 * tc + 64], pt[:, 0:64]),
                    outs=(vsb,), ins=(ptB,))
                kb.op("v", lambda e, tc=tc, pt=pt: e.tensor_copy(
                    vs[:, 130 * tc + 65:130 * tc + 129], pt[:, 64:128]),
                    outs=(vsb,), ins=(ptB,))
            for hl in range(2):
                hs = slice(64 * hl, 64 * hl + 64)
                for j in range(TT):
                    bO = 2 + (j % 2)
                    n_ck = min(4 * j + 4, TC)
                    for ck in range(n_ck):
                        bA = ck % 2
                        kb.op("t", lambda e, hp=hp, hs=hs, j=j, ck=ck, bA=bA: e.matmul(
                            PB[bA][:, :],
                            qx[hp][hs, slice(T + 128 * ck, T + 128 * ck + 128)],
                            qx[hp][hs, ts(j, 512)],
                            start=True, stop=True),
                            outs=(PBB[bA],), ins=(xsB[hp],))
                        pts = self.vs2[1]
                        po = 512 * (ck % 2)
                        kb.op("s", lambda e, bA=bA, po=po, pts=pts: e.activation(
                            pts[:, po:po + 512], PB[bA][:, :], AF.Exp, bias=negm, scale=0.125),
                            outs=(vsB[1],), ins=(PBB[bA], smB[2]))
                        dz = (ck - 4 * j) * 128
                        if dz > 0:
                            kb.op("v", lambda e, po=po, dz=dz, pts=pts: e.memset(
                                pts[:, po:po + dz], 0.0),
                                outs=(vsB[1],), ins=())
                        if 0 <= dz < 512:
                            kb.op("v", lambda e, po=po, dz=dz, pts=pts: e.tensor_tensor(
                                pts[:, po + dz:po + dz + 128], pts[:, po + dz:po + dz + 128],
                                utf[:], OP.mult),
                                outs=(vsB[1],), ins=(vsB[1], utfB))
                        kb.op("t", lambda e, ck=ck, hl=hl, bO=bO, po=po, pts=pts, n_ck=n_ck: e.matmul(
                            PB[bO][0:65, :], self.vs2[0][:, 130 * ck + 65 * hl:130 * ck + 65 * hl + 65],
                            pts[:, po:po + 512],
                            start=(ck == 0), stop=(ck == n_ck - 1)),
                            outs=(PBB[bO],), ins=(vsB[0], vsB[1]))
                    kb.op("v", lambda e, bO=bO: e.reciprocal(recrow[:, 0:512], PB[bO][64:65, :]),
                          outs=(recrowB,), ins=(PBB[bO],))
                    kb.op("t", lambda e, bO=bO: e.matmul(
                        PB[4][0:64, :], self.cid["ones_f"][0:1, 0:64], recrow[:, 0:512],
                        start=True, stop=True),
                        outs=(PBB[4],), ins=(recrowB, onesfB))
                    kb.op("v", lambda e: e.tensor_copy(self.scr[3][:, 0:512], PB[4][0:64, :][0:64, :])
                          if False else e.tensor_copy(self.scr[3][0:64, 0:512], PB[4][0:64, :]),
                          outs=(scrB[3],), ins=(PBB[4],))
                    kb.op("v", lambda e, hp=hp, hs=hs, j=j, bO=bO: e.tensor_tensor(
                        qx[hp][hs, ts(j, 512)], PB[bO][0:64, :], self.scr[3][0:64, 0:512], OP.mult),
                        outs=(xsB[hp],), ins=(PBB[bO], scrB[3]))

        load_strips("wo1")
        attn2 = qx
        hb = self.vs2[1]
        for oc in range(KC):
            hin = self.scr[1 + (oc % 2)]
            hinB = scrB[1 + (oc % 2)]
            kb.dma("q", hin[:, 0:T], d["xn2_d"][ts(oc), :],
                   outs=(hinB,), ins=(self.B("xn2_d"),))
            for tt in range(TT):
                bank = tt % 2
                emits, ins_per = [], []
                for kc2 in range(KC):
                    wv = self.stg[kc2 // 4]
                    emits.append(lambda e, kc2=kc2, oc=oc, tt=tt, bank=bank, wv=wv: e.matmul(
                        PB[bank][:, :],
                        wv[:, (kc2 % 4) * D + oc * 128:(kc2 % 4) * D + oc * 128 + 128],
                        attn2[kc2][:, ts(tt, 512)],
                        start=(kc2 == 0), stop=(kc2 == KC - 1)))
                    ins_per.append((stgB[kc2 // 4], xsB[kc2]))
                kb.group("t", emits, outs=(PBB[bank],), ins_per=ins_per)
                kb.op("v", lambda e, tt=tt, bank=bank, hin=hin: e.tensor_tensor(
                    hin[:, ts(tt, 512)], hin[:, ts(tt, 512)], PB[bank][:, :], OP.add),
                    outs=(hinB,), ins=(hinB, PBB[bank]))
                kb.op("v", lambda e, tt=tt, hin=hin: e.tensor_copy(
                    hb[:, 0:512], hin[:, ts(tt, 512)]),
                    outs=(vsB[1],), ins=(hinB,))
                kb.op("v", lambda e: e.tensor_tensor(hb[:, 512:1024], hb[:, 0:512], hb[:, 0:512], OP.mult),
                      outs=(vsB[1],), ins=(vsB[1],))
                kb.op("t", lambda e, oc=oc, tt=tt: e.matmul(
                    self.pb[2 + tt][0:1, :],
                    self.cid["ones_b"][:, 0:1], hb[:, 512:1024],
                    start=(oc == 0), stop=(oc == KC - 1)),
                    outs=(self.pbb[2 + tt],), ins=(vsB[1], onesbB))
            kb.dma("s", d["hin_d"][ts(oc), :], hin[:, 0:T], outs=(self.B("hin_d"),), ins=(hinB,))
        self.rms_rows_from_banks()

        b2 = self.pool[0:8]
        for oc in range(KC):
            hin = self.scr[1 + (oc % 2)]
            hinB = scrB[1 + (oc % 2)]
            kb.dma("q", hin[:, 0:T], d["hin_d"][ts(oc), :], outs=(hinB,), ins=(self.B("hin_d"),))
            for tt in range(TT):
                self.bc_rows(tt, rows=("rstd",), banks=(4,))
                kb.op("v", lambda e, oc=oc, tt=tt, hin=hin: e.tensor_tensor(
                    self.scr[3][:, 0:512], hin[:, ts(tt, 512)], self.pb[4][:, :], OP.mult),
                    outs=(scrB[3],), ins=(hinB, self.pbb[4]))
                kb.op("v", lambda e, oc=oc, tt=tt: e.tensor_scalar(
                    b2[oc][:, ts(tt, 512)], self.scr[3][:, 0:512], self.vec("g2c", oc), None, OP.mult),
                    outs=(poolB[oc],), ins=(scrB[3], self.B("vecs")))

        self.emit_router(b2, poolB)

        self.emit_moe(b2, poolB)

    def emit_b2(self):
        kb, T, TC, TT = self.kb, self.T, self.TC, self.TT
        d = self.dram
        xsB = [self.B(f"xs{k}") for k in range(KC)]
        scrB = [self.B(f"scr{i}") for i in range(4)]
        stgB = [self.B(f"stg{i}") for i in range(3)]
        poolB = [self.B(f"pl{i}") for i in range(12)]
        vsB = [self.B("vs0"), self.B("vs1")]
        smB = [self.B(f"sm{i}") for i in range(4)]
        PB, PBB = self.pb, self.pbb
        sm3 = self.small[3]
        for wn in ("Wr", "Wk", "Wv", "Wo"):
            kb.dma("g", d[wn + "_bf"], d[wn], outs=(self.B(wn + "_bf"),))
        self.load_vec("q", "decc", d["decc"], 8)
        self.load_vec("q", "lnwc", d["lnwc"], 8)
        kb.op("s", lambda e: e.activation(sm3[:, 0:8], self.vec("decc"), AF.Sigmoid),
              outs=(smB[3],), ins=(self.B("vecs"),))
        kb.op("v", lambda e: e.tensor_scalar(sm3[:, 0:8], sm3[:, 0:8], 1e-7, None, OP.max),
              outs=(smB[3],), ins=(smB[3],))
        kb.op("s", lambda e: e.activation(sm3[:, 8:16], sm3[:, 0:8], AF.Ln),
              outs=(smB[3],), ins=(smB[3],))

        self.ln_stats("ln3", 1e-5)
        xn3 = self.pool[0:8]
        for k in range(KC):
            kb.dma("s", d["s2_d"][ts(k), :], self.xs[k][:], outs=(self.B("s2_d"),), ins=(xsB[k],))
        for tt in range(TT):
            self.bc_rows(tt)
            mb, rb = self.scr[0], self.scr[1]
            kb.op("v", lambda e: e.tensor_copy(mb[:, 0:512], self.pb[4][:, :]),
                  outs=(scrB[0],), ins=(self.pbb[4],))
            kb.op("v", lambda e: e.tensor_copy(rb[:, 0:512], self.pb[5][:, :]),
                  outs=(scrB[1],), ins=(self.pbb[5],))
            for k in range(KC):
                tmp = self.scr[2]
                kb.op("g", lambda e, k=k, tt=tt: e.tensor_tensor(
                    tmp[:, ts(tt, 512)], self.xs[k][:, ts(tt, 512)], mb[:, 0:512], OP.subtract),
                    outs=(scrB[2],), ins=(xsB[k], scrB[0]))
                kb.op("v", lambda e, k=k, tt=tt: e.tensor_tensor(
                    xn3[k][:, ts(tt, 512)], tmp[:, ts(tt, 512)], rb[:, 0:512], OP.mult),
                    outs=(poolB[k],), ins=(scrB[2], scrB[1]))

        Wr_bf, Wk_bf, Wv_bf = d["Wr_bf"], d["Wk_bf"], d["Wv_bf"]
        for oc in range(KC):
            stg = self.stg[0]
            sB = stgB[0]
            for pi, wbf, wB in ((0, Wr_bf, self.B("Wr_bf")), (1, Wk_bf, self.B("Wk_bf")),
                                (2, Wv_bf, self.B("Wv_bf"))):
                for kc2 in range(KC):
                    kb.dmat("q", stg[:, pi * 1024 + 128 * kc2:pi * 1024 + 128 * kc2 + 128],
                            wbf[oc * 128:oc * 128 + 128, ts(kc2)],
                            outs=(sB,), ins=(wB,))
            for tt in range(TT):
                for pi, bank in ((0, 0), (1, 1), (2, 2)):
                    emits, ins_per = [], []
                    for kc2 in range(KC):
                        emits.append(lambda e, kc2=kc2, tt=tt, pi=pi, bank=bank, stg=stg: e.matmul(
                            PB[bank][:, :], stg[:, pi * 1024 + 128 * kc2:pi * 1024 + 128 * kc2 + 128],
                            xn3[kc2][:, ts(tt, 512)],
                            start=(kc2 == 0), stop=(kc2 == KC - 1)))
                        ins_per.append((sB, poolB[kc2]))
                    kb.group("t", emits, outs=(PBB[bank],), ins_per=ins_per)
                kb.op("s", lambda e, tt=tt: e.activation(
                    self.scr[3][:, ts(tt, 512)], PB[0][:, :], AF.Sigmoid),
                    outs=(scrB[3],), ins=(PBB[0],))
                kb.op("v", lambda e, tt=tt: e.tensor_copy(self.scr[2][:, ts(tt, 512)], PB[1][:, :]),
                      outs=(scrB[2],), ins=(PBB[1],))
                kb.op("v", lambda e, tt=tt: e.tensor_tensor(
                    self.scr[2][:, ts(tt, 512)], self.scr[2][:, ts(tt, 512)], PB[2][:, :], OP.mult),
                    outs=(scrB[2],), ins=(scrB[2], PBB[2]))
            kb.dma("q", self.scr[0][:], d["t_bc"], outs=(scrB[0],))
            ldap = sm3[:, 8 + oc:9 + oc]
            kb.op("v", lambda e, ldap=ldap: e.tensor_scalar(
                self.scr[0][:], self.scr[0][:], ldap, None, OP.mult),
                outs=(scrB[0],), ins=(scrB[0], smB[3]))
            kb.op("v", lambda e: e.tensor_scalar(self.scr[1][:], self.scr[0][:], -1.0, 23.1,
                                                 OP.mult, OP.min),
                  outs=(scrB[1],), ins=(scrB[0],))
            kb.op("s", lambda e: e.activation(self.scr[1][:], self.scr[1][:], AF.Exp),
                  outs=(scrB[1],), ins=(scrB[1],))
            kb.op("v", lambda e: e.tensor_scalar(self.scr[1][:], self.scr[1][:], 1e10, None, OP.min),
                  outs=(scrB[1],), ins=(scrB[1],))
            kb.op("s", lambda e: e.activation(self.scr[0][:], self.scr[0][:], AF.Exp),
                  outs=(scrB[0],), ins=(scrB[0],))
            kb.op("v", lambda e: e.tensor_tensor(
                self.scr[2][:], self.scr[2][:], self.scr[1][:], OP.mult),
                outs=(scrB[2],), ins=(scrB[2], scrB[1]))
            kb.op("v", lambda e: e.tensor_tensor_scan(
                self.scr[2][:], self.scr[2][:], self.scr[2][:], 0.0, OP.add, OP.bypass),
                outs=(scrB[2],), ins=(scrB[2],))
            kb.op("v", lambda e: e.tensor_tensor(
                self.scr[2][:], self.scr[2][:], self.scr[0][:], OP.mult),
                outs=(scrB[2],), ins=(scrB[2], scrB[0]))
            kb.op("v", lambda e, oc=oc: e.tensor_tensor(
                self.xs[oc][:], self.scr[2][:], self.scr[3][:], OP.mult),
                outs=(xsB[oc],), ins=(scrB[2], scrB[3]))

        if getattr(self, "b2_stop", None) == "rs":
            return
        self.ln_stats("rms_l1", 1e-6, rms=True, src=self.xs, src_bufs=xsB)
        ybf = self.pool[0:8]
        for oc in range(KC):
            for tt in range(TT):
                self.bc_rows(tt, rows=("rstd",), banks=(4,))
                kb.op("v", lambda e, oc=oc, tt=tt: e.tensor_tensor(
                    self.scr[2][:, 0:512], self.xs[oc][:, ts(tt, 512)], self.pb[4][:, :], OP.mult),
                    outs=(scrB[2],), ins=(xsB[oc], self.pbb[4]))
                kb.op("v", lambda e, oc=oc, tt=tt: e.tensor_scalar(
                    ybf[oc][:, ts(tt, 512)], self.scr[2][:, 0:512], self.vec("lnwc", oc), None, OP.mult),
                    outs=(poolB[oc],), ins=(scrB[2], self.B("vecs")))

        for kc2 in range(KC):
            dst = self.stg[kc2 // 4]
            kb.dmat("q", dst[:, (kc2 % 4) * D:(kc2 % 4) * D + D],
                    d["Wo_bf"][:, ts(kc2)], outs=(stgB[kc2 // 4],), ins=(self.B("Wo_bf"),))
        for oc in range(KC):
            s2t = self.scr[oc % 2]
            s2tB = scrB[oc % 2]
            kb.dma("q", s2t[:], d["s2_d"][ts(oc), :], outs=(s2tB,), ins=(self.B("s2_d"),))
            for tt in range(TT):
                bank = tt % 4
                emits, ins_per = [], []
                for kc2 in range(KC):
                    wv = self.stg[kc2 // 4]
                    emits.append(lambda e, kc2=kc2, oc=oc, tt=tt, bank=bank, wv=wv: e.matmul(
                        PB[bank][:, :],
                        wv[:, (kc2 % 4) * D + oc * 128:(kc2 % 4) * D + oc * 128 + 128],
                        ybf[kc2][:, ts(tt, 512)],
                        start=(kc2 == 0), stop=(kc2 == KC - 1)))
                    ins_per.append((stgB[kc2 // 4], poolB[kc2]))
                kb.group("t", emits, outs=(PBB[bank],), ins_per=ins_per)
                kb.op("v", lambda e, oc=oc, tt=tt, bank=bank, s2t=s2t: e.tensor_tensor(
                    self.xs[oc][:, ts(tt, 512)], s2t[:, ts(tt, 512)], PB[bank][:, :], OP.add),
                    outs=(xsB[oc],), ins=(s2tB, PBB[bank]))

    def emit_b3(self):
        kb, T, TC, TT = self.kb, self.T, self.TC, self.TT
        d = self.dram
        xsB = [self.B(f"xs{k}") for k in range(KC)]
        scrB = [self.B(f"scr{i}") for i in range(4)]
        stgB = [self.B(f"stg{i}") for i in range(3)]
        poolB = [self.B(f"pl{i}") for i in range(12)]
        PB, PBB = self.pb, self.pbb
        for wn, src_n in (("w1_bf", "w1w"), ("w2_bf", "w2w"), ("w3_bf", "w3w")):
            kb.dma("g", d[wn], d[src_n], outs=(self.B(wn),))
        for wn, src_n in (("h1_bf", "h1w"), ("h2_bf", "h2w")):
            for i in range(4):
                kb.dma("g", d[wn][ts(i, 1024), :], d[src_n][ts(i, 1024), :], outs=(self.B(wn),))
        self.load_vec("q", "s1c", d["s1c"], 32)
        self.load_vec("q", "s2c", d["s2c"], 32)
        self.load_vec("q", "hs1c", d["hs1c"], 32)
        self.load_vec("q", "hs2c", d["hs2c"], 32)
        self.load_vec("q", "s3c", d["s3c"], 8)

        self.ln_stats("ln4", 1e-5)
        for k in range(KC):
            kb.dma("s", d["s3_d"][ts(k), :], self.xs[k][:], outs=(self.B("s3_d"),), ins=(xsB[k],))

        slots = []
        for i in range(3):
            for j in range(8):
                slots.append((self.stg[i][:, 512 * j:512 * j + 512], stgB[i]))
        for pi in (10, 11):
            for j in range(4):
                slots.append((self.pool[pi][:, 512 * j:512 * j + 512], poolB[pi]))
        self.slot_i = 0

        def next_slot():
            s = slots[self.slot_i % len(slots)]
            self.slot_i += 1
            return s

        def hx(i):
            return self.xs[i // 8].bitcast(BF16)[:, (i % 8) * 512:(i % 8) * 512 + 512], xsB[i // 8]

        def gx(i):
            return self.xs[4 + i // 8].bitcast(BF16)[:, (i % 8) * 512:(i % 8) * 512 + 512], xsB[4 + i // 8]

        def big_gemm(wbf_name, n_out, n_k, rhs_fn, epi, tq):
            wbf = d[wbf_name]
            wB = self.B(wbf_name)
            for ocg in range((n_out + 3) // 4):
                ocs = [ocg * 4 + i for i in range(4) if ocg * 4 + i < n_out]
                burst_slots = {}
                for kc0 in range(0, n_k, 8):
                    cur = []
                    for kc2 in range(kc0, min(kc0 + 8, n_k)):
                        sl, slB = next_slot()
                        kb.dmat("q", sl, wbf[ocg * 512:ocg * 512 + 512, ts(kc2)],
                                outs=(slB,), ins=(wB,))
                        cur.append((kc2, sl, slB))
                    for oci, oc in enumerate(ocs):
                        for bi, (kc2, sl, slB) in enumerate(cur):
                            rhs, rhsB = rhs_fn(kc2)
                            kb.op("t", lambda e, sl=sl, oci=oci, oc=oc, rhs=rhs, kc2=kc2, n_k=n_k: e.matmul(
                                PB[oci][:, :], sl[:, 128 * (oc % 4):128 * (oc % 4) + 128], rhs,
                                start=(kc2 == 0), stop=(kc2 == n_k - 1)),
                                outs=(PBB[oci],), ins=(slB, rhsB))
                for oci, oc in enumerate(ocs):
                    epi(oc, oci)

        identf = self.cid["ident_f"]
        idB = self.B("ident_f_s")
        for tt in range(TT):
            self.bc_rows(tt)
            mb, rb = self.scr[0], self.scr[1]
            kb.op("v", lambda e: e.tensor_copy(mb[:, 0:512], self.pb[4][:, :]),
                  outs=(scrB[0],), ins=(self.pbb[4],))
            kb.op("v", lambda e: e.tensor_copy(rb[:, 0:512], self.pb[5][:, :]),
                  outs=(scrB[1],), ins=(self.pbb[5],))
            for kc2 in range(KC):
                tmp = self.scr[2]
                kb.op("g", lambda e, kc2=kc2, tt=tt: e.tensor_tensor(
                    tmp[:, ts(tt, 512)], self.xs[kc2][:, ts(tt, 512)], mb[:, 0:512], OP.subtract),
                    outs=(scrB[2],), ins=(xsB[kc2], scrB[0]))
                kb.op("v", lambda e, kc2=kc2, tt=tt: e.tensor_tensor(
                    self.pool[kc2][:, ts(tt, 512)], tmp[:, ts(tt, 512)], rb[:, 0:512], OP.mult),
                    outs=(poolB[kc2],), ins=(scrB[2], scrB[1]))
        for tq in range(TT):
            q0 = tq * 512

            def xn4(kc2, q0=q0):
                return (self.pool[kc2][:, q0:q0 + 512], poolB[kc2])
            if getattr(self, "b2_stop", None) == "xn4":
                return
            def epi_w1(oc, bank):
                sgt, sgtB = gx(oc)
                kb.op("s", lambda e, oc=oc, bank=bank, sgt=sgt: e.activation(
                    sgt, PB[bank][:, :], AF.Silu, scale=self.vec("s1c", oc)),
                    outs=(sgtB,), ins=(PBB[bank], self.B("vecs")))
            def epi_w2(oc, bank):
                sgt, sgtB = gx(oc)
                ht, htB = hx(oc)
                kb.op("v", lambda e, oc=oc, bank=bank: e.tensor_scalar(
                    self.scr[2][:, 0:512], PB[bank][:, :], self.vec("s2c", oc), None, OP.mult),
                    outs=(scrB[2],), ins=(PBB[bank], self.B("vecs")))
                kb.op("v", lambda e, ht=ht, sgt=sgt: e.tensor_tensor(
                    ht, sgt, self.scr[2][:, 0:512], OP.mult),
                    outs=(htB,), ins=(sgtB, scrB[2]))
            big_gemm("w1_bf", 32, 8, xn4, epi_w1, tq)
            big_gemm("w2_bf", 32, 8, xn4, epi_w2, tq)
            if getattr(self, "b2_stop", None) == "h":
                return

            def epi_h1(oc, bank):
                gt, gtB = gx(oc)
                kb.op("s", lambda e, oc=oc, bank=bank, gt=gt: e.activation(
                    gt, PB[bank][:, :], AF.Gelu, scale=self.vec("hs1c", oc)),
                    outs=(gtB,), ins=(PBB[bank], self.B("vecs")))
            big_gemm("h1_bf", 32, 32, hx, epi_h1, tq)
            if getattr(self, "b2_stop", None) == "g":
                return

            def epi_h2(oc, bank):
                ht, htB = hx(oc)
                kb.op("s", lambda e, oc=oc, bank=bank, ht=ht: e.activation(
                    ht, PB[bank][:, :], AF.Gelu, scale=self.vec("hs2c", oc)),
                    outs=(htB,), ins=(PBB[bank], self.B("vecs")))
            big_gemm("h2_bf", 32, 32, gx, epi_h2, tq)

            def epi_w3(oc, bank):
                s3t = self.scr[1 + (oc % 2)]
                s3tB = scrB[1 + (oc % 2)]
                kb.dma("q", s3t[:, 0:512], d["s3_d"][ts(oc), q0:q0 + 512],
                       outs=(s3tB,), ins=(self.B("s3_d"),))
                kb.op("v", lambda e, oc=oc, bank=bank, s3t=s3t: e.scalar_tensor_tensor(
                    self.scr[3][:, 0:512], PB[bank][:, :], self.vec("s3c", oc), s3t[:, 0:512],
                    OP.mult, OP.add),
                    outs=(scrB[3],), ins=(PBB[bank], self.B("vecs"), s3tB))
                if self.io_out == "fm":
                    kb.dma("s", d["y_out_fm"][ts(oc), q0:q0 + 512], self.scr[3][:, 0:512],
                           outs=(self.B("y_out_fm"),), ins=(scrB[3],))
                else:
                    def stage(tb):
                        if tb < 2:
                            return (self.pool[8 + tb].bitcast(F32)[:, 0:1024],
                                    poolB[8 + tb])
                        return self.scr[0][:, 1024 * (tb - 2):1024 * (tb - 2) + 1024], scrB[0]
                    for tb in range(4):
                        bb = 4 + tb % 2
                        stg_ap, stg_b = stage(tb)
                        kb.op("t", lambda e, tb=tb, bb=bb: e.transpose(
                            PB[bb][:, 0:128], self.scr[3][:, 128 * tb:128 * tb + 128], identf[:]),
                            outs=(PBB[bb],), ins=(scrB[3], idB))
                        kb.op("v", lambda e, oc=oc, bb=bb, stg_ap=stg_ap: e.tensor_copy(
                            stg_ap[:, 128 * oc:128 * oc + 128], PB[bb][:, 0:128]),
                            outs=(stg_b,), ins=(PBB[bb],))
                        if oc == KC - 1:
                            kb.dma("s", d["y_out"][q0 + 128 * tb:q0 + 128 * tb + 128, :],
                                   stg_ap[:, 0:1024],
                                   outs=(self.B("y_out"),), ins=(stg_b,))
            big_gemm("w3_bf", 8, 32, hx, epi_w3, tq)

    def finalize(self):
        kb = self.kb
        for sn, c in list(kb.cnt.items()):
            if sn.startswith("D") and c > 0:
                kb._wait("s", sn, c)


def build_program(T=2048, blocks=("b0",), io_in="tm", io_out="fm", b2_stop=None):
    nc = bass.Bass("TRN2", target_bir_lowering=False)
    p = Prog(nc, T, blocks, io_in, io_out)
    p.b2_stop = b2_stop
    p.declare()
    for e in ("t", "v", "s", "g"):
        p.kb.sem(e)
    p.emit_input()
    if "b0" in blocks:
        p.emit_b0()
    if "b1" in blocks:
        p.emit_b1()
    if "b2" in blocks:
        p.emit_b2()
    if "b3" in blocks:
        p.emit_b3()
    if "b3" not in blocks:
        p.emit_output()
    p.finalize()

    kb = p.kb
    def replay(eng, name):
        for entry in kb.q[name]:
            if entry[0] == "w":
                eng.wait_ge(kb.sems[entry[1]], entry[2])
            else:
                inst = entry[1](eng)
                if entry[2] is not None:
                    inst.then_inc(kb.sems[entry[2]], entry[3])

    with nc.Block() as block:
        @block.sync
        def _(sync):
            replay(sync, "q")

        @block.tensor
        def _(tensor):
            replay(tensor, "t")

        @block.vector
        def _(vector):
            replay(vector, "v")

        @block.scalar
        def _(scalar):
            replay(scalar, "s")

        @block.gpsimd
        def _(gpsimd):
            replay(gpsimd, "g")
    kb.st.close()
    return nc, p


def host_consts(T):
    c = {}
    c["ident_f"] = np.eye(128, dtype=np.float32)
    c["ident_b"] = np.eye(128, dtype=np.float32).astype(ml_dtypes.bfloat16)
    c["ones_f"] = np.ones((128, 128), np.float32)
    c["ones_b"] = np.ones((128, 128), np.float32).astype(ml_dtypes.bfloat16)
    c["ut_f"] = np.triu(np.ones((128, 128), np.float32))
    R = np.zeros((64, 64), np.float32)
    for m in range(32):
        R[m + 32, m] = -1.0
    for m in range(32, 64):
        R[m - 32, m] = 1.0
    R2 = np.zeros((128, 128), np.float32)
    R2[0:64, 0:64] = R
    R2[64:128, 64:128] = R
    c["R2"] = R2.astype(ml_dtypes.bfloat16)
    dim = 64
    inv_freq = 1.0 / (10000.0 ** (np.arange(0, dim, 2, dtype=np.float32) / dim))
    tpos = np.arange(T, dtype=np.float32)
    freqs = np.einsum("i,j->ij", tpos, inv_freq)
    emb = np.concatenate([freqs, freqs], axis=-1)
    cos = np.cos(emb).T.astype(np.float32)
    sin = np.sin(emb).T.astype(np.float32)
    c["cos_t"] = np.concatenate([cos, cos], axis=0)
    c["sin_t"] = np.concatenate([sin, sin], axis=0)
    c["t_bc"] = np.broadcast_to(np.arange(T, dtype=np.float32), (128, T)).copy()
    sel = np.zeros((8, 1024), np.float32)
    for e in range(8):
        sel[e, 128 * e:128 * (e + 1)] = 1.0
    c["sel8"] = sel.astype(ml_dtypes.bfloat16)
    cc = np.zeros((128, 8), np.float32)
    cc[:, 0] = 1e-5
    cc[:, 1] = 1e-6
    cc[:, 3] = 1.0
    c["constc"] = cc
    return c


def vec_cols(v, groups, width=128):
    return np.ascontiguousarray(v.reshape(groups, width).T.astype(np.float32))


def _f32(a):
    return np.ascontiguousarray(np.asarray(a, dtype=np.float32))


def block_weight_map(blk, inputs):
    m = {}
    if blk == "b0":
        m["w_qkv"] = _f32(inputs["L0_attn_w_qkv"])
        m["bqc"] = vec_cols(_f32(inputs["L0_attn_b_qkv"]), 24)
        m["w_out"] = _f32(inputs["L0_attn_w_out"])
        m["boc"] = vec_cols(_f32(inputs["L0_attn_b_out"]), 8)
    elif blk == "b1":
        m["wq1"] = _f32(inputs["L0_ffn_wq"])
        m["wk1"] = _f32(inputs["L0_ffn_wk"])
        m["wv1"] = _f32(inputs["L0_ffn_wv"])
        m["wo1"] = _f32(inputs["L0_ffn_wo"])
        m["w_router"] = _f32(inputs["L0_ffn_w_router"])
        m["wg"] = _f32(inputs["L0_ffn_wg"])
        m["wu"] = _f32(inputs["L0_ffn_wu"])
        m["wd"] = _f32(inputs["L0_ffn_wd"])
        m["g1c"] = vec_cols(_f32(inputs["L0_ffn_g1"]), 8)
        m["g2c"] = vec_cols(_f32(inputs["L0_ffn_g2"]), 8)
    elif blk == "b2":
        m["Wr"] = _f32(inputs["L1_attn_Wr"])
        m["Wk"] = _f32(inputs["L1_attn_Wk"])
        m["Wv"] = _f32(inputs["L1_attn_Wv"])
        m["Wo"] = _f32(inputs["L1_attn_Wo"])
        m["decc"] = vec_cols(_f32(inputs["L1_attn_decay"]), 8)
        m["lnwc"] = vec_cols(_f32(inputs["L1_attn_ln_w"]), 8)
    elif blk == "b3":
        m["w1w"] = _f32(inputs["L1_ffn_w1_w"])
        m["w2w"] = _f32(inputs["L1_ffn_w2_w"])
        m["h1w"] = _f32(inputs["L1_ffn_h1_w"])
        m["h2w"] = _f32(inputs["L1_ffn_h2_w"])
        m["w3w"] = _f32(inputs["L1_ffn_w3_w"])
        m["s1c"] = vec_cols(_f32(inputs["L1_ffn_w1_s"]), 32)
        m["s2c"] = vec_cols(_f32(inputs["L1_ffn_w2_s"]), 32)
        m["hs1c"] = vec_cols(_f32(inputs["L1_ffn_h1_s"]), 32)
        m["hs2c"] = vec_cols(_f32(inputs["L1_ffn_h2_s"]), 32)
        m["s3c"] = vec_cols(_f32(inputs["L1_ffn_w3_s"]), 8)
    return m


FUSED = True


def kernel(**inputs):
    T = 2048
    n = 8
    consts = host_consts(T)
    x0 = _f32(inputs["L0_attn_x"])
    if FUSED:
        nc, p = build_program(T=T, blocks=("b0", "b1", "b2", "b3"), io_in="tm", io_out="tm")
        wmap = {}
        for blk in ("b0", "b1", "b2", "b3"):
            wmap.update(block_weight_map(blk, inputs))
        in_maps = []
        for c in range(n):
            m = dict(consts)
            m.update(wmap)
            m["x_in"] = np.ascontiguousarray(x0[c])
            in_maps.append(m)
        res = run_bass_kernel_spmd(nc, in_maps, core_ids=list(range(n)))
        return np.stack([np.ascontiguousarray(np.asarray(res.results[c]["y_out"], dtype=np.float32))
                         for c in range(n)], axis=0)
    cur = [np.ascontiguousarray(x0[c]) for c in range(n)]
    specs = [("b0", "tm", "fm"), ("b1", "fm", "fm"), ("b2", "fm", "fm"), ("b3", "fm", "tm")]
    for blk, ioi, ioo in specs:
        nc, p = build_program(T=T, blocks=(blk,), io_in=ioi, io_out=ioo)
        wmap = block_weight_map(blk, inputs)
        in_maps = []
        for c in range(n):
            m = dict(consts)
            m.update(wmap)
            m["x_in" if ioi == "tm" else "x_in_fm"] = cur[c]
            in_maps.append(m)
        res = run_bass_kernel_spmd(nc, in_maps, core_ids=list(range(n)))
        key = "y_out" if ioo == "tm" else "y_out_fm"
        cur = [np.ascontiguousarray(np.asarray(res.results[c][key], dtype=np.float32))
               for c in range(n)]
    return np.stack(cur, axis=0)


# version 08
# speedup vs baseline: 1.0692x; 1.0692x over previous
import numpy as np
import ml_dtypes
from contextlib import ExitStack

import concourse.bass as bass
import concourse.mybir as mybir
from concourse.bass_utils import run_bass_kernel_spmd
from concourse.alu_op_type import AluOpType as OP

F32 = mybir.dt.float32
BF16 = mybir.dt.bfloat16
AF = mybir.ActivationFunctionType
AX = mybir.AxisListType.X

D = 1024
KC = 8


class Buf:
    __slots__ = ("name", "writer", "readers")

    def __init__(self, name):
        self.name = name
        self.writer = None
        self.readers = {}


class KB:
    COMPUTE = {"t": 1, "v": 1, "s": 1, "g": 1}
    DMA_SEM = {"q": "dq", "s": "ds", "g": "dg", "v": "dv", "t": "dt"}

    def __init__(self, nc, T):
        self.nc = nc
        self.T = T
        self.TC = T // 128
        self.TT = T // 512
        self.st = ExitStack()
        self.eng = None
        self.sems = {}
        self.cnt = {}
        self.obs = {}
        self.bufs = {}
        self.q = {e: [] for e in ("t", "v", "s", "g", "q")}

    def sem(self, name):
        if name not in self.sems:
            self.sems[name] = self.st.enter_context(self.nc.semaphore(name))
            self.cnt[name] = 0
        return self.sems[name]

    def S(self, name, shape, dtype):
        h = self.st.enter_context(self.nc.sbuf_tensor(name, shape, dtype))
        self.bufs[name] = Buf(name)
        return h

    def PS(self, name, shape, dtype):
        h = self.st.enter_context(self.nc.psum_tensor(name, shape, dtype))
        self.bufs[name] = Buf(name)
        return h

    def buf(self, name):
        return self.bufs[name]

    def _wait(self, e, sem_name, val):
        if val <= 0:
            return
        key = (e, sem_name)
        if self.obs.get(key, 0) >= val:
            return
        self.q[e].append(("w", sem_name, val))
        self.obs[key] = val

    SELF_SYNC = True

    def _pre(self, e, outs, ins, is_compute):
        skip_own = is_compute and (e == "t" or not self.SELF_SYNC)
        own = e if skip_own else None
        for b in ins:
            if b.writer:
                ws, wc = b.writer
                if ws != own:
                    self._wait(e, ws, wc)
        for b in outs:
            if b.writer:
                ws, wc = b.writer
                if ws != own:
                    self._wait(e, ws, wc)
            for rs, rc in b.readers.items():
                if rs != own:
                    self._wait(e, rs, rc)

    def _post(self, sem_name, outs, ins):
        c = self.cnt[sem_name]
        for b in ins:
            if b.readers.get(sem_name, 0) < c:
                b.readers[sem_name] = c
        for b in outs:
            b.writer = (sem_name, c)
            b.readers = {}

    def op(self, e, emit, outs=(), ins=(), inc=True):
        self._pre(e, outs, ins, True)
        if inc:
            self.cnt[e] += 1
            self.q[e].append(("i", emit, e, 1))
            self._post(e, outs, ins)
        else:
            self.q[e].append(("i", emit, None, 0))

    def group(self, e, emits, outs=(), ins_per=None, inc=True):
        all_ins = []
        last = len(emits) - 1
        for i, emit in enumerate(emits):
            ins = ins_per[i] if ins_per else ()
            all_ins += list(ins)
            self._pre(e, outs if i == 0 else (), ins, True)
            if i < last:
                self.q[e].append(("i", emit, None, 0))
        self.cnt[e] += 1
        self.q[e].append(("i", emits[last], e, 1))
        self._post(e, outs, all_ins)

    def dma(self, q, out_ap, in_ap, outs=(), ins=(), **kw):
        assert outs, "dma needs a destination buf"
        sem_name = "D" + outs[0].name
        self.sem(sem_name)
        self._pre(q, outs, ins, False)
        self.cnt[sem_name] += 16
        self.q[q].append(("i", lambda e, o=out_ap, i_=in_ap, kw=kw: e.dma_start(out=o, in_=i_, **kw),
                          sem_name, 16))
        self._post(sem_name, outs, ins)

    def dmat(self, q, out_ap, in_ap, outs=(), ins=()):
        assert outs, "dmat needs a destination buf"
        sem_name = "D" + outs[0].name
        self.sem(sem_name)
        self._pre(q, outs, ins, False)
        self.cnt[sem_name] += 16
        self.q[q].append(("i", lambda e, o=out_ap, i_=in_ap: e.dma_start_transpose(o, i_),
                          sem_name, 16))
        self._post(sem_name, outs, ins)

    def barrier_engine(self, e):
        for o, c in self.cnt.items():
            if o != e and c > 0:
                self._wait(e, o, c)


def ts(i, w=128):
    return slice(i * w, (i + 1) * w)


class Prog:

    def __init__(self, nc, T, blocks, io_in, io_out):
        self.nc = nc
        self.T = T
        self.TC = T // 128
        self.TT = T // 512
        self.blocks = blocks
        self.io_in = io_in
        self.io_out = io_out
        self.kb = KB(nc, T)

    def declare(self):
        nc, T = self.nc, self.T
        kb = self.kb
        bl = self.blocks
        self.dram = {}

        def din(name, shape, dtype=F32):
            self.dram[name] = nc.dram_tensor(name, shape, dtype, kind="ExternalInput").ap()

        def dscr(name, shape, dtype=BF16):
            self.dram[name] = nc.dram_tensor(name, shape, dtype, kind="Internal").ap()
            self.kb.bufs[name] = Buf(name)

        if self.io_in == "tm":
            din("x_in", [T, D])
        else:
            din("x_in_fm", [KC * 128, T])
        if self.io_out == "tm":
            self.dram["y_out"] = nc.dram_tensor("y_out", [T, D], F32, kind="ExternalOutput").ap()
            self.kb.bufs["y_out"] = Buf("y_out")
        else:
            self.dram["y_out_fm"] = nc.dram_tensor("y_out_fm", [KC * 128, T], F32, kind="ExternalOutput").ap()
            self.kb.bufs["y_out_fm"] = Buf("y_out_fm")

        din("ident_f", [128, 128])
        din("ident_b", [128, 128], BF16)
        din("ones_f", [128, 128])
        din("ones_b", [128, 128], BF16)
        din("ut_f", [128, 128])
        din("R2", [128, 128], BF16)
        din("cos_t", [128, T])
        din("sin_t", [128, T])
        din("t_bc", [128, T])
        din("constc", [128, 8])
        din("sel8", [8, 1024], BF16)
        if "b0" in bl:
            din("w_qkv", [3 * D, D])
            din("bqc", [128, 24])
            din("w_out", [D, D])
            din("boc", [128, 8])
            dscr("wqkv_bf", [3 * D, D])
            dscr("wout_bf", [D, D])
        if "b1" in bl:
            din("wq1", [D, D]); din("wk1", [D, D]); din("wv1", [D, D]); din("wo1", [D, D])
            din("w_router", [D, 8])
            din("wg", [8, D, 2752]); din("wu", [8, D, 2752]); din("wd", [8, 2752, D])
            din("g1c", [128, 8]); din("g2c", [128, 8])
            dscr("wd_bf", [8, 2752, D])
            dscr("wgu_t", [704 * 128, 1024])

            dscr("wq1_bf", [D, D]); dscr("wk1_bf", [D, D])
            dscr("wv1_bf", [D, D]); dscr("wo1_bf", [D, D])
        if "b2" in bl:
            din("Wr", [D, D]); din("Wk", [D, D]); din("Wv", [D, D]); din("Wo", [D, D])
            din("decc", [128, 8]); din("lnwc", [128, 8])
            dscr("Wr_bf", [D, D]); dscr("Wk_bf", [D, D]); dscr("Wv_bf", [D, D]); dscr("Wo_bf", [D, D])
        if "b3" in bl:
            din("w1w", [4096, D]); din("w2w", [4096, D])
            din("h1w", [4096, 4096]); din("h2w", [4096, 4096])
            din("w3w", [D, 4096])
            din("s1c", [128, 32]); din("s2c", [128, 32])
            din("hs1c", [128, 32]); din("hs2c", [128, 32]); din("s3c", [128, 8])
            dscr("w1_bf", [4096, D]); dscr("w2_bf", [4096, D])
            dscr("h1_bf", [4096, 4096]); dscr("h2_bf", [4096, 4096])
            dscr("w3_bf", [D, 4096])
        for n in ("s1_d", "xn2_d", "hin_d", "s2_d", "s3_d"):
            dscr(n, [KC * 128, T], F32)

        S = kb.S
        self.xs = [S(f"xs{k}", [128, T], F32) for k in range(KC)]
        NPOOL = 12
        self.pool = [S(f"pl{i}", [128, T], BF16) for i in range(NPOOL)]
        self.scr = [S(f"scr{i}", [128, T], F32) for i in range(4)]
        self.stg = [S(f"stg{i}", [128, max(2 * T, 4096)], BF16) for i in range(3)]
        self.vs2 = [S(f"vs{i}", [128, 2080 * T // 2048], BF16) for i in range(2)]
        self.cid = {}
        for n in ("ident_f", "ones_f", "ut_f"):
            self.cid[n] = S(n + "_s", [128, 128], F32)
        for n in ("ident_b", "ones_b", "R2"):
            self.cid[n] = S(n + "_s", [128, 128], BF16)
        self.sel_s = S("sel_s", [8, 1024], BF16)
        self.vecs = S("vecs", [128, 360], F32)
        self.vecmap = {}
        self.vec_used = 0
        self.rows = {n: S("row_" + n, [1, T], F32)
                     for n in ("mean", "rstd")}
        self.rr = {n: kb.buf("row_" + n) for n in self.rows}
        self.pb = [kb.PS(f"pb{i}", [128, 512], F32) for i in range(6)]
        self.pt = [kb.PS(f"pt{i}", [128, 1024], BF16) for i in range(2)]
        self.pbb = [kb.buf(f"pb{i}") for i in range(6)]
        self.ptb = [kb.buf(f"pt{i}") for i in range(2)]
        for i in range(4):
            kb.bufs[f"prepslot{i}"] = Buf(f"prepslot{i}")
        for e in range(8):
            kb.bufs[f"wgu_t_e{e}"] = Buf(f"wgu_t_e{e}")
        self.Ssf = S("Ssf", [128, 65], F32)
        self.Ssb = S("Ssb", [128, 65], BF16)
        self.small = [S(f"sm{i}", [128, 16], F32) for i in range(4)]
        self.pa = S("pa", [128, 256], BF16)
        if "b1" in bl:
            self.rt_s = S("rt_s", [128, 64], BF16)
            self.lg8 = S("lg8", [8, 512], F32)
            self.cfm = S("cfm", [8, T], BF16)
            self.ctm = S("ctm", [128, 128], F32)

    def vslot(self, name, ncols):
        if name not in self.vecmap:
            self.vecmap[name] = (self.vec_used, ncols)
            self.vec_used += ncols
            assert self.vec_used <= 360
        off, n = self.vecmap[name]
        return self.vecs[:, off:off + ncols]

    def B(self, name):
        return self.kb.buf(name)

    def load_vec(self, q, name, src_ap, ncols):
        ap = self.vslot(name, ncols)
        self.kb.dma(q, ap, src_ap, outs=(self.B("vecs"),))
        return ap

    def vec(self, name, col=None):
        off, n = self.vecmap[name]
        if col is None:
            return self.vecs[:, off:off + n]
        return self.vecs[:, off + col:off + col + 1]

    def ln_stats(self, tag, eps, rms=False, src=None, src_bufs=None, sq_eng="g"):
        kb, TT, T = self.kb, self.TT, self.T
        src = src or self.xs
        src_bufs = src_bufs or [self.B(f"xs{k}") for k in range(KC)]
        xc = self.vs2[0]
        xcB = self.B("vs0")
        sq = self.scr[3].bitcast(BF16)
        sqB = self.B("scr3")
        onesb = self.cid["ones_b"]
        onesbB = self.B("ones_b_s")
        per = 2 if TT >= 2 else 1
        nh = TT // per if TT >= 2 else 1
        for h in range(nh):
            for k in range(KC):
                c0 = (h * per) * 512
                w = per * 512
                sl = slice(c0, c0 + w)
                lsl = slice(0, w)
                kb.op("v", lambda e, k=k, sl=sl, lsl=lsl: e.tensor_copy(xc[:, lsl], src[k][:, sl]),
                      outs=(xcB,), ins=(src_bufs[k],))
                kb.op(sq_eng, lambda e, lsl=lsl: e.tensor_tensor(sq[:, lsl], xc[:, lsl], xc[:, lsl], OP.mult),
                      outs=(sqB,), ins=(xcB,))
                for j in range(per):
                    tt = h * per + j
                    if not rms:
                        kb.op("t", lambda e, k=k, j=j: e.matmul(
                            self.pb[j][0:1, :], onesb[:, 0:1], xc[:, ts(j, 512)],
                            start=(k == 0), stop=(k == KC - 1)),
                            outs=(self.pbb[j],), ins=(xcB, onesbB))
                    kb.op("t", lambda e, k=k, j=j: e.matmul(
                        self.pb[2 + j][0:1, :], onesb[:, 0:1], sq[:, ts(j, 512)],
                        start=(k == 0), stop=(k == KC - 1)),
                        outs=(self.pbb[2 + j],), ins=(sqB, onesbB))
            for j in range(per):
                tt = h * per + j
                R = self.rows
                RR = self.rr
                epscol = 0 if eps == 1e-5 else 1
                epsap = self.vec("cst", epscol)[0:1, :]
                bsum, bsq = self.pb[j], self.pb[2 + j]
                bsumB, bsqB = self.pbb[j], self.pbb[2 + j]
                if rms:
                    kb.op("v", lambda e, bsq=bsq: e.tensor_scalar(
                        bsq[0:1, :], bsq[0:1, :], 1.0 / D, None, OP.mult),
                        outs=(bsqB,), ins=(bsqB,))
                    kb.op("s", lambda e, bsq=bsq: e.activation(
                        bsq[0:1, :], bsq[0:1, :], AF.Ln, bias=epsap, scale=1.0),
                        outs=(bsqB,), ins=(bsqB, self.B("vecs")))
                    kb.op("s", lambda e, tt=tt, bsq=bsq: e.activation(
                        R["rstd"][:, ts(tt, 512)], bsq[0:1, :], AF.Exp, scale=-0.5),
                        outs=(RR["rstd"],), ins=(bsqB,))
                else:
                    kb.op("v", lambda e, tt=tt, bsum=bsum: e.tensor_scalar(
                        R["mean"][:, ts(tt, 512)], bsum[0:1, :], 1.0 / D, None, OP.mult),
                        outs=(RR["mean"],), ins=(bsumB,))
                    kb.op("v", lambda e, bsq=bsq: e.tensor_scalar(
                        bsq[0:1, :], bsq[0:1, :], 1.0 / D, None, OP.mult),
                        outs=(bsqB,), ins=(bsqB,))
                    m2 = self.scr[2]
                    m2B = self.B("scr2")
                    kb.op("v", lambda e, tt=tt, m2=m2: e.tensor_tensor(
                        m2[0:1, 0:512], R["mean"][:, ts(tt, 512)], R["mean"][:, ts(tt, 512)], OP.mult),
                        outs=(m2B,), ins=(RR["mean"],))
                    kb.op("v", lambda e, m2=m2, bsq=bsq: e.tensor_tensor(
                        bsq[0:1, :], bsq[0:1, :], m2[0:1, 0:512], OP.subtract),
                        outs=(bsqB,), ins=(bsqB, m2B))
                    kb.op("s", lambda e, bsq=bsq: e.activation(
                        bsq[0:1, :], bsq[0:1, :], AF.Ln, bias=epsap, scale=1.0),
                        outs=(bsqB,), ins=(bsqB, self.B("vecs")))
                    kb.op("s", lambda e, tt=tt, bsq=bsq: e.activation(
                        R["rstd"][:, ts(tt, 512)], bsq[0:1, :], AF.Exp, scale=-0.5),
                        outs=(RR["rstd"],), ins=(bsqB,))

    def bc_rows(self, tt, rows=("mean", "rstd"), banks=(4, 5)):
        kb = self.kb
        onesf = self.cid["ones_f"]
        onesfB = self.B("ones_f_s")
        out = []
        for rn, bi in zip(rows, banks):
            kb.op("t", lambda e, rn=rn, bi=bi: e.matmul(
                self.pb[bi][:, :], onesf[0:1, :], self.rows[rn][:, ts(tt, 512)],
                start=True, stop=True),
                outs=(self.pbb[bi],), ins=(self.rr[rn], onesfB))
            out.append(self.pb[bi])
        return out

    def emit_input(self):
        kb, T, TC = self.kb, self.T, self.TC
        d = self.dram
        for n in ("ident_f", "ones_f", "ut_f", "ident_b", "ones_b", "R2"):
            kb.dma("q", self.cid[n][:], d[n], outs=(self.B(n + "_s"),))
        kb.dma("q", self.sel_s[:], d["sel8"], outs=(self.B("sel_s"),))
        self.load_vec("q", "cst", d["constc"], 8)
        if self.io_in == "fm":
            for k in range(KC):
                kb.dma("q", self.xs[k][:], d["x_in_fm"][ts(k), :], outs=(self.B(f"xs{k}"),))
            return
        identf = self.cid["ident_f"]
        idB = self.B("ident_f_s")
        for tc in range(TC):
            stgv = self.stg[tc % 2].bitcast(F32)
            stgB = self.B(f"stg{tc % 2}")
            kb.dma("q", stgv[:, 0:D], d["x_in"][ts(tc), :], outs=(stgB,))
            for k in range(KC):
                b = 4 + (k % 2)
                kb.op("t", lambda e, k=k, b=b, stgv=stgv: e.transpose(
                    self.pb[b][:, 0:128], stgv[:, ts(k)], identf[:]),
                    outs=(self.pbb[b],), ins=(stgB, idB))
                kb.op("v", lambda e, k=k, tc=tc, b=b: e.tensor_copy(
                    self.xs[k][:, ts(tc)], self.pb[b][:, 0:128]),
                    outs=(self.B(f"xs{k}"),), ins=(self.pbb[b],))

    def emit_output(self):
        kb, T, TC = self.kb, self.T, self.TC
        d = self.dram
        if self.io_out == "fm":
            for k in range(KC):
                kb.dma("s", d["y_out_fm"][ts(k), :], self.xs[k][:],
                       outs=(self.B("y_out_fm"),), ins=(self.B(f"xs{k}"),))
            return
        identf = self.cid["ident_f"]
        idB = self.B("ident_f_s")
        for tc in range(TC):
            stgv = self.stg[tc % 2].bitcast(F32)
            stgB = self.B(f"stg{tc % 2}")
            for k in range(KC):
                b = 4 + (k % 2)
                kb.op("t", lambda e, k=k, tc=tc, b=b: e.transpose(
                    self.pb[b][:, 0:128], self.xs[k][:, ts(tc)], identf[:]),
                    outs=(self.pbb[b],), ins=(self.B(f"xs{k}"), idB))
                kb.op("v", lambda e, k=k, b=b, stgv=stgv: e.tensor_copy(
                    stgv[:, ts(k)], self.pb[b][:, 0:128]),
                    outs=(stgB,), ins=(self.pbb[b],))
            kb.dma("s", d["y_out"][ts(tc), :], stgv[:, 0:D],
                   outs=(self.B("y_out"),), ins=(stgB,))

    def emit_b0(self):
        kb, T, TC, TT = self.kb, self.T, self.TC, self.TT
        d = self.dram
        for i in range(3):
            kb.dma("g", d["wqkv_bf"][ts(i, D), :], d["w_qkv"][ts(i, D), :],
                   outs=(self.B("wqkv_bf"),))
        kb.dma("g", d["wout_bf"], d["w_out"], outs=(self.B("wout_bf"),))
        self.load_vec("q", "bqc", d["bqc"], 24)
        self.load_vec("q", "boc", d["boc"], 8)

        self.ln_stats("ln1", 1e-5)
        xn1 = self.pool[0:8]
        xn1B = [self.B(f"pl{i}") for i in range(8)]
        for tt in range(TT):
            self.bc_rows(tt)
            mb = self.scr[0]; mbB = self.B("scr0")
            rb = self.scr[1]; rbB = self.B("scr1")
            kb.op("v", lambda e, tt=tt: e.tensor_copy(mb[:, 0:512], self.pb[4][:, :]),
                  outs=(mbB,), ins=(self.pbb[4],))
            kb.op("v", lambda e, tt=tt: e.tensor_copy(rb[:, 0:512], self.pb[5][:, :]),
                  outs=(rbB,), ins=(self.pbb[5],))
            for k in range(KC):
                tmp = self.scr[2]
                tmpB = self.B("scr2")
                kb.op("g", lambda e, k=k, tt=tt: e.tensor_tensor(
                    tmp[:, ts(tt, 512)], self.xs[k][:, ts(tt, 512)], mb[:, 0:512], OP.subtract),
                    outs=(tmpB,), ins=(self.B(f"xs{k}"), mbB))
                kb.op("v", lambda e, k=k, tt=tt: e.tensor_tensor(
                    xn1[k][:, ts(tt, 512)], tmp[:, ts(tt, 512)], rb[:, 0:512], OP.mult),
                    outs=(xn1B[k],), ins=(tmpB, rbB))

        qp, kp, ktm, vfm = self.pool[8], self.pool[9], self.pool[10], self.pool[11]
        qpB, kpB, ktmB, vfmB = (self.B("pl8"), self.B("pl9"), self.B("pl10"), self.B("pl11"))
        atm = self.stg[2]
        atmB = self.B("stg2")
        identb = self.cid["ident_b"]
        idbB = self.B("ident_b_s")
        utf = self.cid["ut_f"]
        utfB = self.B("ut_f_s")
        SsfB, SsbB = self.B("Ssf"), self.B("Ssb")
        paB = self.B("pa")
        vsB = [self.B(f"vs{i}") for i in range(2)]

        for hp in range(8):
            stgw = self.stg[hp % 2]
            stgwB = self.B(f"stg{hp % 2}")
            for part in range(3):
                base = part * D + hp * 128
                for kc2 in range(KC):
                    col = part * D + kc2 * 128
                    kb.dmat("q", stgw[:, col:col + 128],
                            d["wqkv_bf"][base:base + 128, ts(kc2)],
                            outs=(stgwB,), ins=(self.B("wqkv_bf"),))
            for part, dst, dstB in ((0, qp, qpB), (1, kp, kpB), (2, vfm, vfmB)):
                bias = self.vec("bqc", part * 8 + hp)
                for tt in range(TT):
                    bank = tt % 4
                    emits = []
                    ins_per = []
                    for kc2 in range(KC):
                        emits.append(lambda e, kc2=kc2, tt=tt, bank=bank, part=part, stgw=stgw: e.matmul(
                            self.pb[bank][:, :],
                            stgw[:, slice(part * D + kc2 * 128, part * D + kc2 * 128 + 128)],
                            xn1[kc2][:, ts(tt, 512)],
                            start=(kc2 == 0), stop=(kc2 == KC - 1)))
                        ins_per.append((stgwB, xn1B[kc2]))
                    kb.group("t", emits, outs=(self.pbb[bank],), ins_per=ins_per)
                    if part < 2:
                        sc12 = self.stg[2].bitcast(F32)
                        sc12B = self.B("stg2")
                        co = self.T // 2
                        ee = sc12[:, co:co + 512]
                        rr2 = sc12[:, co + 512:co + 1024]
                        kb.op("s", lambda e, bank=bank, bias=bias, ee=ee: e.activation(
                            ee, self.pb[bank][:, :], AF.Exp, bias=bias, scale=1.0),
                            outs=(sc12B,), ins=(self.pbb[bank], self.B("vecs")))
                        kb.op("s", lambda e, bank=bank, bias=bias, rr2=rr2: e.activation(
                            rr2, self.pb[bank][:, :], AF.Relu, bias=bias, scale=1.0),
                            outs=(sc12B,), ins=(self.pbb[bank], self.B("vecs")))
                        kb.op("v", lambda e, tt=tt, dst=dst, ee=ee, rr2=rr2: e.scalar_tensor_tensor(
                            dst[:, ts(tt, 512)], ee, 1.0, rr2,
                            OP.min, OP.add),
                            outs=(dstB,), ins=(sc12B,))
                    else:
                        kb.op("v", lambda e, tt=tt, bank=bank, dst=dst, bias=bias: e.tensor_scalar(
                            dst[:, ts(tt, 512)], self.pb[bank][:, :], bias, None, OP.add),
                            outs=(dstB,), ins=(self.pbb[bank], self.B("vecs")))
            vs = self.vs2[hp % 2]
            vsb = vsB[hp % 2]
            kb.op("v", lambda e, vs=vs: e.memset(vs[:], 1.0), outs=(vsb,))
            for tc in range(TC):
                pt = self.pt[tc % 2]
                ptB = self.ptb[tc % 2]
                kb.op("t", lambda e, tc=tc, pt=pt: e.transpose(
                    pt[:, 0:128], kp[:, ts(tc)], identb[:]),
                    outs=(ptB,), ins=(kpB, idbB))
                kb.op("v", lambda e, tc=tc, pt=pt: e.tensor_copy(ktm[:, ts(tc)], pt[:, 0:128]),
                      outs=(ktmB,), ins=(ptB,))
                kb.op("t", lambda e, tc=tc, pt=pt: e.transpose(
                    pt[:, 128:256], vfm[:, ts(tc)], identb[:]),
                    outs=(ptB,), ins=(vfmB, idbB))
                kb.op("v", lambda e, tc=tc, pt=pt, vs=vs: e.tensor_copy(
                    vs[:, 130 * tc:130 * tc + 64], pt[:, 128:192]),
                    outs=(vsb,), ins=(ptB,))
                kb.op("v", lambda e, tc=tc, pt=pt, vs=vs: e.tensor_copy(
                    vs[:, 130 * tc + 65:130 * tc + 129], pt[:, 192:256]),
                    outs=(vsb,), ins=(ptB,))
            kb.op("v", lambda e: e.memset(self.Ssf[:], 0.0), outs=(SsfB,))
            kb.op("v", lambda e: e.memset(self.Ssb[:], 0.0), outs=(SsbB,))
            for tc in range(TC):
                for hl in range(2):
                    hs = slice(64 * hl, 64 * hl + 64)
                    bA = 0 if hl == 0 else 1
                    bO = 2 if hl == 0 else 3
                    kb.op("t", lambda e, tc=tc, hs=hs, bA=bA: e.matmul(
                        self.pb[bA][:, 0:128], kp[hs, ts(tc)], qp[hs, ts(tc)],
                        start=True, stop=True),
                        outs=(self.pbb[bA],), ins=(kpB, qpB))
                    pasl = slice(128 * hl, 128 * hl + 128)
                    kb.op("v", lambda e, bA=bA, pasl=pasl: e.tensor_tensor(
                        self.pa[:, pasl], self.pb[bA][:, 0:128], utf[:], OP.mult),
                        outs=(paB,), ins=(self.pbb[bA], utfB))
                    e1 = lambda e, tc=tc, hl=hl, bO=bO, pasl=pasl, vs=vs: e.matmul(
                        self.pb[bO][:, 0:65], self.pa[:, pasl],
                        vs[:, 130 * tc + 65 * hl: 130 * tc + 65 * hl + 65],
                        start=True, stop=False)
                    e2 = lambda e, tc=tc, hs=hs, bO=bO: e.matmul(
                        self.pb[bO][:, 0:65], qp[hs, ts(tc)], self.Ssb[hs, :],
                        start=False, stop=True)
                    kb.group("t", [e1, e2], outs=(self.pbb[bO],),
                             ins_per=[(paB, vsb), (qpB, SsbB)])
                    bS = 4 + hl
                    kb.op("t", lambda e, tc=tc, hl=hl, hs=hs, bS=bS, vs=vs: e.matmul(
                        self.pb[bS][hs, 0:65],
                        ktm[:, 128 * tc + 64 * hl:128 * tc + 64 * hl + 64],
                        vs[:, 130 * tc + 65 * hl: 130 * tc + 65 * hl + 65],
                        start=True, stop=True),
                        outs=(self.pbb[bS],), ins=(ktmB, vsb))
                    kb.op("v", lambda e, hs=hs, bS=bS: e.tensor_tensor(
                        self.Ssf[hs, :], self.Ssf[hs, :], self.pb[bS][hs, 0:65], OP.add),
                        outs=(SsfB,), ins=(SsfB, self.pbb[bS]))
                    kb.op("v", lambda e, hs=hs: e.tensor_copy(self.Ssb[hs, :], self.Ssf[hs, :]),
                          outs=(SsbB,), ins=(SsfB,))
                    nrm = self.small[0]
                    nrmB = self.B("sm0")
                    kb.op("v", lambda e, bO=bO: e.tensor_scalar(
                        nrm[:, 0:1], self.pb[bO][:, 64:65], 1e-6, None, OP.add),
                        outs=(nrmB,), ins=(self.pbb[bO],))
                    kb.op("v", lambda e: e.reciprocal(nrm[:, 1:2], nrm[:, 0:1]),
                          outs=(nrmB,), ins=(nrmB,))
                    kb.op("v", lambda e, tc=tc, hl=hl, bO=bO: e.tensor_scalar(
                        atm[:, 128 * tc + 64 * hl:128 * tc + 64 * hl + 64],
                        self.pb[bO][:, 0:64], nrm[:, 1:2], None, OP.mult),
                        outs=(atmB,), ins=(self.pbb[bO], nrmB))
            for tc in range(TC):
                pt = self.pt[tc % 2]
                ptB = self.ptb[tc % 2]
                kb.op("t", lambda e, tc=tc, pt=pt: e.transpose(
                    pt[:, 0:128], atm[:, ts(tc)], identb[:]),
                    outs=(ptB,), ins=(atmB, idbB))
                sc = self.scr[hp // 2].bitcast(BF16)
                scB = self.B(f"scr{hp // 2}")
                kb.op("v", lambda e, tc=tc, pt=pt, sc=sc, hp=hp: e.tensor_copy(
                    sc[:, (hp % 2) * T + 128 * tc:(hp % 2) * T + 128 * tc + 128], pt[:, 0:128]),
                    outs=(scB,), ins=(ptB,))

        for kc2 in range(KC):
            dst = self.stg[kc2 // 4]
            dstB = self.B(f"stg{kc2 // 4}")
            kb.dmat("q", dst[:, (kc2 % 4) * D:(kc2 % 4) * D + D],
                    d["wout_bf"][:, ts(kc2)], outs=(dstB,), ins=(self.B("wout_bf"),))
        for oc in range(KC):
            for tt in range(TT):
                bank = (oc * TT + tt) % 4
                emits, ins_per = [], []
                for kc2 in range(KC):
                    wv = self.stg[kc2 // 4]
                    wvB = self.B(f"stg{kc2 // 4}")
                    afm = self.scr[kc2 // 2].bitcast(BF16)
                    afmB = self.B(f"scr{kc2 // 2}")
                    emits.append(lambda e, kc2=kc2, oc=oc, tt=tt, bank=bank, wv=wv, afm=afm: e.matmul(
                        self.pb[bank][:, :],
                        wv[:, (kc2 % 4) * D + oc * 128:(kc2 % 4) * D + oc * 128 + 128],
                        afm[:, (kc2 % 2) * T + tt * 512:(kc2 % 2) * T + tt * 512 + 512],
                        start=(kc2 == 0), stop=(kc2 == KC - 1)))
                    ins_per.append((wvB, afmB))
                kb.group("t", emits, outs=(self.pbb[bank],), ins_per=ins_per)
                kb.op("v", lambda e, oc=oc, tt=tt, bank=bank: e.scalar_tensor_tensor(
                    self.xs[oc][:, ts(tt, 512)], self.pb[bank][:, :],
                    self.vec("boc", oc), self.xs[oc][:, ts(tt, 512)], OP.add, OP.add),
                    outs=(self.B(f"xs{oc}"),),
                    ins=(self.pbb[bank], self.B("vecs"), self.B(f"xs{oc}")))


    def prep_wgu_init(self):
        kb, T = self.kb, self.T
        d = self.dram
        wg_r = d["wg"].rearrange("x (c p) h -> p (x c) h", c=8)
        wu_r = d["wu"].rearrange("x (c p) h -> p (x c) h", c=8)
        wgt = d["wgu_t"]
        pslots = []
        for i in range(4):
            t = self.pool[9 + i // 2]
            ap = t[:, (i % 2) * 1024:(i % 2) * 1024 + 1024]
            pslots.append((ap, kb.buf(f"prepslot{i}"), self.B(f"pl{9 + i // 2}")))

        def gen():
            si = 0
            for e8 in range(8):
                for hz in range(22):
                    w = 128 if hz < 21 else 64
                    for gi, wr in ((0, wg_r), (1, wu_r)):
                        ap, sb, rb = pslots[si % 4]
                        si += 1
                        kb.dma("g", ap[:, 0:8 * w].rearrange("p (c h) -> p c h", c=8),
                               wr[:, 8 * e8:8 * e8 + 8, 128 * hz:128 * hz + w],
                               outs=(sb, rb), ins=())
                        row = ((e8 * 22 + hz) * 2 + gi) * 128
                        kb.dma("s", wgt[row:row + 128, 0:8 * w], ap[:, 0:8 * w],
                               outs=(kb.buf(f"wgu_t_e{e8}"),), ins=(sb,))
                        yield
        self._prep_gen = gen()

    def prep_pump(self, n):
        g = getattr(self, "_prep_gen", None)
        if g is None:
            return
        for _ in range(n):
            try:
                next(g)
            except StopIteration:
                self._prep_gen = None
                return

    def rms_rows_from_banks(self):
        kb = self.kb
        epsap = self.vec("cst", 1)[0:1, :]
        for tt in range(self.TT):
            b = self.pb[2 + tt]
            bB = self.pbb[2 + tt]
            kb.op("v", lambda e, b=b: e.tensor_scalar(b[0:1, :], b[0:1, :], 1.0 / D, None, OP.mult),
                  outs=(bB,), ins=(bB,))
            kb.op("s", lambda e, b=b: e.activation(b[0:1, :], b[0:1, :], AF.Ln, bias=epsap, scale=1.0),
                  outs=(bB,), ins=(bB, self.B("vecs")))
            kb.op("s", lambda e, tt=tt, b=b: e.activation(
                self.rows["rstd"][:, ts(tt, 512)], b[0:1, :], AF.Exp, scale=-0.5),
                outs=(self.rr["rstd"],), ins=(bB,))

    def emit_router(self, b2, poolB):
        kb, T, TC, TT = self.kb, self.T, self.TC, self.TT
        d = self.dram
        PB, PBB = self.pb, self.pbb
        smB = [self.B(f"sm{i}") for i in range(4)]
        rtB, lgB, ctmB, cfmB = self.B("rt_s"), self.B("lg8"), self.B("ctm"), self.B("cfm")
        sm0, sm1, sm2, sm3 = self.small
        identf = self.cid["ident_f"]
        idB = self.B("ident_f_s")
        for kc2 in range(KC):
            kb.dma("g", self.rt_s[:, 8 * kc2:8 * kc2 + 8], d["w_router"][ts(kc2), :], outs=(rtB,))
        for tt in range(TT):
            emits, ins_per = [], []
            for kc2 in range(KC):
                emits.append(lambda e, kc2=kc2, tt=tt: e.matmul(
                    PB[0][0:8, :], self.rt_s[:, 8 * kc2:8 * kc2 + 8], b2[kc2][:, ts(tt, 512)],
                    start=(kc2 == 0), stop=(kc2 == KC - 1)))
                ins_per.append((rtB, poolB[kc2]))
            kb.group("t", emits, outs=(PBB[0],), ins_per=ins_per)
            kb.op("v", lambda e: e.tensor_copy(self.lg8[:, :], PB[0][0:8, :]),
                  outs=(lgB,), ins=(PBB[0],))
            for tcl in range(4 if TT > 0 else 0):
                tc = tt * 4 + tcl
                if tc >= TC:
                    break
                kb.op("t", lambda e, tcl=tcl: e.transpose(
                    PB[1][:, 0:8], self.lg8[:, ts(tcl)], identf[0:8, 0:8]),
                    outs=(PBB[1],), ins=(lgB, idB))
                kb.op("v", lambda e: e.reduce_max(sm0[:, 0:1], PB[1][:, 0:8], axis=AX, negate=True),
                      outs=(smB[0],), ins=(PBB[1],))
                kb.op("s", lambda e: e.activation(sm1[:, 0:8], PB[1][:, 0:8], AF.Exp,
                                                  bias=sm0[:, 0:1], scale=1.0),
                      outs=(smB[1],), ins=(PBB[1], smB[0]))
                kb.op("v", lambda e: e.reduce_sum(sm0[:, 1:2], sm1[:, 0:8], axis=AX),
                      outs=(smB[0],), ins=(smB[1],))
                kb.op("v", lambda e: e.reciprocal(sm0[:, 2:3], sm0[:, 1:2]),
                      outs=(smB[0],), ins=(smB[0],))
                kb.op("v", lambda e: e.tensor_scalar(sm1[:, 8:16], sm1[:, 0:8], sm0[:, 2:3], None, OP.mult),
                      outs=(smB[1],), ins=(smB[1], smB[0]))
                kb.op("v", lambda e: e.reduce_max(sm0[:, 3:4], sm1[:, 8:16], axis=AX),
                      outs=(smB[0],), ins=(smB[1],))
                kb.op("v", lambda e: e.tensor_scalar(sm2[:, 0:8], sm1[:, 8:16], sm0[:, 3:4], None, OP.is_ge),
                      outs=(smB[2],), ins=(smB[1], smB[0]))
                kb.op("v", lambda e: e.scalar_tensor_tensor(sm2[:, 8:16], sm2[:, 0:8], -1e30,
                                                            sm1[:, 8:16], OP.mult, OP.add),
                      outs=(smB[2],), ins=(smB[2], smB[1]))
                kb.op("v", lambda e: e.reduce_max(sm0[:, 4:5], sm2[:, 8:16], axis=AX),
                      outs=(smB[0],), ins=(smB[2],))
                kb.op("v", lambda e: e.tensor_scalar(sm2[:, 0:8], sm1[:, 8:16], sm0[:, 4:5], None, OP.is_ge),
                      outs=(smB[2],), ins=(smB[1], smB[0]))
                kb.op("v", lambda e: e.tensor_tensor(sm0[:, 5:6], sm0[:, 3:4], sm0[:, 4:5], OP.add),
                      outs=(smB[0],), ins=(smB[0],))
                kb.op("v", lambda e: e.reciprocal(sm0[:, 6:7], sm0[:, 5:6]),
                      outs=(smB[0],), ins=(smB[0],))
                kb.op("v", lambda e: e.tensor_tensor(sm3[:, 0:8], sm1[:, 8:16], sm2[:, 0:8], OP.mult),
                      outs=(smB[3],), ins=(smB[1], smB[2]))
                kb.op("v", lambda e, tc=tc: e.tensor_scalar(
                    self.ctm[:, 8 * tc:8 * tc + 8], sm3[:, 0:8], sm0[:, 6:7], None, OP.mult),
                    outs=(ctmB,), ins=(smB[3], smB[0]))
        for tc in range(TC):
            kb.op("t", lambda e, tc=tc: e.transpose(
                PB[1][0:8, 0:128], self.ctm[:, 8 * tc:8 * tc + 8], identf[:]),
                outs=(PBB[1],), ins=(ctmB, idB))
            kb.op("v", lambda e, tc=tc: e.tensor_copy(
                self.cfm[0:8, ts(tc)], PB[1][0:8, 0:128]),
                outs=(cfmB,), ins=(PBB[1],))

    def emit_moe(self, b2, poolB):
        kb, T, TC, TT = self.kb, self.T, self.TC, self.TT
        assert T == 2048, "MoE gu layout assumes T=2048"
        d = self.dram
        PB, PBB = self.pb, self.pbb
        xsB = [self.B(f"xs{k}") for k in range(KC)]
        scrB = [self.B(f"scr{i}") for i in range(4)]
        stgB = [self.B(f"stg{i}") for i in range(3)]
        vsB = [self.B("vs0"), self.B("vs1")]
        cfmB = self.B("cfm")
        selB = self.B("sel_s")
        wd_r = d["wd_bf"].rearrange("x h o -> (x h) o")
        wgt = d["wgu_t"]
        wgtB = self.B("wgu_t")
        NHZ = 22

        def hzw(hz):
            return 128 if hz < 21 else 64

        def gu_tile(t):
            if t < 16:
                xb = self.xs[t // 2].bitcast(BF16)
                c0 = T + (t % 2) * 1024
                return xb[:, c0:c0 + 1024], xsB[t // 2]
            u = t - 16
            return self.pool[8 + u // 2][:, (u % 2) * 1024:(u % 2) * 1024 + 1024], self.B(f"pl{8 + u // 2}")

        for half in range(2):
            h0 = half * 1024
            for e8 in range(8):
                for tt in range(2):
                    kb.op("t", lambda e, e8=e8, tt=tt, h0=h0: e.matmul(
                        PB[0][:, :], self.sel_s[0:8, ts(e8)], self.cfm[0:8, h0 + tt * 512:h0 + tt * 512 + 512],
                        start=True, stop=True),
                        outs=(PBB[0],), ins=(selB, cfmB))
                    kb.op("v", lambda e, tt=tt: e.tensor_copy(
                        self.vs2[1][:, ts(tt, 512)], PB[0][:, :]),
                        outs=(vsB[1],), ins=(PBB[0],))
                for hz in range(NHZ):
                    w = hzw(hz)
                    stg = self.stg[hz % 2]
                    sB = stgB[hz % 2]
                    row = (e8 * 22 + hz) * 2 * 128
                    eb = kb.buf(f"wgu_t_e{e8}")
                    kb.dma("q", stg[:, 0:8 * w], wgt[row:row + 128, 0:8 * w],
                           outs=(sB,), ins=(eb,))
                    kb.dma("s", stg[:, 1024:1024 + 8 * w], wgt[row + 128:row + 256, 0:8 * w],
                           outs=(sB,), ins=(eb,))
                    gt, gtB = gu_tile(hz)
                    for tt in range(2):
                        bk = 0 if (hz * 2 + tt) % 2 == 0 else 2
                        for which, bank, cb in ((0, bk, 0), (1, bk + 1, 1024)):
                            emits, ins_per = [], []
                            for kc2 in range(KC):
                                emits.append(lambda e, kc2=kc2, tt=tt, bank=bank, cb=cb, stg=stg, w=w, h0=h0: e.matmul(
                                    PB[bank][0:w, :], stg[:, cb + w * kc2:cb + w * kc2 + w],
                                    b2[kc2][:, h0 + tt * 512:h0 + tt * 512 + 512],
                                    start=(kc2 == 0), stop=(kc2 == KC - 1)))
                                ins_per.append((sB, poolB[kc2]))
                            kb.group("t", emits, outs=(PBB[bank],), ins_per=ins_per)
                        sg = self.vs2[0]
                        sgsl = slice(0, 512) if bk == 0 else slice(2080 - 512, 2080)
                        kb.op("s", lambda e, sg=sg, w=w, bk=bk, sgsl=sgsl: e.activation(
                            sg[0:w, sgsl], PB[bk][0:w, :], AF.Silu),
                            outs=(vsB[0],), ins=(PBB[bk],))
                        kb.op("v", lambda e, tt=tt, gt=gt, sg=sg, w=w, bk=bk, sgsl=sgsl: e.tensor_tensor(
                            gt[0:w, ts(tt, 512)], sg[0:w, sgsl], PB[bk + 1][0:w, :], OP.mult),
                            outs=(gtB,), ins=(vsB[0], PBB[bk + 1]))
                first_e = (e8 == 0)
                for ocg in range(2):
                    for tt in range(2):
                        for hz in range(NHZ):
                            w = hzw(hz)
                            wdt = self.stg[2]
                            wdtB = stgB[2]
                            po = 512 * (hz % 2)
                            kb.dma("q", wdt[0:w, po + 1024 * ocg:po + 1024 * ocg + 512],
                                   wd_r[2752 * e8 + 128 * hz:2752 * e8 + 128 * hz + w,
                                        512 * ocg:512 * ocg + 512],
                                   outs=(wdtB,), ins=(self.B("wd_bf"),))
                            gt, gtB = gu_tile(hz)
                            for oci in range(4):
                                kb.op("t", lambda e, hz=hz, oci=oci, tt=tt, po=po, ocg=ocg, gt=gt, wdt=wdt, w=w: e.matmul(
                                    PB[2 + oci][:, :],
                                    wdt[0:w, po + 1024 * ocg + 128 * oci:po + 1024 * ocg + 128 * oci + 128],
                                    gt[0:w, ts(tt, 512)],
                                    start=(hz == 0), stop=(hz == NHZ - 1)),
                                    outs=(PBB[2 + oci],), ins=(wdtB, gtB))
                        t1 = self.vs2[0].bitcast(F32)
                        for oci in range(4):
                            oc = ocg * 4 + oci
                            ysl = self.scr[oc // 2][:, (oc % 2) * 1024 + 512 * tt:(oc % 2) * 1024 + 512 * tt + 512]
                            kb.op("v", lambda e, oci=oci, tt=tt, t1=t1: e.tensor_tensor(
                                t1[:, 256:768], PB[2 + oci][:, :], self.vs2[1][:, ts(tt, 512)], OP.mult),
                                outs=(vsB[0],), ins=(PBB[2 + oci], vsB[1]))
                            if first_e:
                                kb.op("v", lambda e, ysl=ysl, t1=t1: e.tensor_copy(ysl, t1[:, 256:768]),
                                      outs=(scrB[oc // 2],), ins=(vsB[0],))
                            else:
                                kb.op("v", lambda e, ysl=ysl, t1=t1: e.tensor_tensor(ysl, ysl, t1[:, 256:768], OP.add),
                                      outs=(scrB[oc // 2],),
                                      ins=(scrB[oc // 2], vsB[0]))
            p8 = self.pool[8].bitcast(F32)
            p9 = self.pool[9].bitcast(F32)
            p10 = self.pool[10].bitcast(F32)
            for k in range(KC):
                kb.dma("q", p8[:, 0:1024], d["s1_d"][ts(k), h0:h0 + 1024],
                       outs=(self.B("pl8"),), ins=(self.B("s1_d"),))
                kb.dma("q", p9[:, 0:1024], d["hin_d"][ts(k), h0:h0 + 1024],
                       outs=(self.B("pl9"),), ins=(self.B("hin_d"),))
                kb.op("v", lambda e: e.tensor_tensor(p10[:, 0:1024], p8[:, 0:1024], p9[:, 0:1024], OP.add),
                      outs=(self.B("pl10"),), ins=(self.B("pl8"), self.B("pl9")))
                ysl = self.scr[k // 2][:, (k % 2) * 1024:(k % 2) * 1024 + 1024]
                kb.op("v", lambda e, k=k, ysl=ysl, h0=h0: e.tensor_tensor(
                    self.xs[k][:, h0:h0 + 1024], p10[:, 0:1024], ysl, OP.add),
                    outs=(xsB[k],), ins=(self.B("pl10"), scrB[k // 2]))

    def emit_b1(self):
        kb, T, TC, TT = self.kb, self.T, self.TC, self.TT
        d = self.dram
        for wn in ("wq1", "wk1", "wv1", "wo1"):
            kb.dma("g", d[wn + "_bf"], d[wn], outs=(self.B(wn + "_bf"),))
        kb.dma("g", d["wd_bf"].rearrange("x h o -> (x h) o"),
               d["wd"].rearrange("x h o -> (x h) o"), outs=(self.B("wd_bf"),))
        self.prep_wgu_init()
        self.load_vec("q", "g1c", d["g1c"], 8)
        self.load_vec("q", "g2c", d["g2c"], 8)
        xsB = [self.B(f"xs{k}") for k in range(KC)]
        scrB = [self.B(f"scr{i}") for i in range(4)]
        stgB = [self.B(f"stg{i}") for i in range(3)]
        poolB = [self.B(f"pl{i}") for i in range(12)]
        vsB = [self.B("vs0"), self.B("vs1")]
        onesfB = self.B("ones_f_s")
        onesbB = self.B("ones_b_s")
        smB = [self.B(f"sm{i}") for i in range(4)]

        self.ln_stats("ln2", 1e-5, sq_eng="v")
        for k in range(KC):
            kb.dma("s", d["s1_d"][ts(k), :], self.xs[k][:], outs=(self.B("s1_d"),), ins=(xsB[k],))
        b1 = self.pool[0:8]
        for tt in range(TT):
            self.bc_rows(tt)
            mb, rb = self.scr[0], self.scr[1]
            kb.op("v", lambda e: e.tensor_copy(mb[:, 0:512], self.pb[4][:, :]),
                  outs=(scrB[0],), ins=(self.pbb[4],))
            kb.op("v", lambda e: e.tensor_copy(rb[:, 0:512], self.pb[5][:, :]),
                  outs=(scrB[1],), ins=(self.pbb[5],))
            for k in range(KC):
                tmp = self.scr[2]
                kb.op("v", lambda e, k=k, tt=tt: e.tensor_tensor(
                    tmp[:, ts(tt, 512)], self.xs[k][:, ts(tt, 512)], mb[:, 0:512], OP.subtract),
                    outs=(scrB[2],), ins=(xsB[k], scrB[0]))
                kb.op("v", lambda e, k=k, tt=tt: e.tensor_tensor(
                    self.xs[k][:, ts(tt, 512)], tmp[:, ts(tt, 512)], rb[:, 0:512], OP.mult),
                    outs=(xsB[k],), ins=(scrB[2], scrB[1]))
        for k in range(KC):
            kb.dma("s", d["xn2_d"][ts(k), :], self.xs[k][:], outs=(self.B("xn2_d"),), ins=(xsB[k],))
            kb.op("v", lambda e, k=k: e.tensor_scalar(
                b1[k][:], self.xs[k][:], self.vec("g1c", k), None, OP.mult),
                outs=(poolB[k],), ins=(xsB[k], self.B("vecs")))
        self.ln_stats("rms1", 1e-6, rms=True, src=self.xs, src_bufs=xsB, sq_eng="v")

        qx = [self.xs[h].bitcast(BF16) for h in range(KC)]
        PB, PBB = self.pb, self.pbb

        def load_strips(wname):
            for kc2 in range(KC):
                dst = self.stg[kc2 // 4]
                kb.dma("q", dst[:, (kc2 % 4) * D:(kc2 % 4) * D + D], d[wname + "_bf"][ts(kc2), :],
                       outs=(stgB[kc2 // 4],), ins=(self.B(wname + "_bf"),))

        def gemm_1024(dst_fn, rhs_list, rhs_bufs, epi, n_oc=8, banks=(0, 1, 2, 3)):
            for oc in range(n_oc):
                for tt in range(TT):
                    bank = banks[(oc * TT + tt) % len(banks)]
                    emits, ins_per = [], []
                    for kc2 in range(KC):
                        wv = self.stg[kc2 // 4]
                        emits.append(lambda e, kc2=kc2, oc=oc, tt=tt, bank=bank, wv=wv: e.matmul(
                            PB[bank][:, :],
                            wv[:, (kc2 % 4) * D + oc * 128:(kc2 % 4) * D + oc * 128 + 128],
                            rhs_list[kc2][:, ts(tt, 512)],
                            start=(kc2 == 0), stop=(kc2 == KC - 1)))
                        ins_per.append((stgB[kc2 // 4], rhs_bufs[kc2]))
                    kb.group("t", emits, outs=(PBB[bank],), ins_per=ins_per)
                    epi(oc, tt, bank)

        b1B = poolB[0:8]
        for proj, wname, colbase in (("q", "wq1", 0), ("k", "wk1", T)):
            load_strips(wname)
            self.prep_pump(16)

            def epi(oc, tt, bank, colbase=colbase):
                kb.op("v", lambda e, oc=oc, tt=tt, bank=bank: e.tensor_copy(
                    qx[oc][:, colbase + tt * 512: colbase + tt * 512 + 512], PB[bank][:, :]),
                    outs=(xsB[oc],), ins=(PBB[bank],))
                self.prep_pump(1)
            gemm_1024(None, b1, b1B, epi)

        cqs, sqs = self.scr[0], self.scr[1]
        for tt in range(TT):
            self.bc_rows(tt, rows=("rstd",), banks=(4,))
            ct = self.scr[2]
            kb.dma("q", ct[:, 0:512], d["cos_t"][:, ts(tt, 512)], outs=(scrB[2],))
            kb.dma("q", ct[:, 512:1024], d["sin_t"][:, ts(tt, 512)], outs=(scrB[2],))
            kb.op("v", lambda e, tt=tt: e.tensor_tensor(
                cqs[:, ts(tt, 512)], ct[:, 0:512], self.pb[4][:, :], OP.mult),
                outs=(scrB[0],), ins=(scrB[2], self.pbb[4]))
            kb.op("v", lambda e, tt=tt: e.tensor_tensor(
                sqs[:, ts(tt, 512)], ct[:, 512:1024], self.pb[4][:, :], OP.mult),
                outs=(scrB[1],), ins=(scrB[2], self.pbb[4]))
        R2 = self.cid["R2"]
        R2B = self.B("R2_s")
        kb.op("v", lambda e: e.memset(self.small[1][:], 0.0), outs=(smB[1],))
        for colbase in (0, T):
            for hp in range(KC):
                self.prep_pump(2)
                for tt in range(TT):
                    sl = slice(colbase + tt * 512, colbase + tt * 512 + 512)
                    bank = tt % 2
                    kb.op("t", lambda e, hp=hp, sl=sl, bank=bank: e.matmul(
                        PB[bank][:, :], R2[:], qx[hp][:, sl], start=True, stop=True),
                        outs=(PBB[bank],), ins=(xsB[hp], R2B))
                    t1 = self.scr[3]
                    kb.op("v", lambda e, hp=hp, sl=sl, tt=tt: e.tensor_tensor(
                        t1[:, 0:512], qx[hp][:, sl], cqs[:, ts(tt, 512)], OP.mult),
                        outs=(scrB[3],), ins=(xsB[hp], scrB[0]))
                    kb.op("v", lambda e, tt=tt, bank=bank: e.tensor_tensor(
                        t1[:, 512:1024], PB[bank][:, :], sqs[:, ts(tt, 512)], OP.mult),
                        outs=(scrB[3],), ins=(PBB[bank], scrB[1]))
                    kb.op("v", lambda e, hp=hp, sl=sl: e.tensor_tensor(
                        qx[hp][:, sl], t1[:, 0:512], t1[:, 512:1024], OP.add),
                        outs=(xsB[hp],), ins=(scrB[3],))
                    sqt = self.vs2[1]
                    kb.op("v", lambda e, hp=hp, sl=sl: e.tensor_tensor(
                        sqt[:, 0:512], qx[hp][:, sl], qx[hp][:, sl], OP.mult),
                        outs=(vsB[1],), ins=(xsB[hp],))
                    kb.op("t", lambda e: e.matmul(
                        PB[5][0:1, 0:512], self.cid["ones_b"][:, 0:1], sqt[:, 0:512],
                        start=True, stop=True),
                        outs=(PBB[5],), ins=(vsB[1], onesbB))
                    gi = 0 if colbase == 0 else 1
                    kb.op("v", lambda e: e.reduce_max(
                        self.small[1][0:1, 4:5], PB[5][0:1, 0:512], axis=AX),
                        outs=(smB[1],), ins=(PBB[5], smB[1]))
                    kb.op("v", lambda e, gi=gi: e.tensor_tensor(
                        self.small[1][0:1, gi:gi + 1], self.small[1][0:1, gi:gi + 1],
                        self.small[1][0:1, 4:5], OP.max),
                        outs=(smB[1],), ins=(smB[1],))
        kb.op("s", lambda e: e.activation(self.small[1][0:1, 2:4], self.small[1][0:1, 0:2], AF.Sqrt),
              outs=(smB[1],), ins=(smB[1],))
        kb.op("v", lambda e: e.tensor_tensor(
            self.small[1][0:1, 4:5], self.small[1][0:1, 2:3], self.small[1][0:1, 3:4], OP.mult),
            outs=(smB[1],), ins=(smB[1],))
        kb.op("v", lambda e: e.tensor_scalar(
            self.small[1][0:1, 5:6], self.small[1][0:1, 4:5], -0.125, None, OP.mult),
            outs=(smB[1],), ins=(smB[1],))
        kb.op("t", lambda e: e.matmul(
            PB[5][:, 0:1], self.cid["ones_f"][0:1, :], self.small[1][0:1, 5:6],
            start=True, stop=True),
            outs=(PBB[5],), ins=(smB[1], onesfB))
        kb.op("v", lambda e: e.tensor_copy(self.small[2][:, 0:1], PB[5][:, 0:1]),
              outs=(smB[2],), ins=(PBB[5],))
        negm = self.small[2][:, 0:1]

        for tt in range(TT):
            self.bc_rows(tt, rows=("rstd",), banks=(4,))
            kb.op("v", lambda e, tt=tt: e.tensor_copy(self.scr[0][:, ts(tt, 512)], self.pb[4][:, :]),
                  outs=(scrB[0],), ins=(self.pbb[4],))

        load_strips("wv1")
        identb = self.cid["ident_b"]
        idbB = self.B("ident_b_s")
        utf = self.cid["ut_f"]
        utfB = self.B("ut_f_s")
        recrow = self.rows["mean"]
        recrowB = self.rr["mean"]
        for hp in range(KC):
            vfm = self.pool[8]
            for tt in range(TT):
                bank = tt % 2
                emits, ins_per = [], []
                for kc2 in range(KC):
                    wv = self.stg[kc2 // 4]
                    emits.append(lambda e, kc2=kc2, hp=hp, tt=tt, bank=bank, wv=wv: e.matmul(
                        PB[bank][:, :],
                        wv[:, (kc2 % 4) * D + hp * 128:(kc2 % 4) * D + hp * 128 + 128],
                        b1[kc2][:, ts(tt, 512)],
                        start=(kc2 == 0), stop=(kc2 == KC - 1)))
                    ins_per.append((stgB[kc2 // 4], b1B[kc2]))
                kb.group("t", emits, outs=(PBB[bank],), ins_per=ins_per)
                kb.op("v", lambda e, tt=tt, bank=bank: e.tensor_tensor(
                    vfm[:, ts(tt, 512)], PB[bank][:, :], self.scr[0][:, ts(tt, 512)], OP.mult),
                    outs=(poolB[8],), ins=(PBB[bank], scrB[0]))
            vs = self.vs2[0]
            vsb = vsB[0]
            kb.op("v", lambda e: e.memset(vs[:], 1.0), outs=(vsb,))
            for tc in range(TC):
                pt = self.pt[tc % 2]
                ptB = self.ptb[tc % 2]
                kb.op("t", lambda e, tc=tc, pt=pt: e.transpose(
                    pt[:, 0:128], vfm[:, ts(tc)], identb[:]),
                    outs=(ptB,), ins=(poolB[8], idbB))
                kb.op("v", lambda e, tc=tc, pt=pt: e.tensor_copy(
                    vs[:, 130 * tc:130 * tc + 64], pt[:, 0:64]),
                    outs=(vsb,), ins=(ptB,))
                kb.op("v", lambda e, tc=tc, pt=pt: e.tensor_copy(
                    vs[:, 130 * tc + 65:130 * tc + 129], pt[:, 64:128]),
                    outs=(vsb,), ins=(ptB,))
            for hl in range(2):
                hs = slice(64 * hl, 64 * hl + 64)
                for j in range(TT):
                    self.prep_pump(3)
                    bO = 2 + (j % 2)
                    n_ck = min(4 * j + 4, TC)
                    for ck in range(n_ck):
                        bA = ck % 2
                        kb.op("t", lambda e, hp=hp, hs=hs, j=j, ck=ck, bA=bA: e.matmul(
                            PB[bA][:, :],
                            qx[hp][hs, slice(T + 128 * ck, T + 128 * ck + 128)],
                            qx[hp][hs, ts(j, 512)],
                            start=True, stop=True),
                            outs=(PBB[bA],), ins=(xsB[hp],))
                        pts = self.vs2[1]
                        po = 512 * (ck % 2)
                        kb.op("s", lambda e, bA=bA, po=po, pts=pts: e.activation(
                            pts[:, po:po + 512], PB[bA][:, :], AF.Exp, bias=negm, scale=0.125),
                            outs=(vsB[1],), ins=(PBB[bA], smB[2]))
                        dz = (ck - 4 * j) * 128
                        if dz > 0:
                            kb.op("v", lambda e, po=po, dz=dz, pts=pts: e.memset(
                                pts[:, po:po + dz], 0.0),
                                outs=(vsB[1],), ins=())
                        if 0 <= dz < 512:
                            kb.op("v", lambda e, po=po, dz=dz, pts=pts: e.tensor_tensor(
                                pts[:, po + dz:po + dz + 128], pts[:, po + dz:po + dz + 128],
                                utf[:], OP.mult),
                                outs=(vsB[1],), ins=(vsB[1], utfB))
                        kb.op("t", lambda e, ck=ck, hl=hl, bO=bO, po=po, pts=pts, n_ck=n_ck: e.matmul(
                            PB[bO][0:65, :], self.vs2[0][:, 130 * ck + 65 * hl:130 * ck + 65 * hl + 65],
                            pts[:, po:po + 512],
                            start=(ck == 0), stop=(ck == n_ck - 1)),
                            outs=(PBB[bO],), ins=(vsB[0], vsB[1]))
                    kb.op("v", lambda e, bO=bO: e.reciprocal(recrow[:, 0:512], PB[bO][64:65, :]),
                          outs=(recrowB,), ins=(PBB[bO],))
                    kb.op("t", lambda e, bO=bO: e.matmul(
                        PB[4][0:64, :], self.cid["ones_f"][0:1, 0:64], recrow[:, 0:512],
                        start=True, stop=True),
                        outs=(PBB[4],), ins=(recrowB, onesfB))
                    kb.op("v", lambda e: e.tensor_copy(self.scr[3][:, 0:512], PB[4][0:64, :][0:64, :])
                          if False else e.tensor_copy(self.scr[3][0:64, 0:512], PB[4][0:64, :]),
                          outs=(scrB[3],), ins=(PBB[4],))
                    kb.op("v", lambda e, hp=hp, hs=hs, j=j, bO=bO: e.tensor_tensor(
                        qx[hp][hs, ts(j, 512)], PB[bO][0:64, :], self.scr[3][0:64, 0:512], OP.mult),
                        outs=(xsB[hp],), ins=(PBB[bO], scrB[3]))

        load_strips("wo1")
        attn2 = qx
        hb = self.vs2[1]
        for oc in range(KC):
            hin = self.scr[1 + (oc % 2)]
            hinB = scrB[1 + (oc % 2)]
            kb.dma("q", hin[:, 0:T], d["xn2_d"][ts(oc), :],
                   outs=(hinB,), ins=(self.B("xn2_d"),))
            for tt in range(TT):
                bank = tt % 2
                emits, ins_per = [], []
                for kc2 in range(KC):
                    wv = self.stg[kc2 // 4]
                    emits.append(lambda e, kc2=kc2, oc=oc, tt=tt, bank=bank, wv=wv: e.matmul(
                        PB[bank][:, :],
                        wv[:, (kc2 % 4) * D + oc * 128:(kc2 % 4) * D + oc * 128 + 128],
                        attn2[kc2][:, ts(tt, 512)],
                        start=(kc2 == 0), stop=(kc2 == KC - 1)))
                    ins_per.append((stgB[kc2 // 4], xsB[kc2]))
                kb.group("t", emits, outs=(PBB[bank],), ins_per=ins_per)
                kb.op("v", lambda e, tt=tt, bank=bank, hin=hin: e.tensor_tensor(
                    hin[:, ts(tt, 512)], hin[:, ts(tt, 512)], PB[bank][:, :], OP.add),
                    outs=(hinB,), ins=(hinB, PBB[bank]))
                kb.op("v", lambda e, tt=tt, hin=hin: e.tensor_copy(
                    hb[:, 0:512], hin[:, ts(tt, 512)]),
                    outs=(vsB[1],), ins=(hinB,))
                kb.op("v", lambda e: e.tensor_tensor(hb[:, 512:1024], hb[:, 0:512], hb[:, 0:512], OP.mult),
                      outs=(vsB[1],), ins=(vsB[1],))
                kb.op("t", lambda e, oc=oc, tt=tt: e.matmul(
                    self.pb[2 + tt][0:1, :],
                    self.cid["ones_b"][:, 0:1], hb[:, 512:1024],
                    start=(oc == 0), stop=(oc == KC - 1)),
                    outs=(self.pbb[2 + tt],), ins=(vsB[1], onesbB))
            kb.dma("s", d["hin_d"][ts(oc), :], hin[:, 0:T], outs=(self.B("hin_d"),), ins=(hinB,))
        self.rms_rows_from_banks()

        b2 = self.pool[0:8]
        for oc in range(KC):
            hin = self.scr[1 + (oc % 2)]
            hinB = scrB[1 + (oc % 2)]
            kb.dma("q", hin[:, 0:T], d["hin_d"][ts(oc), :], outs=(hinB,), ins=(self.B("hin_d"),))
            for tt in range(TT):
                self.bc_rows(tt, rows=("rstd",), banks=(4,))
                kb.op("v", lambda e, oc=oc, tt=tt, hin=hin: e.tensor_tensor(
                    self.scr[3][:, 0:512], hin[:, ts(tt, 512)], self.pb[4][:, :], OP.mult),
                    outs=(scrB[3],), ins=(hinB, self.pbb[4]))
                kb.op("v", lambda e, oc=oc, tt=tt: e.tensor_scalar(
                    b2[oc][:, ts(tt, 512)], self.scr[3][:, 0:512], self.vec("g2c", oc), None, OP.mult),
                    outs=(poolB[oc],), ins=(scrB[3], self.B("vecs")))

        self.prep_pump(10000)
        self.emit_router(b2, poolB)

        self.emit_moe(b2, poolB)

    def emit_b2(self):
        kb, T, TC, TT = self.kb, self.T, self.TC, self.TT
        d = self.dram
        xsB = [self.B(f"xs{k}") for k in range(KC)]
        scrB = [self.B(f"scr{i}") for i in range(4)]
        stgB = [self.B(f"stg{i}") for i in range(3)]
        poolB = [self.B(f"pl{i}") for i in range(12)]
        vsB = [self.B("vs0"), self.B("vs1")]
        smB = [self.B(f"sm{i}") for i in range(4)]
        PB, PBB = self.pb, self.pbb
        sm3 = self.small[3]
        for wn in ("Wr", "Wk", "Wv", "Wo"):
            kb.dma("g", d[wn + "_bf"], d[wn], outs=(self.B(wn + "_bf"),))
        self.load_vec("q", "decc", d["decc"], 8)
        self.load_vec("q", "lnwc", d["lnwc"], 8)
        kb.op("s", lambda e: e.activation(sm3[:, 0:8], self.vec("decc"), AF.Sigmoid),
              outs=(smB[3],), ins=(self.B("vecs"),))
        kb.op("v", lambda e: e.tensor_scalar(sm3[:, 0:8], sm3[:, 0:8], 1e-7, None, OP.max),
              outs=(smB[3],), ins=(smB[3],))
        kb.op("s", lambda e: e.activation(sm3[:, 8:16], sm3[:, 0:8], AF.Ln),
              outs=(smB[3],), ins=(smB[3],))

        self.ln_stats("ln3", 1e-5)
        xn3 = self.pool[0:8]
        for k in range(KC):
            kb.dma("s", d["s2_d"][ts(k), :], self.xs[k][:], outs=(self.B("s2_d"),), ins=(xsB[k],))
        for tt in range(TT):
            self.bc_rows(tt)
            mb, rb = self.scr[0], self.scr[1]
            kb.op("v", lambda e: e.tensor_copy(mb[:, 0:512], self.pb[4][:, :]),
                  outs=(scrB[0],), ins=(self.pbb[4],))
            kb.op("v", lambda e: e.tensor_copy(rb[:, 0:512], self.pb[5][:, :]),
                  outs=(scrB[1],), ins=(self.pbb[5],))
            for k in range(KC):
                tmp = self.scr[2]
                kb.op("g", lambda e, k=k, tt=tt: e.tensor_tensor(
                    tmp[:, ts(tt, 512)], self.xs[k][:, ts(tt, 512)], mb[:, 0:512], OP.subtract),
                    outs=(scrB[2],), ins=(xsB[k], scrB[0]))
                kb.op("v", lambda e, k=k, tt=tt: e.tensor_tensor(
                    xn3[k][:, ts(tt, 512)], tmp[:, ts(tt, 512)], rb[:, 0:512], OP.mult),
                    outs=(poolB[k],), ins=(scrB[2], scrB[1]))

        Wr_bf, Wk_bf, Wv_bf = d["Wr_bf"], d["Wk_bf"], d["Wv_bf"]
        for oc in range(KC):
            stg = self.stg[0]
            sB = stgB[0]
            for pi, wbf, wB in ((0, Wr_bf, self.B("Wr_bf")), (1, Wk_bf, self.B("Wk_bf")),
                                (2, Wv_bf, self.B("Wv_bf"))):
                for kc2 in range(KC):
                    kb.dmat("q", stg[:, pi * 1024 + 128 * kc2:pi * 1024 + 128 * kc2 + 128],
                            wbf[oc * 128:oc * 128 + 128, ts(kc2)],
                            outs=(sB,), ins=(wB,))
            for tt in range(TT):
                for pi, bank in ((0, 0), (1, 1), (2, 2)):
                    emits, ins_per = [], []
                    for kc2 in range(KC):
                        emits.append(lambda e, kc2=kc2, tt=tt, pi=pi, bank=bank, stg=stg: e.matmul(
                            PB[bank][:, :], stg[:, pi * 1024 + 128 * kc2:pi * 1024 + 128 * kc2 + 128],
                            xn3[kc2][:, ts(tt, 512)],
                            start=(kc2 == 0), stop=(kc2 == KC - 1)))
                        ins_per.append((sB, poolB[kc2]))
                    kb.group("t", emits, outs=(PBB[bank],), ins_per=ins_per)
                kb.op("s", lambda e, tt=tt: e.activation(
                    self.scr[3][:, ts(tt, 512)], PB[0][:, :], AF.Sigmoid),
                    outs=(scrB[3],), ins=(PBB[0],))
                kb.op("v", lambda e, tt=tt: e.tensor_copy(self.scr[2][:, ts(tt, 512)], PB[1][:, :]),
                      outs=(scrB[2],), ins=(PBB[1],))
                kb.op("v", lambda e, tt=tt: e.tensor_tensor(
                    self.scr[2][:, ts(tt, 512)], self.scr[2][:, ts(tt, 512)], PB[2][:, :], OP.mult),
                    outs=(scrB[2],), ins=(scrB[2], PBB[2]))
            kb.dma("q", self.scr[0][:], d["t_bc"], outs=(scrB[0],))
            ldap = sm3[:, 8 + oc:9 + oc]
            kb.op("v", lambda e, ldap=ldap: e.tensor_scalar(
                self.scr[0][:], self.scr[0][:], ldap, None, OP.mult),
                outs=(scrB[0],), ins=(scrB[0], smB[3]))
            kb.op("v", lambda e: e.tensor_scalar(self.scr[1][:], self.scr[0][:], -1.0, 23.1,
                                                 OP.mult, OP.min),
                  outs=(scrB[1],), ins=(scrB[0],))
            kb.op("s", lambda e: e.activation(self.scr[1][:], self.scr[1][:], AF.Exp),
                  outs=(scrB[1],), ins=(scrB[1],))
            kb.op("v", lambda e: e.tensor_scalar(self.scr[1][:], self.scr[1][:], 1e10, None, OP.min),
                  outs=(scrB[1],), ins=(scrB[1],))
            kb.op("s", lambda e: e.activation(self.scr[0][:], self.scr[0][:], AF.Exp),
                  outs=(scrB[0],), ins=(scrB[0],))
            kb.op("v", lambda e: e.tensor_tensor(
                self.scr[2][:], self.scr[2][:], self.scr[1][:], OP.mult),
                outs=(scrB[2],), ins=(scrB[2], scrB[1]))
            kb.op("v", lambda e: e.tensor_tensor_scan(
                self.scr[2][:], self.scr[2][:], self.scr[2][:], 0.0, OP.add, OP.bypass),
                outs=(scrB[2],), ins=(scrB[2],))
            kb.op("v", lambda e: e.tensor_tensor(
                self.scr[2][:], self.scr[2][:], self.scr[0][:], OP.mult),
                outs=(scrB[2],), ins=(scrB[2], scrB[0]))
            kb.op("v", lambda e, oc=oc: e.tensor_tensor(
                self.xs[oc][:], self.scr[2][:], self.scr[3][:], OP.mult),
                outs=(xsB[oc],), ins=(scrB[2], scrB[3]))

        if getattr(self, "b2_stop", None) == "rs":
            return
        self.ln_stats("rms_l1", 1e-6, rms=True, src=self.xs, src_bufs=xsB)
        ybf = self.pool[0:8]
        for oc in range(KC):
            for tt in range(TT):
                self.bc_rows(tt, rows=("rstd",), banks=(4,))
                kb.op("v", lambda e, oc=oc, tt=tt: e.tensor_tensor(
                    self.scr[2][:, 0:512], self.xs[oc][:, ts(tt, 512)], self.pb[4][:, :], OP.mult),
                    outs=(scrB[2],), ins=(xsB[oc], self.pbb[4]))
                kb.op("v", lambda e, oc=oc, tt=tt: e.tensor_scalar(
                    ybf[oc][:, ts(tt, 512)], self.scr[2][:, 0:512], self.vec("lnwc", oc), None, OP.mult),
                    outs=(poolB[oc],), ins=(scrB[2], self.B("vecs")))

        for kc2 in range(KC):
            dst = self.stg[kc2 // 4]
            kb.dmat("q", dst[:, (kc2 % 4) * D:(kc2 % 4) * D + D],
                    d["Wo_bf"][:, ts(kc2)], outs=(stgB[kc2 // 4],), ins=(self.B("Wo_bf"),))
        for oc in range(KC):
            s2t = self.scr[oc % 2]
            s2tB = scrB[oc % 2]
            kb.dma("q", s2t[:], d["s2_d"][ts(oc), :], outs=(s2tB,), ins=(self.B("s2_d"),))
            for tt in range(TT):
                bank = tt % 4
                emits, ins_per = [], []
                for kc2 in range(KC):
                    wv = self.stg[kc2 // 4]
                    emits.append(lambda e, kc2=kc2, oc=oc, tt=tt, bank=bank, wv=wv: e.matmul(
                        PB[bank][:, :],
                        wv[:, (kc2 % 4) * D + oc * 128:(kc2 % 4) * D + oc * 128 + 128],
                        ybf[kc2][:, ts(tt, 512)],
                        start=(kc2 == 0), stop=(kc2 == KC - 1)))
                    ins_per.append((stgB[kc2 // 4], poolB[kc2]))
                kb.group("t", emits, outs=(PBB[bank],), ins_per=ins_per)
                kb.op("v", lambda e, oc=oc, tt=tt, bank=bank, s2t=s2t: e.tensor_tensor(
                    self.xs[oc][:, ts(tt, 512)], s2t[:, ts(tt, 512)], PB[bank][:, :], OP.add),
                    outs=(xsB[oc],), ins=(s2tB, PBB[bank]))

    def emit_b3(self):
        kb, T, TC, TT = self.kb, self.T, self.TC, self.TT
        d = self.dram
        xsB = [self.B(f"xs{k}") for k in range(KC)]
        scrB = [self.B(f"scr{i}") for i in range(4)]
        stgB = [self.B(f"stg{i}") for i in range(3)]
        poolB = [self.B(f"pl{i}") for i in range(12)]
        PB, PBB = self.pb, self.pbb
        for wn, src_n in (("w1_bf", "w1w"), ("w2_bf", "w2w"), ("w3_bf", "w3w")):
            kb.dma("g", d[wn], d[src_n], outs=(self.B(wn),))
        for wn, src_n in (("h1_bf", "h1w"), ("h2_bf", "h2w")):
            for i in range(4):
                kb.dma("g", d[wn][ts(i, 1024), :], d[src_n][ts(i, 1024), :], outs=(self.B(wn),))
        self.load_vec("q", "s1c", d["s1c"], 32)
        self.load_vec("q", "s2c", d["s2c"], 32)
        self.load_vec("q", "hs1c", d["hs1c"], 32)
        self.load_vec("q", "hs2c", d["hs2c"], 32)
        self.load_vec("q", "s3c", d["s3c"], 8)

        self.ln_stats("ln4", 1e-5)
        for k in range(KC):
            kb.dma("s", d["s3_d"][ts(k), :], self.xs[k][:], outs=(self.B("s3_d"),), ins=(xsB[k],))

        slots = []
        for i in range(3):
            for j in range(8):
                slots.append((self.stg[i][:, 512 * j:512 * j + 512], stgB[i]))
        for pi in (10, 11):
            for j in range(4):
                slots.append((self.pool[pi][:, 512 * j:512 * j + 512], poolB[pi]))
        self.slot_i = 0

        def next_slot():
            s = slots[self.slot_i % len(slots)]
            self.slot_i += 1
            return s

        def hx(i):
            return self.xs[i // 8].bitcast(BF16)[:, (i % 8) * 512:(i % 8) * 512 + 512], xsB[i // 8]

        def gx(i):
            return self.xs[4 + i // 8].bitcast(BF16)[:, (i % 8) * 512:(i % 8) * 512 + 512], xsB[4 + i // 8]

        def big_gemm(wbf_name, n_out, n_k, rhs_fn, epi, tq):
            wbf = d[wbf_name]
            wB = self.B(wbf_name)
            for ocg in range((n_out + 3) // 4):
                ocs = [ocg * 4 + i for i in range(4) if ocg * 4 + i < n_out]
                burst_slots = {}
                for kc0 in range(0, n_k, 8):
                    cur = []
                    for kc2 in range(kc0, min(kc0 + 8, n_k)):
                        sl, slB = next_slot()
                        kb.dmat("q", sl, wbf[ocg * 512:ocg * 512 + 512, ts(kc2)],
                                outs=(slB,), ins=(wB,))
                        cur.append((kc2, sl, slB))
                    for oci, oc in enumerate(ocs):
                        for bi, (kc2, sl, slB) in enumerate(cur):
                            rhs, rhsB = rhs_fn(kc2)
                            kb.op("t", lambda e, sl=sl, oci=oci, oc=oc, rhs=rhs, kc2=kc2, n_k=n_k: e.matmul(
                                PB[oci][:, :], sl[:, 128 * (oc % 4):128 * (oc % 4) + 128], rhs,
                                start=(kc2 == 0), stop=(kc2 == n_k - 1)),
                                outs=(PBB[oci],), ins=(slB, rhsB))
                for oci, oc in enumerate(ocs):
                    epi(oc, oci)

        identf = self.cid["ident_f"]
        idB = self.B("ident_f_s")
        for tt in range(TT):
            self.bc_rows(tt)
            mb, rb = self.scr[0], self.scr[1]
            kb.op("v", lambda e: e.tensor_copy(mb[:, 0:512], self.pb[4][:, :]),
                  outs=(scrB[0],), ins=(self.pbb[4],))
            kb.op("v", lambda e: e.tensor_copy(rb[:, 0:512], self.pb[5][:, :]),
                  outs=(scrB[1],), ins=(self.pbb[5],))
            for kc2 in range(KC):
                tmp = self.scr[2]
                kb.op("g", lambda e, kc2=kc2, tt=tt: e.tensor_tensor(
                    tmp[:, ts(tt, 512)], self.xs[kc2][:, ts(tt, 512)], mb[:, 0:512], OP.subtract),
                    outs=(scrB[2],), ins=(xsB[kc2], scrB[0]))
                kb.op("v", lambda e, kc2=kc2, tt=tt: e.tensor_tensor(
                    self.pool[kc2][:, ts(tt, 512)], tmp[:, ts(tt, 512)], rb[:, 0:512], OP.mult),
                    outs=(poolB[kc2],), ins=(scrB[2], scrB[1]))
        for tq in range(TT):
            q0 = tq * 512

            def xn4(kc2, q0=q0):
                return (self.pool[kc2][:, q0:q0 + 512], poolB[kc2])
            if getattr(self, "b2_stop", None) == "xn4":
                return
            def epi_w1(oc, bank):
                sgt, sgtB = gx(oc)
                kb.op("s", lambda e, oc=oc, bank=bank, sgt=sgt: e.activation(
                    sgt, PB[bank][:, :], AF.Silu, scale=self.vec("s1c", oc)),
                    outs=(sgtB,), ins=(PBB[bank], self.B("vecs")))
            def epi_w2(oc, bank):
                sgt, sgtB = gx(oc)
                ht, htB = hx(oc)
                kb.op("v", lambda e, oc=oc, bank=bank: e.tensor_scalar(
                    self.scr[2][:, 0:512], PB[bank][:, :], self.vec("s2c", oc), None, OP.mult),
                    outs=(scrB[2],), ins=(PBB[bank], self.B("vecs")))
                kb.op("v", lambda e, ht=ht, sgt=sgt: e.tensor_tensor(
                    ht, sgt, self.scr[2][:, 0:512], OP.mult),
                    outs=(htB,), ins=(sgtB, scrB[2]))
            big_gemm("w1_bf", 32, 8, xn4, epi_w1, tq)
            big_gemm("w2_bf", 32, 8, xn4, epi_w2, tq)
            if getattr(self, "b2_stop", None) == "h":
                return

            def epi_h1(oc, bank):
                gt, gtB = gx(oc)
                kb.op("s", lambda e, oc=oc, bank=bank, gt=gt: e.activation(
                    gt, PB[bank][:, :], AF.Gelu, scale=self.vec("hs1c", oc)),
                    outs=(gtB,), ins=(PBB[bank], self.B("vecs")))
            big_gemm("h1_bf", 32, 32, hx, epi_h1, tq)
            if getattr(self, "b2_stop", None) == "g":
                return

            def epi_h2(oc, bank):
                ht, htB = hx(oc)
                kb.op("s", lambda e, oc=oc, bank=bank, ht=ht: e.activation(
                    ht, PB[bank][:, :], AF.Gelu, scale=self.vec("hs2c", oc)),
                    outs=(htB,), ins=(PBB[bank], self.B("vecs")))
            big_gemm("h2_bf", 32, 32, gx, epi_h2, tq)

            def epi_w3(oc, bank):
                s3t = self.scr[1 + (oc % 2)]
                s3tB = scrB[1 + (oc % 2)]
                kb.dma("q", s3t[:, 0:512], d["s3_d"][ts(oc), q0:q0 + 512],
                       outs=(s3tB,), ins=(self.B("s3_d"),))
                kb.op("v", lambda e, oc=oc, bank=bank, s3t=s3t: e.scalar_tensor_tensor(
                    self.scr[3][:, 0:512], PB[bank][:, :], self.vec("s3c", oc), s3t[:, 0:512],
                    OP.mult, OP.add),
                    outs=(scrB[3],), ins=(PBB[bank], self.B("vecs"), s3tB))
                if self.io_out == "fm":
                    kb.dma("s", d["y_out_fm"][ts(oc), q0:q0 + 512], self.scr[3][:, 0:512],
                           outs=(self.B("y_out_fm"),), ins=(scrB[3],))
                else:
                    def stage(tb):
                        if tb < 2:
                            return (self.pool[8 + tb].bitcast(F32)[:, 0:1024],
                                    poolB[8 + tb])
                        return self.scr[0][:, 1024 * (tb - 2):1024 * (tb - 2) + 1024], scrB[0]
                    for tb in range(4):
                        bb = 4 + tb % 2
                        stg_ap, stg_b = stage(tb)
                        kb.op("t", lambda e, tb=tb, bb=bb: e.transpose(
                            PB[bb][:, 0:128], self.scr[3][:, 128 * tb:128 * tb + 128], identf[:]),
                            outs=(PBB[bb],), ins=(scrB[3], idB))
                        kb.op("v", lambda e, oc=oc, bb=bb, stg_ap=stg_ap: e.tensor_copy(
                            stg_ap[:, 128 * oc:128 * oc + 128], PB[bb][:, 0:128]),
                            outs=(stg_b,), ins=(PBB[bb],))
                        if oc == KC - 1:
                            kb.dma("s", d["y_out"][q0 + 128 * tb:q0 + 128 * tb + 128, :],
                                   stg_ap[:, 0:1024],
                                   outs=(self.B("y_out"),), ins=(stg_b,))
            big_gemm("w3_bf", 8, 32, hx, epi_w3, tq)

    def finalize(self):
        kb = self.kb
        for sn, c in list(kb.cnt.items()):
            if sn.startswith("D") and c > 0:
                kb._wait("s", sn, c)


def build_program(T=2048, blocks=("b0",), io_in="tm", io_out="fm", b2_stop=None):
    nc = bass.Bass("TRN2", target_bir_lowering=False)
    p = Prog(nc, T, blocks, io_in, io_out)
    p.b2_stop = b2_stop
    p.declare()
    for e in ("t", "v", "s", "g"):
        p.kb.sem(e)
    p.emit_input()
    if "b0" in blocks:
        p.emit_b0()
    if "b1" in blocks:
        p.emit_b1()
    if "b2" in blocks:
        p.emit_b2()
    if "b3" in blocks:
        p.emit_b3()
    if "b3" not in blocks:
        p.emit_output()
    p.finalize()

    kb = p.kb
    def replay(eng, name):
        for entry in kb.q[name]:
            if entry[0] == "w":
                eng.wait_ge(kb.sems[entry[1]], entry[2])
            else:
                inst = entry[1](eng)
                if entry[2] is not None:
                    inst.then_inc(kb.sems[entry[2]], entry[3])

    with nc.Block() as block:
        @block.sync
        def _(sync):
            replay(sync, "q")

        @block.tensor
        def _(tensor):
            replay(tensor, "t")

        @block.vector
        def _(vector):
            replay(vector, "v")

        @block.scalar
        def _(scalar):
            replay(scalar, "s")

        @block.gpsimd
        def _(gpsimd):
            replay(gpsimd, "g")
    kb.st.close()
    return nc, p


def host_consts(T):
    c = {}
    c["ident_f"] = np.eye(128, dtype=np.float32)
    c["ident_b"] = np.eye(128, dtype=np.float32).astype(ml_dtypes.bfloat16)
    c["ones_f"] = np.ones((128, 128), np.float32)
    c["ones_b"] = np.ones((128, 128), np.float32).astype(ml_dtypes.bfloat16)
    c["ut_f"] = np.triu(np.ones((128, 128), np.float32))
    R = np.zeros((64, 64), np.float32)
    for m in range(32):
        R[m + 32, m] = -1.0
    for m in range(32, 64):
        R[m - 32, m] = 1.0
    R2 = np.zeros((128, 128), np.float32)
    R2[0:64, 0:64] = R
    R2[64:128, 64:128] = R
    c["R2"] = R2.astype(ml_dtypes.bfloat16)
    dim = 64
    inv_freq = 1.0 / (10000.0 ** (np.arange(0, dim, 2, dtype=np.float32) / dim))
    tpos = np.arange(T, dtype=np.float32)
    freqs = np.einsum("i,j->ij", tpos, inv_freq)
    emb = np.concatenate([freqs, freqs], axis=-1)
    cos = np.cos(emb).T.astype(np.float32)
    sin = np.sin(emb).T.astype(np.float32)
    c["cos_t"] = np.concatenate([cos, cos], axis=0)
    c["sin_t"] = np.concatenate([sin, sin], axis=0)
    c["t_bc"] = np.broadcast_to(np.arange(T, dtype=np.float32), (128, T)).copy()
    sel = np.zeros((8, 1024), np.float32)
    for e in range(8):
        sel[e, 128 * e:128 * (e + 1)] = 1.0
    c["sel8"] = sel.astype(ml_dtypes.bfloat16)
    cc = np.zeros((128, 8), np.float32)
    cc[:, 0] = 1e-5
    cc[:, 1] = 1e-6
    cc[:, 3] = 1.0
    c["constc"] = cc
    return c


def vec_cols(v, groups, width=128):
    return np.ascontiguousarray(v.reshape(groups, width).T.astype(np.float32))


def _f32(a):
    return np.ascontiguousarray(np.asarray(a, dtype=np.float32))


def block_weight_map(blk, inputs):
    m = {}
    if blk == "b0":
        m["w_qkv"] = _f32(inputs["L0_attn_w_qkv"])
        m["bqc"] = vec_cols(_f32(inputs["L0_attn_b_qkv"]), 24)
        m["w_out"] = _f32(inputs["L0_attn_w_out"])
        m["boc"] = vec_cols(_f32(inputs["L0_attn_b_out"]), 8)
    elif blk == "b1":
        m["wq1"] = _f32(inputs["L0_ffn_wq"])
        m["wk1"] = _f32(inputs["L0_ffn_wk"])
        m["wv1"] = _f32(inputs["L0_ffn_wv"])
        m["wo1"] = _f32(inputs["L0_ffn_wo"])
        m["w_router"] = _f32(inputs["L0_ffn_w_router"])
        m["wg"] = _f32(inputs["L0_ffn_wg"])
        m["wu"] = _f32(inputs["L0_ffn_wu"])
        m["wd"] = _f32(inputs["L0_ffn_wd"])
        m["g1c"] = vec_cols(_f32(inputs["L0_ffn_g1"]), 8)
        m["g2c"] = vec_cols(_f32(inputs["L0_ffn_g2"]), 8)
    elif blk == "b2":
        m["Wr"] = _f32(inputs["L1_attn_Wr"])
        m["Wk"] = _f32(inputs["L1_attn_Wk"])
        m["Wv"] = _f32(inputs["L1_attn_Wv"])
        m["Wo"] = _f32(inputs["L1_attn_Wo"])
        m["decc"] = vec_cols(_f32(inputs["L1_attn_decay"]), 8)
        m["lnwc"] = vec_cols(_f32(inputs["L1_attn_ln_w"]), 8)
    elif blk == "b3":
        m["w1w"] = _f32(inputs["L1_ffn_w1_w"])
        m["w2w"] = _f32(inputs["L1_ffn_w2_w"])
        m["h1w"] = _f32(inputs["L1_ffn_h1_w"])
        m["h2w"] = _f32(inputs["L1_ffn_h2_w"])
        m["w3w"] = _f32(inputs["L1_ffn_w3_w"])
        m["s1c"] = vec_cols(_f32(inputs["L1_ffn_w1_s"]), 32)
        m["s2c"] = vec_cols(_f32(inputs["L1_ffn_w2_s"]), 32)
        m["hs1c"] = vec_cols(_f32(inputs["L1_ffn_h1_s"]), 32)
        m["hs2c"] = vec_cols(_f32(inputs["L1_ffn_h2_s"]), 32)
        m["s3c"] = vec_cols(_f32(inputs["L1_ffn_w3_s"]), 8)
    return m


FUSED = True


def kernel(**inputs):
    T = 2048
    n = 8
    consts = host_consts(T)
    x0 = _f32(inputs["L0_attn_x"])
    if FUSED:
        nc, p = build_program(T=T, blocks=("b0", "b1", "b2", "b3"), io_in="tm", io_out="tm")
        wmap = {}
        for blk in ("b0", "b1", "b2", "b3"):
            wmap.update(block_weight_map(blk, inputs))
        in_maps = []
        for c in range(n):
            m = dict(consts)
            m.update(wmap)
            m["x_in"] = np.ascontiguousarray(x0[c])
            in_maps.append(m)
        res = run_bass_kernel_spmd(nc, in_maps, core_ids=list(range(n)))
        return np.stack([np.ascontiguousarray(np.asarray(res.results[c]["y_out"], dtype=np.float32))
                         for c in range(n)], axis=0)
    cur = [np.ascontiguousarray(x0[c]) for c in range(n)]
    specs = [("b0", "tm", "fm"), ("b1", "fm", "fm"), ("b2", "fm", "fm"), ("b3", "fm", "tm")]
    for blk, ioi, ioo in specs:
        nc, p = build_program(T=T, blocks=(blk,), io_in=ioi, io_out=ioo)
        wmap = block_weight_map(blk, inputs)
        in_maps = []
        for c in range(n):
            m = dict(consts)
            m.update(wmap)
            m["x_in" if ioi == "tm" else "x_in_fm"] = cur[c]
            in_maps.append(m)
        res = run_bass_kernel_spmd(nc, in_maps, core_ids=list(range(n)))
        key = "y_out" if ioo == "tm" else "y_out_fm"
        cur = [np.ascontiguousarray(np.asarray(res.results[c][key], dtype=np.float32))
               for c in range(n)]
    return np.stack(cur, axis=0)


# version 09
# speedup vs baseline: 1.2003x; 1.1226x over previous
import numpy as np
import ml_dtypes
from contextlib import ExitStack

import concourse.bass as bass
import concourse.mybir as mybir
from concourse.bass_utils import run_bass_kernel_spmd
from concourse.alu_op_type import AluOpType as OP

F32 = mybir.dt.float32
BF16 = mybir.dt.bfloat16
AF = mybir.ActivationFunctionType
AX = mybir.AxisListType.X

D = 1024
KC = 8


class Buf:
    __slots__ = ("name", "writer", "readers")

    def __init__(self, name):
        self.name = name
        self.writer = None
        self.readers = {}


class KB:
    COMPUTE = {"t": 1, "v": 1, "s": 1, "g": 1}
    DMA_SEM = {"q": "dq", "s": "ds", "g": "dg", "v": "dv", "t": "dt"}

    def __init__(self, nc, T):
        self.nc = nc
        self.T = T
        self.TC = T // 128
        self.TT = T // 512
        self.st = ExitStack()
        self.eng = None
        self.sems = {}
        self.cnt = {}
        self.obs = {}
        self.bufs = {}
        self.q = {e: [] for e in ("t", "v", "s", "g", "q")}

    def sem(self, name):
        if name not in self.sems:
            self.sems[name] = self.st.enter_context(self.nc.semaphore(name))
            self.cnt[name] = 0
        return self.sems[name]

    def S(self, name, shape, dtype):
        h = self.st.enter_context(self.nc.sbuf_tensor(name, shape, dtype))
        self.bufs[name] = Buf(name)
        return h

    def PS(self, name, shape, dtype):
        h = self.st.enter_context(self.nc.psum_tensor(name, shape, dtype))
        self.bufs[name] = Buf(name)
        return h

    def buf(self, name):
        return self.bufs[name]

    def _wait(self, e, sem_name, val):
        if val <= 0:
            return
        key = (e, sem_name)
        if self.obs.get(key, 0) >= val:
            return
        self.q[e].append(("w", sem_name, val))
        self.obs[key] = val

    SELF_SYNC = True

    def _pre(self, e, outs, ins, is_compute):
        skip_own = is_compute and (e == "t" or not self.SELF_SYNC)
        own = e if skip_own else None
        for b in ins:
            if b.writer:
                ws, wc = b.writer
                if ws != own:
                    self._wait(e, ws, wc)
        for b in outs:
            if b.writer:
                ws, wc = b.writer
                if ws != own:
                    self._wait(e, ws, wc)
            for rs, rc in b.readers.items():
                if rs != own:
                    self._wait(e, rs, rc)

    def _post(self, sem_name, outs, ins):
        c = self.cnt[sem_name]
        for b in ins:
            if b.readers.get(sem_name, 0) < c:
                b.readers[sem_name] = c
        for b in outs:
            b.writer = (sem_name, c)
            b.readers = {}

    def op(self, e, emit, outs=(), ins=(), inc=True):
        self._pre(e, outs, ins, True)
        if inc:
            self.cnt[e] += 1
            self.q[e].append(("i", emit, e, 1))
            self._post(e, outs, ins)
        else:
            self.q[e].append(("i", emit, None, 0))

    def group(self, e, emits, outs=(), ins_per=None, inc=True):
        all_ins = []
        last = len(emits) - 1
        for i, emit in enumerate(emits):
            ins = ins_per[i] if ins_per else ()
            all_ins += list(ins)
            self._pre(e, outs if i == 0 else (), ins, True)
            if i < last:
                self.q[e].append(("i", emit, None, 0))
        self.cnt[e] += 1
        self.q[e].append(("i", emits[last], e, 1))
        self._post(e, outs, all_ins)

    def dma(self, q, out_ap, in_ap, outs=(), ins=(), **kw):
        assert outs, "dma needs a destination buf"
        sem_name = "D" + outs[0].name
        self.sem(sem_name)
        self._pre(q, outs, ins, False)
        self.cnt[sem_name] += 16
        self.q[q].append(("i", lambda e, o=out_ap, i_=in_ap, kw=kw: e.dma_start(out=o, in_=i_, **kw),
                          sem_name, 16))
        self._post(sem_name, outs, ins)

    def dmat(self, q, out_ap, in_ap, outs=(), ins=()):
        assert outs, "dmat needs a destination buf"
        sem_name = "D" + outs[0].name
        self.sem(sem_name)
        self._pre(q, outs, ins, False)
        self.cnt[sem_name] += 16
        self.q[q].append(("i", lambda e, o=out_ap, i_=in_ap: e.dma_start_transpose(o, i_),
                          sem_name, 16))
        self._post(sem_name, outs, ins)

    def split_buf(self, real, slots):
        for s in slots:
            s.writer = real.writer
            s.readers = dict(real.readers)

    def merge_buf(self, real, slots):
        rd = {}
        for s in slots:
            for k, v in s.readers.items():
                rd[k] = max(rd.get(k, 0), v)
            if s.writer:
                k, v = s.writer
                rd[k] = max(rd.get(k, 0), v)
        real.readers = rd
        real.writer = None

    def barrier_engine(self, e):
        for o, c in self.cnt.items():
            if o != e and c > 0:
                self._wait(e, o, c)


def ts(i, w=128):
    return slice(i * w, (i + 1) * w)


class Prog:

    def __init__(self, nc, T, blocks, io_in, io_out):
        self.nc = nc
        self.T = T
        self.TC = T // 128
        self.TT = T // 512
        self.blocks = blocks
        self.io_in = io_in
        self.io_out = io_out
        self.kb = KB(nc, T)

    def declare(self):
        nc, T = self.nc, self.T
        kb = self.kb
        bl = self.blocks
        self.dram = {}

        def din(name, shape, dtype=F32):
            self.dram[name] = nc.dram_tensor(name, shape, dtype, kind="ExternalInput").ap()

        def dscr(name, shape, dtype=BF16):
            self.dram[name] = nc.dram_tensor(name, shape, dtype, kind="Internal").ap()
            self.kb.bufs[name] = Buf(name)

        if self.io_in == "tm":
            din("x_in", [T, D])
        else:
            din("x_in_fm", [KC * 128, T])
        if self.io_out == "tm":
            self.dram["y_out"] = nc.dram_tensor("y_out", [T, D], F32, kind="ExternalOutput").ap()
            self.kb.bufs["y_out"] = Buf("y_out")
        else:
            self.dram["y_out_fm"] = nc.dram_tensor("y_out_fm", [KC * 128, T], F32, kind="ExternalOutput").ap()
            self.kb.bufs["y_out_fm"] = Buf("y_out_fm")

        din("ident_f", [128, 128])
        din("ident_b", [128, 128], BF16)
        din("ones_f", [128, 128])
        din("ones_b", [128, 128], BF16)
        din("ut_f", [128, 128])
        din("R2", [128, 128], BF16)
        din("cos_t", [128, T])
        din("sin_t", [128, T])
        din("t_bc", [128, T])
        din("constc", [128, 8])
        din("sel8", [8, 1024], BF16)
        if "b0" in bl:
            din("w_qkv", [3 * D, D])
            din("bqc", [128, 24])
            din("w_out", [D, D])
            din("boc", [128, 8])
            dscr("wqkv_bf", [3 * D, D])
            dscr("wout_bf", [D, D])
        if "b1" in bl:
            din("wq1", [D, D]); din("wk1", [D, D]); din("wv1", [D, D]); din("wo1", [D, D])
            din("w_router", [D, 8])
            din("wg", [8, D, 2752]); din("wu", [8, D, 2752]); din("wd", [8, 2752, D])
            din("g1c", [128, 8]); din("g2c", [128, 8])
            dscr("wd_bf", [8, 2752, D])
            dscr("wgu_t", [704 * 128, 1024])

            dscr("wq1_bf", [D, D]); dscr("wk1_bf", [D, D])
            dscr("wv1_bf", [D, D]); dscr("wo1_bf", [D, D])
        if "b2" in bl:
            din("Wr", [D, D]); din("Wk", [D, D]); din("Wv", [D, D]); din("Wo", [D, D])
            din("decc", [128, 8]); din("lnwc", [128, 8])
            dscr("Wr_bf", [D, D]); dscr("Wk_bf", [D, D]); dscr("Wv_bf", [D, D]); dscr("Wo_bf", [D, D])
        if "b3" in bl:
            din("w1w", [4096, D]); din("w2w", [4096, D])
            din("h1w", [4096, 4096]); din("h2w", [4096, 4096])
            din("w3w", [D, 4096])
            din("s1c", [128, 32]); din("s2c", [128, 32])
            din("hs1c", [128, 32]); din("hs2c", [128, 32]); din("s3c", [128, 8])
            dscr("w1_bf", [4096, D]); dscr("w2_bf", [4096, D])
            dscr("h1_bf", [4096, 4096]); dscr("h2_bf", [4096, 4096])
            dscr("w3_bf", [D, 4096])
        for n in ("s1_d", "xn2_d", "hin_d", "s2_d", "s3_d"):
            dscr(n, [KC * 128, T], F32)

        S = kb.S
        self.xs = [S(f"xs{k}", [128, T], F32) for k in range(KC)]
        NPOOL = 12
        self.pool = [S(f"pl{i}", [128, T], BF16) for i in range(NPOOL)]
        self.scr = [S(f"scr{i}", [128, T], F32) for i in range(4)]
        self.stg = [S(f"stg{i}", [128, max(2 * T, 4096)], BF16) for i in range(3)]
        self.vs2 = [S(f"vs{i}", [128, 2080 * T // 2048], BF16) for i in range(2)]
        self.cid = {}
        for n in ("ident_f", "ones_f", "ut_f"):
            self.cid[n] = S(n + "_s", [128, 128], F32)
        for n in ("ident_b", "ones_b", "R2"):
            self.cid[n] = S(n + "_s", [128, 128], BF16)
        self.sel_s = S("sel_s", [8, 1024], BF16)
        self.vecs = S("vecs", [128, 360], F32)
        self.vecmap = {}
        self.vec_used = 0
        self.rows = {n: S("row_" + n, [1, T], F32)
                     for n in ("mean", "rstd")}
        self.rr = {n: kb.buf("row_" + n) for n in self.rows}
        self.pb = [kb.PS(f"pb{i}", [128, 512], F32) for i in range(6)]
        self.pt = [kb.PS(f"pt{i}", [128, 1024], BF16) for i in range(2)]
        self.pbb = [kb.buf(f"pb{i}") for i in range(6)]
        self.ptb = [kb.buf(f"pt{i}") for i in range(2)]
        for i in range(4):
            kb.bufs[f"prepslot{i}"] = Buf(f"prepslot{i}")
        for e in range(8):
            kb.bufs[f"wgu_t_e{e}"] = Buf(f"wgu_t_e{e}")
        for i in range(4):
            kb.bufs[f"wdslot{i}"] = Buf(f"wdslot{i}")
        self.Ssf = S("Ssf", [128, 65], F32)
        self.Ssb = S("Ssb", [128, 65], BF16)
        self.small = [S(f"sm{i}", [128, 16], F32) for i in range(4)]
        self.pa = S("pa", [128, 256], BF16)
        if "b1" in bl:
            self.rt_s = S("rt_s", [128, 64], BF16)
            self.lg8 = S("lg8", [8, 512], F32)
            self.cfm = S("cfm", [8, T], BF16)
            self.ctm = S("ctm", [128, 128], F32)

    def vslot(self, name, ncols):
        if name not in self.vecmap:
            self.vecmap[name] = (self.vec_used, ncols)
            self.vec_used += ncols
            assert self.vec_used <= 360
        off, n = self.vecmap[name]
        return self.vecs[:, off:off + ncols]

    def B(self, name):
        return self.kb.buf(name)

    def load_vec(self, q, name, src_ap, ncols):
        ap = self.vslot(name, ncols)
        self.kb.dma(q, ap, src_ap, outs=(self.B("vecs"),))
        return ap

    def vec(self, name, col=None):
        off, n = self.vecmap[name]
        if col is None:
            return self.vecs[:, off:off + n]
        return self.vecs[:, off + col:off + col + 1]

    def ln_stats(self, tag, eps, rms=False, src=None, src_bufs=None, sq_eng="g"):
        kb, TT, T = self.kb, self.TT, self.T
        src = src or self.xs
        src_bufs = src_bufs or [self.B(f"xs{k}") for k in range(KC)]
        xc = self.vs2[0]
        xcB = self.B("vs0")
        sq = self.scr[3].bitcast(BF16)
        sqB = self.B("scr3")
        onesb = self.cid["ones_b"]
        onesbB = self.B("ones_b_s")
        per = 2 if TT >= 2 else 1
        nh = TT // per if TT >= 2 else 1
        for h in range(nh):
            for k in range(KC):
                c0 = (h * per) * 512
                w = per * 512
                sl = slice(c0, c0 + w)
                lsl = slice(0, w)
                kb.op("v", lambda e, k=k, sl=sl, lsl=lsl: e.tensor_copy(xc[:, lsl], src[k][:, sl]),
                      outs=(xcB,), ins=(src_bufs[k],))
                kb.op(sq_eng, lambda e, lsl=lsl: e.tensor_tensor(sq[:, lsl], xc[:, lsl], xc[:, lsl], OP.mult),
                      outs=(sqB,), ins=(xcB,))
                for j in range(per):
                    tt = h * per + j
                    if not rms:
                        kb.op("t", lambda e, k=k, j=j: e.matmul(
                            self.pb[j][0:1, :], onesb[:, 0:1], xc[:, ts(j, 512)],
                            start=(k == 0), stop=(k == KC - 1)),
                            outs=(self.pbb[j],), ins=(xcB, onesbB))
                    kb.op("t", lambda e, k=k, j=j: e.matmul(
                        self.pb[2 + j][0:1, :], onesb[:, 0:1], sq[:, ts(j, 512)],
                        start=(k == 0), stop=(k == KC - 1)),
                        outs=(self.pbb[2 + j],), ins=(sqB, onesbB))
            for j in range(per):
                tt = h * per + j
                R = self.rows
                RR = self.rr
                epscol = 0 if eps == 1e-5 else 1
                epsap = self.vec("cst", epscol)[0:1, :]
                bsum, bsq = self.pb[j], self.pb[2 + j]
                bsumB, bsqB = self.pbb[j], self.pbb[2 + j]
                if rms:
                    kb.op("v", lambda e, bsq=bsq: e.tensor_scalar(
                        bsq[0:1, :], bsq[0:1, :], 1.0 / D, None, OP.mult),
                        outs=(bsqB,), ins=(bsqB,))
                    kb.op("s", lambda e, bsq=bsq: e.activation(
                        bsq[0:1, :], bsq[0:1, :], AF.Ln, bias=epsap, scale=1.0),
                        outs=(bsqB,), ins=(bsqB, self.B("vecs")))
                    kb.op("s", lambda e, tt=tt, bsq=bsq: e.activation(
                        R["rstd"][:, ts(tt, 512)], bsq[0:1, :], AF.Exp, scale=-0.5),
                        outs=(RR["rstd"],), ins=(bsqB,))
                else:
                    kb.op("v", lambda e, tt=tt, bsum=bsum: e.tensor_scalar(
                        R["mean"][:, ts(tt, 512)], bsum[0:1, :], 1.0 / D, None, OP.mult),
                        outs=(RR["mean"],), ins=(bsumB,))
                    kb.op("v", lambda e, bsq=bsq: e.tensor_scalar(
                        bsq[0:1, :], bsq[0:1, :], 1.0 / D, None, OP.mult),
                        outs=(bsqB,), ins=(bsqB,))
                    m2 = self.scr[2]
                    m2B = self.B("scr2")
                    kb.op("v", lambda e, tt=tt, m2=m2: e.tensor_tensor(
                        m2[0:1, 0:512], R["mean"][:, ts(tt, 512)], R["mean"][:, ts(tt, 512)], OP.mult),
                        outs=(m2B,), ins=(RR["mean"],))
                    kb.op("v", lambda e, m2=m2, bsq=bsq: e.tensor_tensor(
                        bsq[0:1, :], bsq[0:1, :], m2[0:1, 0:512], OP.subtract),
                        outs=(bsqB,), ins=(bsqB, m2B))
                    kb.op("s", lambda e, bsq=bsq: e.activation(
                        bsq[0:1, :], bsq[0:1, :], AF.Ln, bias=epsap, scale=1.0),
                        outs=(bsqB,), ins=(bsqB, self.B("vecs")))
                    kb.op("s", lambda e, tt=tt, bsq=bsq: e.activation(
                        R["rstd"][:, ts(tt, 512)], bsq[0:1, :], AF.Exp, scale=-0.5),
                        outs=(RR["rstd"],), ins=(bsqB,))

    def bc_rows(self, tt, rows=("mean", "rstd"), banks=(4, 5)):
        kb = self.kb
        onesf = self.cid["ones_f"]
        onesfB = self.B("ones_f_s")
        out = []
        for rn, bi in zip(rows, banks):
            kb.op("t", lambda e, rn=rn, bi=bi: e.matmul(
                self.pb[bi][:, :], onesf[0:1, :], self.rows[rn][:, ts(tt, 512)],
                start=True, stop=True),
                outs=(self.pbb[bi],), ins=(self.rr[rn], onesfB))
            out.append(self.pb[bi])
        return out

    def emit_input(self):
        kb, T, TC = self.kb, self.T, self.TC
        d = self.dram
        for n in ("ident_f", "ones_f", "ut_f", "ident_b", "ones_b", "R2"):
            kb.dma("q", self.cid[n][:], d[n], outs=(self.B(n + "_s"),))
        kb.dma("q", self.sel_s[:], d["sel8"], outs=(self.B("sel_s"),))
        self.load_vec("q", "cst", d["constc"], 8)
        if self.io_in == "fm":
            for k in range(KC):
                kb.dma("q", self.xs[k][:], d["x_in_fm"][ts(k), :], outs=(self.B(f"xs{k}"),))
            return
        identf = self.cid["ident_f"]
        idB = self.B("ident_f_s")
        for tc in range(TC):
            stgv = self.stg[tc % 2].bitcast(F32)
            stgB = self.B(f"stg{tc % 2}")
            kb.dma("q", stgv[:, 0:D], d["x_in"][ts(tc), :], outs=(stgB,))
            for k in range(KC):
                b = 4 + (k % 2)
                kb.op("t", lambda e, k=k, b=b, stgv=stgv: e.transpose(
                    self.pb[b][:, 0:128], stgv[:, ts(k)], identf[:]),
                    outs=(self.pbb[b],), ins=(stgB, idB))
                kb.op("v", lambda e, k=k, tc=tc, b=b: e.tensor_copy(
                    self.xs[k][:, ts(tc)], self.pb[b][:, 0:128]),
                    outs=(self.B(f"xs{k}"),), ins=(self.pbb[b],))

    def emit_output(self):
        kb, T, TC = self.kb, self.T, self.TC
        d = self.dram
        if self.io_out == "fm":
            for k in range(KC):
                kb.dma("s", d["y_out_fm"][ts(k), :], self.xs[k][:],
                       outs=(self.B("y_out_fm"),), ins=(self.B(f"xs{k}"),))
            return
        identf = self.cid["ident_f"]
        idB = self.B("ident_f_s")
        for tc in range(TC):
            stgv = self.stg[tc % 2].bitcast(F32)
            stgB = self.B(f"stg{tc % 2}")
            for k in range(KC):
                b = 4 + (k % 2)
                kb.op("t", lambda e, k=k, tc=tc, b=b: e.transpose(
                    self.pb[b][:, 0:128], self.xs[k][:, ts(tc)], identf[:]),
                    outs=(self.pbb[b],), ins=(self.B(f"xs{k}"), idB))
                kb.op("v", lambda e, k=k, b=b, stgv=stgv: e.tensor_copy(
                    stgv[:, ts(k)], self.pb[b][:, 0:128]),
                    outs=(stgB,), ins=(self.pbb[b],))
            kb.dma("s", d["y_out"][ts(tc), :], stgv[:, 0:D],
                   outs=(self.B("y_out"),), ins=(stgB,))

    def emit_b0(self):
        kb, T, TC, TT = self.kb, self.T, self.TC, self.TT
        d = self.dram
        for i in range(3):
            kb.dma("g", d["wqkv_bf"][ts(i, D), :], d["w_qkv"][ts(i, D), :],
                   outs=(self.B("wqkv_bf"),))
        kb.dma("g", d["wout_bf"], d["w_out"], outs=(self.B("wout_bf"),))
        self.load_vec("q", "bqc", d["bqc"], 24)
        self.load_vec("q", "boc", d["boc"], 8)

        self.ln_stats("ln1", 1e-5)
        xn1 = self.pool[0:8]
        xn1B = [self.B(f"pl{i}") for i in range(8)]
        for tt in range(TT):
            self.bc_rows(tt)
            mb = self.scr[0]; mbB = self.B("scr0")
            rb = self.scr[1]; rbB = self.B("scr1")
            kb.op("v", lambda e, tt=tt: e.tensor_copy(mb[:, 0:512], self.pb[4][:, :]),
                  outs=(mbB,), ins=(self.pbb[4],))
            kb.op("v", lambda e, tt=tt: e.tensor_copy(rb[:, 0:512], self.pb[5][:, :]),
                  outs=(rbB,), ins=(self.pbb[5],))
            for k in range(KC):
                tmp = self.scr[2]
                tmpB = self.B("scr2")
                kb.op("g", lambda e, k=k, tt=tt: e.tensor_tensor(
                    tmp[:, ts(tt, 512)], self.xs[k][:, ts(tt, 512)], mb[:, 0:512], OP.subtract),
                    outs=(tmpB,), ins=(self.B(f"xs{k}"), mbB))
                kb.op("v", lambda e, k=k, tt=tt: e.tensor_tensor(
                    xn1[k][:, ts(tt, 512)], tmp[:, ts(tt, 512)], rb[:, 0:512], OP.mult),
                    outs=(xn1B[k],), ins=(tmpB, rbB))

        qp, kp, ktm, vfm = self.pool[8], self.pool[9], self.pool[10], self.pool[11]
        qpB, kpB, ktmB, vfmB = (self.B("pl8"), self.B("pl9"), self.B("pl10"), self.B("pl11"))
        atm = self.stg[2]
        atmB = self.B("stg2")
        identb = self.cid["ident_b"]
        idbB = self.B("ident_b_s")
        utf = self.cid["ut_f"]
        utfB = self.B("ut_f_s")
        SsfB, SsbB = self.B("Ssf"), self.B("Ssb")
        paB = self.B("pa")
        vsB = [self.B(f"vs{i}") for i in range(2)]

        for hp in range(8):
            stgw = self.stg[hp % 2]
            stgwB = self.B(f"stg{hp % 2}")
            for part in range(3):
                base = part * D + hp * 128
                for kc2 in range(KC):
                    col = part * D + kc2 * 128
                    kb.dmat("q", stgw[:, col:col + 128],
                            d["wqkv_bf"][base:base + 128, ts(kc2)],
                            outs=(stgwB,), ins=(self.B("wqkv_bf"),))
            for part, dst, dstB in ((0, qp, qpB), (1, kp, kpB), (2, vfm, vfmB)):
                bias = self.vec("bqc", part * 8 + hp)
                for tt in range(TT):
                    bank = tt % 4
                    emits = []
                    ins_per = []
                    for kc2 in range(KC):
                        emits.append(lambda e, kc2=kc2, tt=tt, bank=bank, part=part, stgw=stgw: e.matmul(
                            self.pb[bank][:, :],
                            stgw[:, slice(part * D + kc2 * 128, part * D + kc2 * 128 + 128)],
                            xn1[kc2][:, ts(tt, 512)],
                            start=(kc2 == 0), stop=(kc2 == KC - 1)))
                        ins_per.append((stgwB, xn1B[kc2]))
                    kb.group("t", emits, outs=(self.pbb[bank],), ins_per=ins_per)
                    if part < 2:
                        sc12 = self.stg[2].bitcast(F32)
                        sc12B = self.B("stg2")
                        co = self.T // 2
                        ee = sc12[:, co:co + 512]
                        rr2 = sc12[:, co + 512:co + 1024]
                        kb.op("s", lambda e, bank=bank, bias=bias, ee=ee: e.activation(
                            ee, self.pb[bank][:, :], AF.Exp, bias=bias, scale=1.0),
                            outs=(sc12B,), ins=(self.pbb[bank], self.B("vecs")))
                        kb.op("s", lambda e, bank=bank, bias=bias, rr2=rr2: e.activation(
                            rr2, self.pb[bank][:, :], AF.Relu, bias=bias, scale=1.0),
                            outs=(sc12B,), ins=(self.pbb[bank], self.B("vecs")))
                        kb.op("v", lambda e, tt=tt, dst=dst, ee=ee, rr2=rr2: e.scalar_tensor_tensor(
                            dst[:, ts(tt, 512)], ee, 1.0, rr2,
                            OP.min, OP.add),
                            outs=(dstB,), ins=(sc12B,))
                    else:
                        kb.op("v", lambda e, tt=tt, bank=bank, dst=dst, bias=bias: e.tensor_scalar(
                            dst[:, ts(tt, 512)], self.pb[bank][:, :], bias, None, OP.add),
                            outs=(dstB,), ins=(self.pbb[bank], self.B("vecs")))
            vs = self.vs2[hp % 2]
            vsb = vsB[hp % 2]
            kb.op("v", lambda e, vs=vs: e.memset(vs[:], 1.0), outs=(vsb,))
            for tc in range(TC):
                pt = self.pt[tc % 2]
                ptB = self.ptb[tc % 2]
                kb.op("t", lambda e, tc=tc, pt=pt: e.transpose(
                    pt[:, 0:128], kp[:, ts(tc)], identb[:]),
                    outs=(ptB,), ins=(kpB, idbB))
                kb.op("v", lambda e, tc=tc, pt=pt: e.tensor_copy(ktm[:, ts(tc)], pt[:, 0:128]),
                      outs=(ktmB,), ins=(ptB,))
                kb.op("t", lambda e, tc=tc, pt=pt: e.transpose(
                    pt[:, 128:256], vfm[:, ts(tc)], identb[:]),
                    outs=(ptB,), ins=(vfmB, idbB))
                kb.op("v", lambda e, tc=tc, pt=pt, vs=vs: e.tensor_copy(
                    vs[:, 130 * tc:130 * tc + 64], pt[:, 128:192]),
                    outs=(vsb,), ins=(ptB,))
                kb.op("v", lambda e, tc=tc, pt=pt, vs=vs: e.tensor_copy(
                    vs[:, 130 * tc + 65:130 * tc + 129], pt[:, 192:256]),
                    outs=(vsb,), ins=(ptB,))
            kb.op("v", lambda e: e.memset(self.Ssf[:], 0.0), outs=(SsfB,))
            kb.op("v", lambda e: e.memset(self.Ssb[:], 0.0), outs=(SsbB,))
            for tc in range(TC):
                for hl in range(2):
                    hs = slice(64 * hl, 64 * hl + 64)
                    bA = 0 if hl == 0 else 1
                    bO = 2 if hl == 0 else 3
                    kb.op("t", lambda e, tc=tc, hs=hs, bA=bA: e.matmul(
                        self.pb[bA][:, 0:128], kp[hs, ts(tc)], qp[hs, ts(tc)],
                        start=True, stop=True),
                        outs=(self.pbb[bA],), ins=(kpB, qpB))
                    pasl = slice(128 * hl, 128 * hl + 128)
                    kb.op("v", lambda e, bA=bA, pasl=pasl: e.tensor_tensor(
                        self.pa[:, pasl], self.pb[bA][:, 0:128], utf[:], OP.mult),
                        outs=(paB,), ins=(self.pbb[bA], utfB))
                    e1 = lambda e, tc=tc, hl=hl, bO=bO, pasl=pasl, vs=vs: e.matmul(
                        self.pb[bO][:, 0:65], self.pa[:, pasl],
                        vs[:, 130 * tc + 65 * hl: 130 * tc + 65 * hl + 65],
                        start=True, stop=False)
                    e2 = lambda e, tc=tc, hs=hs, bO=bO: e.matmul(
                        self.pb[bO][:, 0:65], qp[hs, ts(tc)], self.Ssb[hs, :],
                        start=False, stop=True)
                    kb.group("t", [e1, e2], outs=(self.pbb[bO],),
                             ins_per=[(paB, vsb), (qpB, SsbB)])
                    bS = 4 + hl
                    kb.op("t", lambda e, tc=tc, hl=hl, hs=hs, bS=bS, vs=vs: e.matmul(
                        self.pb[bS][hs, 0:65],
                        ktm[:, 128 * tc + 64 * hl:128 * tc + 64 * hl + 64],
                        vs[:, 130 * tc + 65 * hl: 130 * tc + 65 * hl + 65],
                        start=True, stop=True),
                        outs=(self.pbb[bS],), ins=(ktmB, vsb))
                    kb.op("v", lambda e, hs=hs, bS=bS: e.tensor_tensor(
                        self.Ssf[hs, :], self.Ssf[hs, :], self.pb[bS][hs, 0:65], OP.add),
                        outs=(SsfB,), ins=(SsfB, self.pbb[bS]))
                    kb.op("v", lambda e, hs=hs: e.tensor_copy(self.Ssb[hs, :], self.Ssf[hs, :]),
                          outs=(SsbB,), ins=(SsfB,))
                    nrm = self.small[0]
                    nrmB = self.B("sm0")
                    kb.op("v", lambda e, bO=bO: e.tensor_scalar(
                        nrm[:, 0:1], self.pb[bO][:, 64:65], 1e-6, None, OP.add),
                        outs=(nrmB,), ins=(self.pbb[bO],))
                    kb.op("v", lambda e: e.reciprocal(nrm[:, 1:2], nrm[:, 0:1]),
                          outs=(nrmB,), ins=(nrmB,))
                    kb.op("v", lambda e, tc=tc, hl=hl, bO=bO: e.tensor_scalar(
                        atm[:, 128 * tc + 64 * hl:128 * tc + 64 * hl + 64],
                        self.pb[bO][:, 0:64], nrm[:, 1:2], None, OP.mult),
                        outs=(atmB,), ins=(self.pbb[bO], nrmB))
            for tc in range(TC):
                pt = self.pt[tc % 2]
                ptB = self.ptb[tc % 2]
                kb.op("t", lambda e, tc=tc, pt=pt: e.transpose(
                    pt[:, 0:128], atm[:, ts(tc)], identb[:]),
                    outs=(ptB,), ins=(atmB, idbB))
                sc = self.scr[hp // 2].bitcast(BF16)
                scB = self.B(f"scr{hp // 2}")
                kb.op("v", lambda e, tc=tc, pt=pt, sc=sc, hp=hp: e.tensor_copy(
                    sc[:, (hp % 2) * T + 128 * tc:(hp % 2) * T + 128 * tc + 128], pt[:, 0:128]),
                    outs=(scB,), ins=(ptB,))

        for kc2 in range(KC):
            dst = self.stg[kc2 // 4]
            dstB = self.B(f"stg{kc2 // 4}")
            kb.dmat("q", dst[:, (kc2 % 4) * D:(kc2 % 4) * D + D],
                    d["wout_bf"][:, ts(kc2)], outs=(dstB,), ins=(self.B("wout_bf"),))
        for oc in range(KC):
            for tt in range(TT):
                bank = (oc * TT + tt) % 4
                emits, ins_per = [], []
                for kc2 in range(KC):
                    wv = self.stg[kc2 // 4]
                    wvB = self.B(f"stg{kc2 // 4}")
                    afm = self.scr[kc2 // 2].bitcast(BF16)
                    afmB = self.B(f"scr{kc2 // 2}")
                    emits.append(lambda e, kc2=kc2, oc=oc, tt=tt, bank=bank, wv=wv, afm=afm: e.matmul(
                        self.pb[bank][:, :],
                        wv[:, (kc2 % 4) * D + oc * 128:(kc2 % 4) * D + oc * 128 + 128],
                        afm[:, (kc2 % 2) * T + tt * 512:(kc2 % 2) * T + tt * 512 + 512],
                        start=(kc2 == 0), stop=(kc2 == KC - 1)))
                    ins_per.append((wvB, afmB))
                kb.group("t", emits, outs=(self.pbb[bank],), ins_per=ins_per)
                kb.op("v", lambda e, oc=oc, tt=tt, bank=bank: e.scalar_tensor_tensor(
                    self.xs[oc][:, ts(tt, 512)], self.pb[bank][:, :],
                    self.vec("boc", oc), self.xs[oc][:, ts(tt, 512)], OP.add, OP.add),
                    outs=(self.B(f"xs{oc}"),),
                    ins=(self.pbb[bank], self.B("vecs"), self.B(f"xs{oc}")))


    def prep_wgu_init(self):
        kb, T = self.kb, self.T
        d = self.dram
        wg_r = d["wg"].rearrange("x (c p) h -> p (x c) h", c=8)
        wu_r = d["wu"].rearrange("x (c p) h -> p (x c) h", c=8)
        wgt = d["wgu_t"]
        pslots = []
        for i in range(4):
            t = self.pool[9 + i // 2]
            ap = t[:, (i % 2) * 1024:(i % 2) * 1024 + 1024]
            pslots.append((ap, kb.buf(f"prepslot{i}"), self.B(f"pl{9 + i // 2}")))

        def gen():
            si = 0
            for e8 in range(8):
                for hz in range(22):
                    w = 128 if hz < 21 else 64
                    for gi, wr in ((0, wg_r), (1, wu_r)):
                        ap, sb, rb = pslots[si % 4]
                        si += 1
                        kb.dma("g", ap[:, 0:8 * w].rearrange("p (c h) -> p c h", c=8),
                               wr[:, 8 * e8:8 * e8 + 8, 128 * hz:128 * hz + w],
                               outs=(sb, rb), ins=())
                        row = ((e8 * 22 + hz) * 2 + gi) * 128
                        kb.dma("s", wgt[row:row + 128, 0:8 * w], ap[:, 0:8 * w],
                               outs=(kb.buf(f"wgu_t_e{e8}"),), ins=(sb, rb))
                        yield
        self._prep_gen = gen()

    def prep_pump(self, n):
        g = getattr(self, "_prep_gen", None)
        if g is None:
            return
        for _ in range(n):
            try:
                next(g)
            except StopIteration:
                self._prep_gen = None
                return

    def rms_rows_from_banks(self):
        kb = self.kb
        epsap = self.vec("cst", 1)[0:1, :]
        for tt in range(self.TT):
            b = self.pb[2 + tt]
            bB = self.pbb[2 + tt]
            kb.op("v", lambda e, b=b: e.tensor_scalar(b[0:1, :], b[0:1, :], 1.0 / D, None, OP.mult),
                  outs=(bB,), ins=(bB,))
            kb.op("s", lambda e, b=b: e.activation(b[0:1, :], b[0:1, :], AF.Ln, bias=epsap, scale=1.0),
                  outs=(bB,), ins=(bB, self.B("vecs")))
            kb.op("s", lambda e, tt=tt, b=b: e.activation(
                self.rows["rstd"][:, ts(tt, 512)], b[0:1, :], AF.Exp, scale=-0.5),
                outs=(self.rr["rstd"],), ins=(bB,))

    def emit_router(self, b2, poolB):
        kb, T, TC, TT = self.kb, self.T, self.TC, self.TT
        d = self.dram
        PB, PBB = self.pb, self.pbb
        smB = [self.B(f"sm{i}") for i in range(4)]
        rtB, lgB, ctmB, cfmB = self.B("rt_s"), self.B("lg8"), self.B("ctm"), self.B("cfm")
        sm0, sm1, sm2, sm3 = self.small
        identf = self.cid["ident_f"]
        idB = self.B("ident_f_s")
        for kc2 in range(KC):
            kb.dma("g", self.rt_s[:, 8 * kc2:8 * kc2 + 8], d["w_router"][ts(kc2), :], outs=(rtB,))
        for tt in range(TT):
            emits, ins_per = [], []
            for kc2 in range(KC):
                emits.append(lambda e, kc2=kc2, tt=tt: e.matmul(
                    PB[0][0:8, :], self.rt_s[:, 8 * kc2:8 * kc2 + 8], b2[kc2][:, ts(tt, 512)],
                    start=(kc2 == 0), stop=(kc2 == KC - 1)))
                ins_per.append((rtB, poolB[kc2]))
            kb.group("t", emits, outs=(PBB[0],), ins_per=ins_per)
            kb.op("v", lambda e: e.tensor_copy(self.lg8[:, :], PB[0][0:8, :]),
                  outs=(lgB,), ins=(PBB[0],))
            for tcl in range(4 if TT > 0 else 0):
                tc = tt * 4 + tcl
                if tc >= TC:
                    break
                kb.op("t", lambda e, tcl=tcl: e.transpose(
                    PB[1][:, 0:8], self.lg8[:, ts(tcl)], identf[0:8, 0:8]),
                    outs=(PBB[1],), ins=(lgB, idB))
                kb.op("v", lambda e: e.reduce_max(sm0[:, 0:1], PB[1][:, 0:8], axis=AX, negate=True),
                      outs=(smB[0],), ins=(PBB[1],))
                kb.op("s", lambda e: e.activation(sm1[:, 0:8], PB[1][:, 0:8], AF.Exp,
                                                  bias=sm0[:, 0:1], scale=1.0),
                      outs=(smB[1],), ins=(PBB[1], smB[0]))
                kb.op("v", lambda e: e.reduce_sum(sm0[:, 1:2], sm1[:, 0:8], axis=AX),
                      outs=(smB[0],), ins=(smB[1],))
                kb.op("v", lambda e: e.reciprocal(sm0[:, 2:3], sm0[:, 1:2]),
                      outs=(smB[0],), ins=(smB[0],))
                kb.op("v", lambda e: e.tensor_scalar(sm1[:, 8:16], sm1[:, 0:8], sm0[:, 2:3], None, OP.mult),
                      outs=(smB[1],), ins=(smB[1], smB[0]))
                kb.op("v", lambda e: e.reduce_max(sm0[:, 3:4], sm1[:, 8:16], axis=AX),
                      outs=(smB[0],), ins=(smB[1],))
                kb.op("v", lambda e: e.tensor_scalar(sm2[:, 0:8], sm1[:, 8:16], sm0[:, 3:4], None, OP.is_ge),
                      outs=(smB[2],), ins=(smB[1], smB[0]))
                kb.op("v", lambda e: e.scalar_tensor_tensor(sm2[:, 8:16], sm2[:, 0:8], -1e30,
                                                            sm1[:, 8:16], OP.mult, OP.add),
                      outs=(smB[2],), ins=(smB[2], smB[1]))
                kb.op("v", lambda e: e.reduce_max(sm0[:, 4:5], sm2[:, 8:16], axis=AX),
                      outs=(smB[0],), ins=(smB[2],))
                kb.op("v", lambda e: e.tensor_scalar(sm2[:, 0:8], sm1[:, 8:16], sm0[:, 4:5], None, OP.is_ge),
                      outs=(smB[2],), ins=(smB[1], smB[0]))
                kb.op("v", lambda e: e.tensor_tensor(sm0[:, 5:6], sm0[:, 3:4], sm0[:, 4:5], OP.add),
                      outs=(smB[0],), ins=(smB[0],))
                kb.op("v", lambda e: e.reciprocal(sm0[:, 6:7], sm0[:, 5:6]),
                      outs=(smB[0],), ins=(smB[0],))
                kb.op("v", lambda e: e.tensor_tensor(sm3[:, 0:8], sm1[:, 8:16], sm2[:, 0:8], OP.mult),
                      outs=(smB[3],), ins=(smB[1], smB[2]))
                kb.op("v", lambda e, tc=tc: e.tensor_scalar(
                    self.ctm[:, 8 * tc:8 * tc + 8], sm3[:, 0:8], sm0[:, 6:7], None, OP.mult),
                    outs=(ctmB,), ins=(smB[3], smB[0]))
        for tc in range(TC):
            kb.op("t", lambda e, tc=tc: e.transpose(
                PB[1][0:8, 0:128], self.ctm[:, 8 * tc:8 * tc + 8], identf[:]),
                outs=(PBB[1],), ins=(ctmB, idB))
            kb.op("v", lambda e, tc=tc: e.tensor_copy(
                self.cfm[0:8, ts(tc)], PB[1][0:8, 0:128]),
                outs=(cfmB,), ins=(PBB[1],))

    def emit_moe(self, b2, poolB):
        kb, T, TC, TT = self.kb, self.T, self.TC, self.TT
        assert T == 2048, "MoE gu layout assumes T=2048"
        d = self.dram
        PB, PBB = self.pb, self.pbb
        xsB = [self.B(f"xs{k}") for k in range(KC)]
        scrB = [self.B(f"scr{i}") for i in range(4)]
        stgB = [self.B(f"stg{i}") for i in range(3)]
        vsB = [self.B("vs0"), self.B("vs1")]
        cfmB = self.B("cfm")
        selB = self.B("sel_s")
        wd_r = d["wd_bf"].rearrange("x h o -> (x h) o")
        wgt = d["wgu_t"]
        wgtB = self.B("wgu_t")
        NHZ = 22

        def hzw(hz):
            return 128 if hz < 21 else 64

        def gu_tile(t):
            if t < 16:
                xb = self.xs[t // 2].bitcast(BF16)
                c0 = T + (t % 2) * 1024
                return xb[:, c0:c0 + 1024], xsB[t // 2]
            u = t - 16
            return self.pool[8 + u // 2][:, (u % 2) * 1024:(u % 2) * 1024 + 1024], self.B(f"pl{8 + u // 2}")

        for half in range(2):
            h0 = half * 1024
            for e8 in range(8):
                for tt in range(2):
                    kb.op("t", lambda e, e8=e8, tt=tt, h0=h0: e.matmul(
                        PB[0][:, :], self.sel_s[0:8, ts(e8)], self.cfm[0:8, h0 + tt * 512:h0 + tt * 512 + 512],
                        start=True, stop=True),
                        outs=(PBB[0],), ins=(selB, cfmB))
                    kb.op("v", lambda e, tt=tt: e.tensor_copy(
                        self.vs2[1][:, ts(tt, 512)], PB[0][:, :]),
                        outs=(vsB[1],), ins=(PBB[0],))
                for hz in range(NHZ):
                    w = hzw(hz)
                    stg = self.stg[hz % 2]
                    sB = stgB[hz % 2]
                    row = (e8 * 22 + hz) * 2 * 128
                    eb = kb.buf(f"wgu_t_e{e8}")
                    kb.dma("q", stg[:, 0:8 * w], wgt[row:row + 128, 0:8 * w],
                           outs=(sB,), ins=(eb,))
                    kb.dma("s", stg[:, 1024:1024 + 8 * w], wgt[row + 128:row + 256, 0:8 * w],
                           outs=(sB,), ins=(eb,))
                    gt, gtB = gu_tile(hz)
                    for tt in range(2):
                        bk = 0 if (hz * 2 + tt) % 2 == 0 else 2
                        for which, bank, cb in ((0, bk, 0), (1, bk + 1, 1024)):
                            emits, ins_per = [], []
                            for kc2 in range(KC):
                                emits.append(lambda e, kc2=kc2, tt=tt, bank=bank, cb=cb, stg=stg, w=w, h0=h0: e.matmul(
                                    PB[bank][0:w, :], stg[:, cb + w * kc2:cb + w * kc2 + w],
                                    b2[kc2][:, h0 + tt * 512:h0 + tt * 512 + 512],
                                    start=(kc2 == 0), stop=(kc2 == KC - 1)))
                                ins_per.append((sB, poolB[kc2]))
                            kb.group("t", emits, outs=(PBB[bank],), ins_per=ins_per)
                        sg = self.vs2[0]
                        sgsl = slice(0, 512) if bk == 0 else slice(2080 - 512, 2080)
                        kb.op("s", lambda e, sg=sg, w=w, bk=bk, sgsl=sgsl: e.activation(
                            sg[0:w, sgsl], PB[bk][0:w, :], AF.Silu),
                            outs=(vsB[0],), ins=(PBB[bk],))
                        kb.op("v", lambda e, tt=tt, gt=gt, sg=sg, w=w, bk=bk, sgsl=sgsl: e.tensor_tensor(
                            gt[0:w, ts(tt, 512)], sg[0:w, sgsl], PB[bk + 1][0:w, :], OP.mult),
                            outs=(gtB,), ins=(vsB[0], PBB[bk + 1]))
                first_e = (e8 == 0)
                wdslots = [kb.buf(f"wdslot{i}") for i in range(4)]
                kb.split_buf(stgB[2], wdslots)
                for ocg in range(2):
                    for tt in range(2):
                        for hz in range(NHZ):
                            w = hzw(hz)
                            wdt = self.stg[2]
                            wdtB = wdslots[ocg * 2 + hz % 2]
                            po = 512 * (hz % 2)
                            kb.dma("q" if hz % 2 == 0 else "s",
                                   wdt[0:w, po + 1024 * ocg:po + 1024 * ocg + 512],
                                   wd_r[2752 * e8 + 128 * hz:2752 * e8 + 128 * hz + w,
                                        512 * ocg:512 * ocg + 512],
                                   outs=(wdtB,), ins=(self.B("wd_bf"),))
                            gt, gtB = gu_tile(hz)
                            for oci in range(4):
                                kb.op("t", lambda e, hz=hz, oci=oci, tt=tt, po=po, ocg=ocg, gt=gt, wdt=wdt, w=w: e.matmul(
                                    PB[2 + oci][:, :],
                                    wdt[0:w, po + 1024 * ocg + 128 * oci:po + 1024 * ocg + 128 * oci + 128],
                                    gt[0:w, ts(tt, 512)],
                                    start=(hz == 0), stop=(hz == NHZ - 1)),
                                    outs=(PBB[2 + oci],), ins=(wdtB, gtB))
                        t1 = self.vs2[0].bitcast(F32)
                        for oci in range(4):
                            oc = ocg * 4 + oci
                            ysl = self.scr[oc // 2][:, (oc % 2) * 1024 + 512 * tt:(oc % 2) * 1024 + 512 * tt + 512]
                            kb.op("v", lambda e, oci=oci, tt=tt, t1=t1: e.tensor_tensor(
                                t1[:, 256:768], PB[2 + oci][:, :], self.vs2[1][:, ts(tt, 512)], OP.mult),
                                outs=(vsB[0],), ins=(PBB[2 + oci], vsB[1]))
                            if first_e:
                                kb.op("v", lambda e, ysl=ysl, t1=t1: e.tensor_copy(ysl, t1[:, 256:768]),
                                      outs=(scrB[oc // 2],), ins=(vsB[0],))
                            else:
                                kb.op("v", lambda e, ysl=ysl, t1=t1: e.tensor_tensor(ysl, ysl, t1[:, 256:768], OP.add),
                                      outs=(scrB[oc // 2],),
                                      ins=(scrB[oc // 2], vsB[0]))
                kb.merge_buf(stgB[2], wdslots)
            p8 = self.pool[8].bitcast(F32)
            p9 = self.pool[9].bitcast(F32)
            p10 = self.pool[10].bitcast(F32)
            for k in range(KC):
                kb.dma("q", p8[:, 0:1024], d["s1_d"][ts(k), h0:h0 + 1024],
                       outs=(self.B("pl8"),), ins=(self.B("s1_d"),))
                kb.dma("q", p9[:, 0:1024], d["hin_d"][ts(k), h0:h0 + 1024],
                       outs=(self.B("pl9"),), ins=(self.B("hin_d"),))
                kb.op("v", lambda e: e.tensor_tensor(p10[:, 0:1024], p8[:, 0:1024], p9[:, 0:1024], OP.add),
                      outs=(self.B("pl10"),), ins=(self.B("pl8"), self.B("pl9")))
                ysl = self.scr[k // 2][:, (k % 2) * 1024:(k % 2) * 1024 + 1024]
                kb.op("v", lambda e, k=k, ysl=ysl, h0=h0: e.tensor_tensor(
                    self.xs[k][:, h0:h0 + 1024], p10[:, 0:1024], ysl, OP.add),
                    outs=(xsB[k],), ins=(self.B("pl10"), scrB[k // 2]))

    def emit_b1(self):
        kb, T, TC, TT = self.kb, self.T, self.TC, self.TT
        d = self.dram
        for wn in ("wq1", "wk1", "wv1", "wo1"):
            kb.dma("g", d[wn + "_bf"], d[wn], outs=(self.B(wn + "_bf"),))
        kb.dma("g", d["wd_bf"].rearrange("x h o -> (x h) o"),
               d["wd"].rearrange("x h o -> (x h) o"), outs=(self.B("wd_bf"),))
        self.prep_wgu_init()
        self.load_vec("q", "g1c", d["g1c"], 8)
        self.load_vec("q", "g2c", d["g2c"], 8)
        xsB = [self.B(f"xs{k}") for k in range(KC)]
        scrB = [self.B(f"scr{i}") for i in range(4)]
        stgB = [self.B(f"stg{i}") for i in range(3)]
        poolB = [self.B(f"pl{i}") for i in range(12)]
        vsB = [self.B("vs0"), self.B("vs1")]
        onesfB = self.B("ones_f_s")
        onesbB = self.B("ones_b_s")
        smB = [self.B(f"sm{i}") for i in range(4)]

        self.ln_stats("ln2", 1e-5, sq_eng="v")
        for k in range(KC):
            kb.dma("s", d["s1_d"][ts(k), :], self.xs[k][:], outs=(self.B("s1_d"),), ins=(xsB[k],))
        b1 = self.pool[0:8]
        for tt in range(TT):
            self.bc_rows(tt)
            mb, rb = self.scr[0], self.scr[1]
            kb.op("v", lambda e: e.tensor_copy(mb[:, 0:512], self.pb[4][:, :]),
                  outs=(scrB[0],), ins=(self.pbb[4],))
            kb.op("v", lambda e: e.tensor_copy(rb[:, 0:512], self.pb[5][:, :]),
                  outs=(scrB[1],), ins=(self.pbb[5],))
            for k in range(KC):
                tmp = self.scr[2]
                kb.op("v", lambda e, k=k, tt=tt: e.tensor_tensor(
                    tmp[:, ts(tt, 512)], self.xs[k][:, ts(tt, 512)], mb[:, 0:512], OP.subtract),
                    outs=(scrB[2],), ins=(xsB[k], scrB[0]))
                kb.op("v", lambda e, k=k, tt=tt: e.tensor_tensor(
                    self.xs[k][:, ts(tt, 512)], tmp[:, ts(tt, 512)], rb[:, 0:512], OP.mult),
                    outs=(xsB[k],), ins=(scrB[2], scrB[1]))
        for k in range(KC):
            kb.dma("s", d["xn2_d"][ts(k), :], self.xs[k][:], outs=(self.B("xn2_d"),), ins=(xsB[k],))
            kb.op("v", lambda e, k=k: e.tensor_scalar(
                b1[k][:], self.xs[k][:], self.vec("g1c", k), None, OP.mult),
                outs=(poolB[k],), ins=(xsB[k], self.B("vecs")))
        self.ln_stats("rms1", 1e-6, rms=True, src=self.xs, src_bufs=xsB, sq_eng="v")

        qx = [self.xs[h].bitcast(BF16) for h in range(KC)]
        PB, PBB = self.pb, self.pbb

        def load_strips(wname):
            for kc2 in range(KC):
                dst = self.stg[kc2 // 4]
                kb.dma("q", dst[:, (kc2 % 4) * D:(kc2 % 4) * D + D], d[wname + "_bf"][ts(kc2), :],
                       outs=(stgB[kc2 // 4],), ins=(self.B(wname + "_bf"),))

        def gemm_1024(dst_fn, rhs_list, rhs_bufs, epi, n_oc=8, banks=(0, 1, 2, 3)):
            for oc in range(n_oc):
                for tt in range(TT):
                    bank = banks[(oc * TT + tt) % len(banks)]
                    emits, ins_per = [], []
                    for kc2 in range(KC):
                        wv = self.stg[kc2 // 4]
                        emits.append(lambda e, kc2=kc2, oc=oc, tt=tt, bank=bank, wv=wv: e.matmul(
                            PB[bank][:, :],
                            wv[:, (kc2 % 4) * D + oc * 128:(kc2 % 4) * D + oc * 128 + 128],
                            rhs_list[kc2][:, ts(tt, 512)],
                            start=(kc2 == 0), stop=(kc2 == KC - 1)))
                        ins_per.append((stgB[kc2 // 4], rhs_bufs[kc2]))
                    kb.group("t", emits, outs=(PBB[bank],), ins_per=ins_per)
                    epi(oc, tt, bank)

        b1B = poolB[0:8]
        for proj, wname, colbase in (("q", "wq1", 0), ("k", "wk1", T)):
            load_strips(wname)
            self.prep_pump(16)

            def epi(oc, tt, bank, colbase=colbase):
                kb.op("v", lambda e, oc=oc, tt=tt, bank=bank: e.tensor_copy(
                    qx[oc][:, colbase + tt * 512: colbase + tt * 512 + 512], PB[bank][:, :]),
                    outs=(xsB[oc],), ins=(PBB[bank],))
                self.prep_pump(1)
            gemm_1024(None, b1, b1B, epi)

        cqs, sqs = self.scr[0], self.scr[1]
        for tt in range(TT):
            self.bc_rows(tt, rows=("rstd",), banks=(4,))
            ct = self.scr[2]
            kb.dma("q", ct[:, 0:512], d["cos_t"][:, ts(tt, 512)], outs=(scrB[2],))
            kb.dma("q", ct[:, 512:1024], d["sin_t"][:, ts(tt, 512)], outs=(scrB[2],))
            kb.op("v", lambda e, tt=tt: e.tensor_tensor(
                cqs[:, ts(tt, 512)], ct[:, 0:512], self.pb[4][:, :], OP.mult),
                outs=(scrB[0],), ins=(scrB[2], self.pbb[4]))
            kb.op("v", lambda e, tt=tt: e.tensor_tensor(
                sqs[:, ts(tt, 512)], ct[:, 512:1024], self.pb[4][:, :], OP.mult),
                outs=(scrB[1],), ins=(scrB[2], self.pbb[4]))
        R2 = self.cid["R2"]
        R2B = self.B("R2_s")
        kb.op("v", lambda e: e.memset(self.small[1][:], 0.0), outs=(smB[1],))
        for colbase in (0, T):
            for hp in range(KC):
                self.prep_pump(2)
                for tt in range(TT):
                    sl = slice(colbase + tt * 512, colbase + tt * 512 + 512)
                    bank = tt % 2
                    kb.op("t", lambda e, hp=hp, sl=sl, bank=bank: e.matmul(
                        PB[bank][:, :], R2[:], qx[hp][:, sl], start=True, stop=True),
                        outs=(PBB[bank],), ins=(xsB[hp], R2B))
                    t1 = self.scr[3]
                    kb.op("v", lambda e, hp=hp, sl=sl, tt=tt: e.tensor_tensor(
                        t1[:, 0:512], qx[hp][:, sl], cqs[:, ts(tt, 512)], OP.mult),
                        outs=(scrB[3],), ins=(xsB[hp], scrB[0]))
                    kb.op("v", lambda e, tt=tt, bank=bank: e.tensor_tensor(
                        t1[:, 512:1024], PB[bank][:, :], sqs[:, ts(tt, 512)], OP.mult),
                        outs=(scrB[3],), ins=(PBB[bank], scrB[1]))
                    kb.op("v", lambda e, hp=hp, sl=sl: e.tensor_tensor(
                        qx[hp][:, sl], t1[:, 0:512], t1[:, 512:1024], OP.add),
                        outs=(xsB[hp],), ins=(scrB[3],))
                    sqt = self.vs2[1]
                    kb.op("v", lambda e, hp=hp, sl=sl: e.tensor_tensor(
                        sqt[:, 0:512], qx[hp][:, sl], qx[hp][:, sl], OP.mult),
                        outs=(vsB[1],), ins=(xsB[hp],))
                    kb.op("t", lambda e: e.matmul(
                        PB[5][0:1, 0:512], self.cid["ones_b"][:, 0:1], sqt[:, 0:512],
                        start=True, stop=True),
                        outs=(PBB[5],), ins=(vsB[1], onesbB))
                    gi = 0 if colbase == 0 else 1
                    kb.op("v", lambda e: e.reduce_max(
                        self.small[1][0:1, 4:5], PB[5][0:1, 0:512], axis=AX),
                        outs=(smB[1],), ins=(PBB[5], smB[1]))
                    kb.op("v", lambda e, gi=gi: e.tensor_tensor(
                        self.small[1][0:1, gi:gi + 1], self.small[1][0:1, gi:gi + 1],
                        self.small[1][0:1, 4:5], OP.max),
                        outs=(smB[1],), ins=(smB[1],))
        kb.op("s", lambda e: e.activation(self.small[1][0:1, 2:4], self.small[1][0:1, 0:2], AF.Sqrt),
              outs=(smB[1],), ins=(smB[1],))
        kb.op("v", lambda e: e.tensor_tensor(
            self.small[1][0:1, 4:5], self.small[1][0:1, 2:3], self.small[1][0:1, 3:4], OP.mult),
            outs=(smB[1],), ins=(smB[1],))
        kb.op("v", lambda e: e.tensor_scalar(
            self.small[1][0:1, 5:6], self.small[1][0:1, 4:5], -0.125, None, OP.mult),
            outs=(smB[1],), ins=(smB[1],))
        kb.op("t", lambda e: e.matmul(
            PB[5][:, 0:1], self.cid["ones_f"][0:1, :], self.small[1][0:1, 5:6],
            start=True, stop=True),
            outs=(PBB[5],), ins=(smB[1], onesfB))
        kb.op("v", lambda e: e.tensor_copy(self.small[2][:, 0:1], PB[5][:, 0:1]),
              outs=(smB[2],), ins=(PBB[5],))
        negm = self.small[2][:, 0:1]

        for tt in range(TT):
            self.bc_rows(tt, rows=("rstd",), banks=(4,))
            kb.op("v", lambda e, tt=tt: e.tensor_copy(self.scr[0][:, ts(tt, 512)], self.pb[4][:, :]),
                  outs=(scrB[0],), ins=(self.pbb[4],))

        load_strips("wv1")
        identb = self.cid["ident_b"]
        idbB = self.B("ident_b_s")
        utf = self.cid["ut_f"]
        utfB = self.B("ut_f_s")
        recrow = self.rows["mean"]
        recrowB = self.rr["mean"]
        for hp in range(KC):
            vfm = self.pool[8]
            for tt in range(TT):
                bank = tt % 2
                emits, ins_per = [], []
                for kc2 in range(KC):
                    wv = self.stg[kc2 // 4]
                    emits.append(lambda e, kc2=kc2, hp=hp, tt=tt, bank=bank, wv=wv: e.matmul(
                        PB[bank][:, :],
                        wv[:, (kc2 % 4) * D + hp * 128:(kc2 % 4) * D + hp * 128 + 128],
                        b1[kc2][:, ts(tt, 512)],
                        start=(kc2 == 0), stop=(kc2 == KC - 1)))
                    ins_per.append((stgB[kc2 // 4], b1B[kc2]))
                kb.group("t", emits, outs=(PBB[bank],), ins_per=ins_per)
                kb.op("v", lambda e, tt=tt, bank=bank: e.tensor_tensor(
                    vfm[:, ts(tt, 512)], PB[bank][:, :], self.scr[0][:, ts(tt, 512)], OP.mult),
                    outs=(poolB[8],), ins=(PBB[bank], scrB[0]))
            vs = self.vs2[0]
            vsb = vsB[0]
            kb.op("v", lambda e: e.memset(vs[:], 1.0), outs=(vsb,))
            for tc in range(TC):
                pt = self.pt[tc % 2]
                ptB = self.ptb[tc % 2]
                kb.op("t", lambda e, tc=tc, pt=pt: e.transpose(
                    pt[:, 0:128], vfm[:, ts(tc)], identb[:]),
                    outs=(ptB,), ins=(poolB[8], idbB))
                kb.op("v", lambda e, tc=tc, pt=pt: e.tensor_copy(
                    vs[:, 130 * tc:130 * tc + 64], pt[:, 0:64]),
                    outs=(vsb,), ins=(ptB,))
                kb.op("v", lambda e, tc=tc, pt=pt: e.tensor_copy(
                    vs[:, 130 * tc + 65:130 * tc + 129], pt[:, 64:128]),
                    outs=(vsb,), ins=(ptB,))
            for hl in range(2):
                hs = slice(64 * hl, 64 * hl + 64)
                for j in range(TT):
                    self.prep_pump(3)
                    bO = 2 + (j % 2)
                    n_ck = min(4 * j + 4, TC)
                    for ck in range(n_ck):
                        bA = ck % 2
                        kb.op("t", lambda e, hp=hp, hs=hs, j=j, ck=ck, bA=bA: e.matmul(
                            PB[bA][:, :],
                            qx[hp][hs, slice(T + 128 * ck, T + 128 * ck + 128)],
                            qx[hp][hs, ts(j, 512)],
                            start=True, stop=True),
                            outs=(PBB[bA],), ins=(xsB[hp],))
                        pts = self.vs2[1]
                        po = 512 * (ck % 2)
                        kb.op("s", lambda e, bA=bA, po=po, pts=pts: e.activation(
                            pts[:, po:po + 512], PB[bA][:, :], AF.Exp, bias=negm, scale=0.125),
                            outs=(vsB[1],), ins=(PBB[bA], smB[2]))
                        dz = (ck - 4 * j) * 128
                        if dz > 0:
                            kb.op("v", lambda e, po=po, dz=dz, pts=pts: e.memset(
                                pts[:, po:po + dz], 0.0),
                                outs=(vsB[1],), ins=())
                        if 0 <= dz < 512:
                            kb.op("v", lambda e, po=po, dz=dz, pts=pts: e.tensor_tensor(
                                pts[:, po + dz:po + dz + 128], pts[:, po + dz:po + dz + 128],
                                utf[:], OP.mult),
                                outs=(vsB[1],), ins=(vsB[1], utfB))
                        kb.op("t", lambda e, ck=ck, hl=hl, bO=bO, po=po, pts=pts, n_ck=n_ck: e.matmul(
                            PB[bO][0:65, :], self.vs2[0][:, 130 * ck + 65 * hl:130 * ck + 65 * hl + 65],
                            pts[:, po:po + 512],
                            start=(ck == 0), stop=(ck == n_ck - 1)),
                            outs=(PBB[bO],), ins=(vsB[0], vsB[1]))
                    kb.op("v", lambda e, bO=bO: e.reciprocal(recrow[:, 0:512], PB[bO][64:65, :]),
                          outs=(recrowB,), ins=(PBB[bO],))
                    kb.op("t", lambda e, bO=bO: e.matmul(
                        PB[4][0:64, :], self.cid["ones_f"][0:1, 0:64], recrow[:, 0:512],
                        start=True, stop=True),
                        outs=(PBB[4],), ins=(recrowB, onesfB))
                    kb.op("v", lambda e: e.tensor_copy(self.scr[3][:, 0:512], PB[4][0:64, :][0:64, :])
                          if False else e.tensor_copy(self.scr[3][0:64, 0:512], PB[4][0:64, :]),
                          outs=(scrB[3],), ins=(PBB[4],))
                    kb.op("v", lambda e, hp=hp, hs=hs, j=j, bO=bO: e.tensor_tensor(
                        qx[hp][hs, ts(j, 512)], PB[bO][0:64, :], self.scr[3][0:64, 0:512], OP.mult),
                        outs=(xsB[hp],), ins=(PBB[bO], scrB[3]))

        load_strips("wo1")
        attn2 = qx
        hb = self.vs2[1]
        for oc in range(KC):
            hin = self.scr[1 + (oc % 2)]
            hinB = scrB[1 + (oc % 2)]
            kb.dma("q", hin[:, 0:T], d["xn2_d"][ts(oc), :],
                   outs=(hinB,), ins=(self.B("xn2_d"),))
            for tt in range(TT):
                bank = tt % 2
                emits, ins_per = [], []
                for kc2 in range(KC):
                    wv = self.stg[kc2 // 4]
                    emits.append(lambda e, kc2=kc2, oc=oc, tt=tt, bank=bank, wv=wv: e.matmul(
                        PB[bank][:, :],
                        wv[:, (kc2 % 4) * D + oc * 128:(kc2 % 4) * D + oc * 128 + 128],
                        attn2[kc2][:, ts(tt, 512)],
                        start=(kc2 == 0), stop=(kc2 == KC - 1)))
                    ins_per.append((stgB[kc2 // 4], xsB[kc2]))
                kb.group("t", emits, outs=(PBB[bank],), ins_per=ins_per)
                kb.op("v", lambda e, tt=tt, bank=bank, hin=hin: e.tensor_tensor(
                    hin[:, ts(tt, 512)], hin[:, ts(tt, 512)], PB[bank][:, :], OP.add),
                    outs=(hinB,), ins=(hinB, PBB[bank]))
                kb.op("v", lambda e, tt=tt, hin=hin: e.tensor_copy(
                    hb[:, 0:512], hin[:, ts(tt, 512)]),
                    outs=(vsB[1],), ins=(hinB,))
                kb.op("v", lambda e: e.tensor_tensor(hb[:, 512:1024], hb[:, 0:512], hb[:, 0:512], OP.mult),
                      outs=(vsB[1],), ins=(vsB[1],))
                kb.op("t", lambda e, oc=oc, tt=tt: e.matmul(
                    self.pb[2 + tt][0:1, :],
                    self.cid["ones_b"][:, 0:1], hb[:, 512:1024],
                    start=(oc == 0), stop=(oc == KC - 1)),
                    outs=(self.pbb[2 + tt],), ins=(vsB[1], onesbB))
            kb.dma("s", d["hin_d"][ts(oc), :], hin[:, 0:T], outs=(self.B("hin_d"),), ins=(hinB,))
        self.rms_rows_from_banks()

        b2 = self.pool[0:8]
        for oc in range(KC):
            hin = self.scr[1 + (oc % 2)]
            hinB = scrB[1 + (oc % 2)]
            kb.dma("q", hin[:, 0:T], d["hin_d"][ts(oc), :], outs=(hinB,), ins=(self.B("hin_d"),))
            for tt in range(TT):
                self.bc_rows(tt, rows=("rstd",), banks=(4,))
                kb.op("v", lambda e, oc=oc, tt=tt, hin=hin: e.tensor_tensor(
                    self.scr[3][:, 0:512], hin[:, ts(tt, 512)], self.pb[4][:, :], OP.mult),
                    outs=(scrB[3],), ins=(hinB, self.pbb[4]))
                kb.op("v", lambda e, oc=oc, tt=tt: e.tensor_scalar(
                    b2[oc][:, ts(tt, 512)], self.scr[3][:, 0:512], self.vec("g2c", oc), None, OP.mult),
                    outs=(poolB[oc],), ins=(scrB[3], self.B("vecs")))

        self.prep_pump(10000)
        self.emit_router(b2, poolB)

        self.emit_moe(b2, poolB)

    def emit_b2(self):
        kb, T, TC, TT = self.kb, self.T, self.TC, self.TT
        d = self.dram
        xsB = [self.B(f"xs{k}") for k in range(KC)]
        scrB = [self.B(f"scr{i}") for i in range(4)]
        stgB = [self.B(f"stg{i}") for i in range(3)]
        poolB = [self.B(f"pl{i}") for i in range(12)]
        vsB = [self.B("vs0"), self.B("vs1")]
        smB = [self.B(f"sm{i}") for i in range(4)]
        PB, PBB = self.pb, self.pbb
        sm3 = self.small[3]
        for wn in ("Wr", "Wk", "Wv", "Wo"):
            kb.dma("g", d[wn + "_bf"], d[wn], outs=(self.B(wn + "_bf"),))
        self.load_vec("q", "decc", d["decc"], 8)
        self.load_vec("q", "lnwc", d["lnwc"], 8)
        kb.op("s", lambda e: e.activation(sm3[:, 0:8], self.vec("decc"), AF.Sigmoid),
              outs=(smB[3],), ins=(self.B("vecs"),))
        kb.op("v", lambda e: e.tensor_scalar(sm3[:, 0:8], sm3[:, 0:8], 1e-7, None, OP.max),
              outs=(smB[3],), ins=(smB[3],))
        kb.op("s", lambda e: e.activation(sm3[:, 8:16], sm3[:, 0:8], AF.Ln),
              outs=(smB[3],), ins=(smB[3],))

        self.ln_stats("ln3", 1e-5)
        xn3 = self.pool[0:8]
        for k in range(KC):
            kb.dma("s", d["s2_d"][ts(k), :], self.xs[k][:], outs=(self.B("s2_d"),), ins=(xsB[k],))
        for tt in range(TT):
            self.bc_rows(tt)
            mb, rb = self.scr[0], self.scr[1]
            kb.op("v", lambda e: e.tensor_copy(mb[:, 0:512], self.pb[4][:, :]),
                  outs=(scrB[0],), ins=(self.pbb[4],))
            kb.op("v", lambda e: e.tensor_copy(rb[:, 0:512], self.pb[5][:, :]),
                  outs=(scrB[1],), ins=(self.pbb[5],))
            for k in range(KC):
                tmp = self.scr[2]
                kb.op("g", lambda e, k=k, tt=tt: e.tensor_tensor(
                    tmp[:, ts(tt, 512)], self.xs[k][:, ts(tt, 512)], mb[:, 0:512], OP.subtract),
                    outs=(scrB[2],), ins=(xsB[k], scrB[0]))
                kb.op("v", lambda e, k=k, tt=tt: e.tensor_tensor(
                    xn3[k][:, ts(tt, 512)], tmp[:, ts(tt, 512)], rb[:, 0:512], OP.mult),
                    outs=(poolB[k],), ins=(scrB[2], scrB[1]))

        Wr_bf, Wk_bf, Wv_bf = d["Wr_bf"], d["Wk_bf"], d["Wv_bf"]
        for oc in range(KC):
            stg = self.stg[0]
            sB = stgB[0]
            for pi, wbf, wB in ((0, Wr_bf, self.B("Wr_bf")), (1, Wk_bf, self.B("Wk_bf")),
                                (2, Wv_bf, self.B("Wv_bf"))):
                for kc2 in range(KC):
                    kb.dmat("q", stg[:, pi * 1024 + 128 * kc2:pi * 1024 + 128 * kc2 + 128],
                            wbf[oc * 128:oc * 128 + 128, ts(kc2)],
                            outs=(sB,), ins=(wB,))
            for tt in range(TT):
                for pi, bank in ((0, 0), (1, 1), (2, 2)):
                    emits, ins_per = [], []
                    for kc2 in range(KC):
                        emits.append(lambda e, kc2=kc2, tt=tt, pi=pi, bank=bank, stg=stg: e.matmul(
                            PB[bank][:, :], stg[:, pi * 1024 + 128 * kc2:pi * 1024 + 128 * kc2 + 128],
                            xn3[kc2][:, ts(tt, 512)],
                            start=(kc2 == 0), stop=(kc2 == KC - 1)))
                        ins_per.append((sB, poolB[kc2]))
                    kb.group("t", emits, outs=(PBB[bank],), ins_per=ins_per)
                kb.op("s", lambda e, tt=tt: e.activation(
                    self.scr[3][:, ts(tt, 512)], PB[0][:, :], AF.Sigmoid),
                    outs=(scrB[3],), ins=(PBB[0],))
                kb.op("v", lambda e, tt=tt: e.tensor_copy(self.scr[2][:, ts(tt, 512)], PB[1][:, :]),
                      outs=(scrB[2],), ins=(PBB[1],))
                kb.op("v", lambda e, tt=tt: e.tensor_tensor(
                    self.scr[2][:, ts(tt, 512)], self.scr[2][:, ts(tt, 512)], PB[2][:, :], OP.mult),
                    outs=(scrB[2],), ins=(scrB[2], PBB[2]))
            kb.dma("q", self.scr[0][:], d["t_bc"], outs=(scrB[0],))
            ldap = sm3[:, 8 + oc:9 + oc]
            kb.op("v", lambda e, ldap=ldap: e.tensor_scalar(
                self.scr[0][:], self.scr[0][:], ldap, None, OP.mult),
                outs=(scrB[0],), ins=(scrB[0], smB[3]))
            kb.op("v", lambda e: e.tensor_scalar(self.scr[1][:], self.scr[0][:], -1.0, 23.1,
                                                 OP.mult, OP.min),
                  outs=(scrB[1],), ins=(scrB[0],))
            kb.op("s", lambda e: e.activation(self.scr[1][:], self.scr[1][:], AF.Exp),
                  outs=(scrB[1],), ins=(scrB[1],))
            kb.op("v", lambda e: e.tensor_scalar(self.scr[1][:], self.scr[1][:], 1e10, None, OP.min),
                  outs=(scrB[1],), ins=(scrB[1],))
            kb.op("s", lambda e: e.activation(self.scr[0][:], self.scr[0][:], AF.Exp),
                  outs=(scrB[0],), ins=(scrB[0],))
            kb.op("v", lambda e: e.tensor_tensor(
                self.scr[2][:], self.scr[2][:], self.scr[1][:], OP.mult),
                outs=(scrB[2],), ins=(scrB[2], scrB[1]))
            kb.op("v", lambda e: e.tensor_tensor_scan(
                self.scr[2][:], self.scr[2][:], self.scr[2][:], 0.0, OP.add, OP.bypass),
                outs=(scrB[2],), ins=(scrB[2],))
            kb.op("v", lambda e: e.tensor_tensor(
                self.scr[2][:], self.scr[2][:], self.scr[0][:], OP.mult),
                outs=(scrB[2],), ins=(scrB[2], scrB[0]))
            kb.op("v", lambda e, oc=oc: e.tensor_tensor(
                self.xs[oc][:], self.scr[2][:], self.scr[3][:], OP.mult),
                outs=(xsB[oc],), ins=(scrB[2], scrB[3]))

        if getattr(self, "b2_stop", None) == "rs":
            return
        self.ln_stats("rms_l1", 1e-6, rms=True, src=self.xs, src_bufs=xsB)
        ybf = self.pool[0:8]
        for oc in range(KC):
            for tt in range(TT):
                self.bc_rows(tt, rows=("rstd",), banks=(4,))
                kb.op("v", lambda e, oc=oc, tt=tt: e.tensor_tensor(
                    self.scr[2][:, 0:512], self.xs[oc][:, ts(tt, 512)], self.pb[4][:, :], OP.mult),
                    outs=(scrB[2],), ins=(xsB[oc], self.pbb[4]))
                kb.op("v", lambda e, oc=oc, tt=tt: e.tensor_scalar(
                    ybf[oc][:, ts(tt, 512)], self.scr[2][:, 0:512], self.vec("lnwc", oc), None, OP.mult),
                    outs=(poolB[oc],), ins=(scrB[2], self.B("vecs")))

        for kc2 in range(KC):
            dst = self.stg[kc2 // 4]
            kb.dmat("q", dst[:, (kc2 % 4) * D:(kc2 % 4) * D + D],
                    d["Wo_bf"][:, ts(kc2)], outs=(stgB[kc2 // 4],), ins=(self.B("Wo_bf"),))
        for oc in range(KC):
            s2t = self.scr[oc % 2]
            s2tB = scrB[oc % 2]
            kb.dma("q", s2t[:], d["s2_d"][ts(oc), :], outs=(s2tB,), ins=(self.B("s2_d"),))
            for tt in range(TT):
                bank = tt % 4
                emits, ins_per = [], []
                for kc2 in range(KC):
                    wv = self.stg[kc2 // 4]
                    emits.append(lambda e, kc2=kc2, oc=oc, tt=tt, bank=bank, wv=wv: e.matmul(
                        PB[bank][:, :],
                        wv[:, (kc2 % 4) * D + oc * 128:(kc2 % 4) * D + oc * 128 + 128],
                        ybf[kc2][:, ts(tt, 512)],
                        start=(kc2 == 0), stop=(kc2 == KC - 1)))
                    ins_per.append((stgB[kc2 // 4], poolB[kc2]))
                kb.group("t", emits, outs=(PBB[bank],), ins_per=ins_per)
                kb.op("v", lambda e, oc=oc, tt=tt, bank=bank, s2t=s2t: e.tensor_tensor(
                    self.xs[oc][:, ts(tt, 512)], s2t[:, ts(tt, 512)], PB[bank][:, :], OP.add),
                    outs=(xsB[oc],), ins=(s2tB, PBB[bank]))

    def emit_b3(self):
        kb, T, TC, TT = self.kb, self.T, self.TC, self.TT
        d = self.dram
        xsB = [self.B(f"xs{k}") for k in range(KC)]
        scrB = [self.B(f"scr{i}") for i in range(4)]
        stgB = [self.B(f"stg{i}") for i in range(3)]
        poolB = [self.B(f"pl{i}") for i in range(12)]
        PB, PBB = self.pb, self.pbb
        for wn, src_n in (("w1_bf", "w1w"), ("w2_bf", "w2w"), ("w3_bf", "w3w")):
            kb.dma("g", d[wn], d[src_n], outs=(self.B(wn),))
        for wn, src_n in (("h1_bf", "h1w"), ("h2_bf", "h2w")):
            for i in range(4):
                kb.dma("g", d[wn][ts(i, 1024), :], d[src_n][ts(i, 1024), :], outs=(self.B(wn),))
        self.load_vec("q", "s1c", d["s1c"], 32)
        self.load_vec("q", "s2c", d["s2c"], 32)
        self.load_vec("q", "hs1c", d["hs1c"], 32)
        self.load_vec("q", "hs2c", d["hs2c"], 32)
        self.load_vec("q", "s3c", d["s3c"], 8)

        self.ln_stats("ln4", 1e-5)
        for k in range(KC):
            kb.dma("s", d["s3_d"][ts(k), :], self.xs[k][:], outs=(self.B("s3_d"),), ins=(xsB[k],))

        slots = []
        for i in range(3):
            for j in range(8):
                slots.append((self.stg[i][:, 512 * j:512 * j + 512], stgB[i]))
        for pi in (10, 11):
            for j in range(4):
                slots.append((self.pool[pi][:, 512 * j:512 * j + 512], poolB[pi]))
        self.slot_i = 0

        def next_slot():
            s = slots[self.slot_i % len(slots)]
            self.slot_i += 1
            return s

        def hx(i):
            return self.xs[i // 8].bitcast(BF16)[:, (i % 8) * 512:(i % 8) * 512 + 512], xsB[i // 8]

        def gx(i):
            return self.xs[4 + i // 8].bitcast(BF16)[:, (i % 8) * 512:(i % 8) * 512 + 512], xsB[4 + i // 8]

        def big_gemm(wbf_name, n_out, n_k, rhs_fn, epi, tq):
            wbf = d[wbf_name]
            wB = self.B(wbf_name)
            for ocg in range((n_out + 3) // 4):
                ocs = [ocg * 4 + i for i in range(4) if ocg * 4 + i < n_out]
                burst_slots = {}
                for kc0 in range(0, n_k, 8):
                    cur = []
                    for kc2 in range(kc0, min(kc0 + 8, n_k)):
                        sl, slB = next_slot()
                        kb.dmat("q", sl, wbf[ocg * 512:ocg * 512 + 512, ts(kc2)],
                                outs=(slB,), ins=(wB,))
                        cur.append((kc2, sl, slB))
                    for oci, oc in enumerate(ocs):
                        for bi, (kc2, sl, slB) in enumerate(cur):
                            rhs, rhsB = rhs_fn(kc2)
                            kb.op("t", lambda e, sl=sl, oci=oci, oc=oc, rhs=rhs, kc2=kc2, n_k=n_k: e.matmul(
                                PB[oci][:, :], sl[:, 128 * (oc % 4):128 * (oc % 4) + 128], rhs,
                                start=(kc2 == 0), stop=(kc2 == n_k - 1)),
                                outs=(PBB[oci],), ins=(slB, rhsB))
                for oci, oc in enumerate(ocs):
                    epi(oc, oci)

        identf = self.cid["ident_f"]
        idB = self.B("ident_f_s")
        for tt in range(TT):
            self.bc_rows(tt)
            mb, rb = self.scr[0], self.scr[1]
            kb.op("v", lambda e: e.tensor_copy(mb[:, 0:512], self.pb[4][:, :]),
                  outs=(scrB[0],), ins=(self.pbb[4],))
            kb.op("v", lambda e: e.tensor_copy(rb[:, 0:512], self.pb[5][:, :]),
                  outs=(scrB[1],), ins=(self.pbb[5],))
            for kc2 in range(KC):
                tmp = self.scr[2]
                kb.op("g", lambda e, kc2=kc2, tt=tt: e.tensor_tensor(
                    tmp[:, ts(tt, 512)], self.xs[kc2][:, ts(tt, 512)], mb[:, 0:512], OP.subtract),
                    outs=(scrB[2],), ins=(xsB[kc2], scrB[0]))
                kb.op("v", lambda e, kc2=kc2, tt=tt: e.tensor_tensor(
                    self.pool[kc2][:, ts(tt, 512)], tmp[:, ts(tt, 512)], rb[:, 0:512], OP.mult),
                    outs=(poolB[kc2],), ins=(scrB[2], scrB[1]))
        for tq in range(TT):
            q0 = tq * 512

            def xn4(kc2, q0=q0):
                return (self.pool[kc2][:, q0:q0 + 512], poolB[kc2])
            if getattr(self, "b2_stop", None) == "xn4":
                return
            def epi_w1(oc, bank):
                sgt, sgtB = gx(oc)
                kb.op("s", lambda e, oc=oc, bank=bank, sgt=sgt: e.activation(
                    sgt, PB[bank][:, :], AF.Silu, scale=self.vec("s1c", oc)),
                    outs=(sgtB,), ins=(PBB[bank], self.B("vecs")))
            def epi_w2(oc, bank):
                sgt, sgtB = gx(oc)
                ht, htB = hx(oc)
                kb.op("v", lambda e, oc=oc, bank=bank: e.tensor_scalar(
                    self.scr[2][:, 0:512], PB[bank][:, :], self.vec("s2c", oc), None, OP.mult),
                    outs=(scrB[2],), ins=(PBB[bank], self.B("vecs")))
                kb.op("v", lambda e, ht=ht, sgt=sgt: e.tensor_tensor(
                    ht, sgt, self.scr[2][:, 0:512], OP.mult),
                    outs=(htB,), ins=(sgtB, scrB[2]))
            big_gemm("w1_bf", 32, 8, xn4, epi_w1, tq)
            big_gemm("w2_bf", 32, 8, xn4, epi_w2, tq)
            if getattr(self, "b2_stop", None) == "h":
                return

            def epi_h1(oc, bank):
                gt, gtB = gx(oc)
                kb.op("s", lambda e, oc=oc, bank=bank, gt=gt: e.activation(
                    gt, PB[bank][:, :], AF.Gelu, scale=self.vec("hs1c", oc)),
                    outs=(gtB,), ins=(PBB[bank], self.B("vecs")))
            big_gemm("h1_bf", 32, 32, hx, epi_h1, tq)
            if getattr(self, "b2_stop", None) == "g":
                return

            def epi_h2(oc, bank):
                ht, htB = hx(oc)
                kb.op("s", lambda e, oc=oc, bank=bank, ht=ht: e.activation(
                    ht, PB[bank][:, :], AF.Gelu, scale=self.vec("hs2c", oc)),
                    outs=(htB,), ins=(PBB[bank], self.B("vecs")))
            big_gemm("h2_bf", 32, 32, gx, epi_h2, tq)

            def epi_w3(oc, bank):
                s3t = self.scr[1 + (oc % 2)]
                s3tB = scrB[1 + (oc % 2)]
                kb.dma("q", s3t[:, 0:512], d["s3_d"][ts(oc), q0:q0 + 512],
                       outs=(s3tB,), ins=(self.B("s3_d"),))
                kb.op("v", lambda e, oc=oc, bank=bank, s3t=s3t: e.scalar_tensor_tensor(
                    self.scr[3][:, 0:512], PB[bank][:, :], self.vec("s3c", oc), s3t[:, 0:512],
                    OP.mult, OP.add),
                    outs=(scrB[3],), ins=(PBB[bank], self.B("vecs"), s3tB))
                if self.io_out == "fm":
                    kb.dma("s", d["y_out_fm"][ts(oc), q0:q0 + 512], self.scr[3][:, 0:512],
                           outs=(self.B("y_out_fm"),), ins=(scrB[3],))
                else:
                    def stage(tb):
                        if tb < 2:
                            return (self.pool[8 + tb].bitcast(F32)[:, 0:1024],
                                    poolB[8 + tb])
                        return self.scr[0][:, 1024 * (tb - 2):1024 * (tb - 2) + 1024], scrB[0]
                    for tb in range(4):
                        bb = 4 + tb % 2
                        stg_ap, stg_b = stage(tb)
                        kb.op("t", lambda e, tb=tb, bb=bb: e.transpose(
                            PB[bb][:, 0:128], self.scr[3][:, 128 * tb:128 * tb + 128], identf[:]),
                            outs=(PBB[bb],), ins=(scrB[3], idB))
                        kb.op("v", lambda e, oc=oc, bb=bb, stg_ap=stg_ap: e.tensor_copy(
                            stg_ap[:, 128 * oc:128 * oc + 128], PB[bb][:, 0:128]),
                            outs=(stg_b,), ins=(PBB[bb],))
                        if oc == KC - 1:
                            kb.dma("s", d["y_out"][q0 + 128 * tb:q0 + 128 * tb + 128, :],
                                   stg_ap[:, 0:1024],
                                   outs=(self.B("y_out"),), ins=(stg_b,))
            big_gemm("w3_bf", 8, 32, hx, epi_w3, tq)

    def finalize(self):
        kb = self.kb
        for sn, c in list(kb.cnt.items()):
            if sn.startswith("D") and c > 0:
                kb._wait("s", sn, c)


def build_program(T=2048, blocks=("b0",), io_in="tm", io_out="fm", b2_stop=None):
    nc = bass.Bass("TRN2", target_bir_lowering=False)
    p = Prog(nc, T, blocks, io_in, io_out)
    p.b2_stop = b2_stop
    p.declare()
    for e in ("t", "v", "s", "g"):
        p.kb.sem(e)
    p.emit_input()
    if "b0" in blocks:
        p.emit_b0()
    if "b1" in blocks:
        p.emit_b1()
    if "b2" in blocks:
        p.emit_b2()
    if "b3" in blocks:
        p.emit_b3()
    if "b3" not in blocks:
        p.emit_output()
    p.finalize()

    kb = p.kb
    def replay(eng, name):
        for entry in kb.q[name]:
            if entry[0] == "w":
                eng.wait_ge(kb.sems[entry[1]], entry[2])
            else:
                inst = entry[1](eng)
                if entry[2] is not None:
                    inst.then_inc(kb.sems[entry[2]], entry[3])

    with nc.Block() as block:
        @block.sync
        def _(sync):
            replay(sync, "q")

        @block.tensor
        def _(tensor):
            replay(tensor, "t")

        @block.vector
        def _(vector):
            replay(vector, "v")

        @block.scalar
        def _(scalar):
            replay(scalar, "s")

        @block.gpsimd
        def _(gpsimd):
            replay(gpsimd, "g")
    kb.st.close()
    return nc, p


def host_consts(T):
    c = {}
    c["ident_f"] = np.eye(128, dtype=np.float32)
    c["ident_b"] = np.eye(128, dtype=np.float32).astype(ml_dtypes.bfloat16)
    c["ones_f"] = np.ones((128, 128), np.float32)
    c["ones_b"] = np.ones((128, 128), np.float32).astype(ml_dtypes.bfloat16)
    c["ut_f"] = np.triu(np.ones((128, 128), np.float32))
    R = np.zeros((64, 64), np.float32)
    for m in range(32):
        R[m + 32, m] = -1.0
    for m in range(32, 64):
        R[m - 32, m] = 1.0
    R2 = np.zeros((128, 128), np.float32)
    R2[0:64, 0:64] = R
    R2[64:128, 64:128] = R
    c["R2"] = R2.astype(ml_dtypes.bfloat16)
    dim = 64
    inv_freq = 1.0 / (10000.0 ** (np.arange(0, dim, 2, dtype=np.float32) / dim))
    tpos = np.arange(T, dtype=np.float32)
    freqs = np.einsum("i,j->ij", tpos, inv_freq)
    emb = np.concatenate([freqs, freqs], axis=-1)
    cos = np.cos(emb).T.astype(np.float32)
    sin = np.sin(emb).T.astype(np.float32)
    c["cos_t"] = np.concatenate([cos, cos], axis=0)
    c["sin_t"] = np.concatenate([sin, sin], axis=0)
    c["t_bc"] = np.broadcast_to(np.arange(T, dtype=np.float32), (128, T)).copy()
    sel = np.zeros((8, 1024), np.float32)
    for e in range(8):
        sel[e, 128 * e:128 * (e + 1)] = 1.0
    c["sel8"] = sel.astype(ml_dtypes.bfloat16)
    cc = np.zeros((128, 8), np.float32)
    cc[:, 0] = 1e-5
    cc[:, 1] = 1e-6
    cc[:, 3] = 1.0
    c["constc"] = cc
    return c


def vec_cols(v, groups, width=128):
    return np.ascontiguousarray(v.reshape(groups, width).T.astype(np.float32))


def _f32(a):
    return np.ascontiguousarray(np.asarray(a, dtype=np.float32))


def block_weight_map(blk, inputs):
    m = {}
    if blk == "b0":
        m["w_qkv"] = _f32(inputs["L0_attn_w_qkv"])
        m["bqc"] = vec_cols(_f32(inputs["L0_attn_b_qkv"]), 24)
        m["w_out"] = _f32(inputs["L0_attn_w_out"])
        m["boc"] = vec_cols(_f32(inputs["L0_attn_b_out"]), 8)
    elif blk == "b1":
        m["wq1"] = _f32(inputs["L0_ffn_wq"])
        m["wk1"] = _f32(inputs["L0_ffn_wk"])
        m["wv1"] = _f32(inputs["L0_ffn_wv"])
        m["wo1"] = _f32(inputs["L0_ffn_wo"])
        m["w_router"] = _f32(inputs["L0_ffn_w_router"])
        m["wg"] = _f32(inputs["L0_ffn_wg"])
        m["wu"] = _f32(inputs["L0_ffn_wu"])
        m["wd"] = _f32(inputs["L0_ffn_wd"])
        m["g1c"] = vec_cols(_f32(inputs["L0_ffn_g1"]), 8)
        m["g2c"] = vec_cols(_f32(inputs["L0_ffn_g2"]), 8)
    elif blk == "b2":
        m["Wr"] = _f32(inputs["L1_attn_Wr"])
        m["Wk"] = _f32(inputs["L1_attn_Wk"])
        m["Wv"] = _f32(inputs["L1_attn_Wv"])
        m["Wo"] = _f32(inputs["L1_attn_Wo"])
        m["decc"] = vec_cols(_f32(inputs["L1_attn_decay"]), 8)
        m["lnwc"] = vec_cols(_f32(inputs["L1_attn_ln_w"]), 8)
    elif blk == "b3":
        m["w1w"] = _f32(inputs["L1_ffn_w1_w"])
        m["w2w"] = _f32(inputs["L1_ffn_w2_w"])
        m["h1w"] = _f32(inputs["L1_ffn_h1_w"])
        m["h2w"] = _f32(inputs["L1_ffn_h2_w"])
        m["w3w"] = _f32(inputs["L1_ffn_w3_w"])
        m["s1c"] = vec_cols(_f32(inputs["L1_ffn_w1_s"]), 32)
        m["s2c"] = vec_cols(_f32(inputs["L1_ffn_w2_s"]), 32)
        m["hs1c"] = vec_cols(_f32(inputs["L1_ffn_h1_s"]), 32)
        m["hs2c"] = vec_cols(_f32(inputs["L1_ffn_h2_s"]), 32)
        m["s3c"] = vec_cols(_f32(inputs["L1_ffn_w3_s"]), 8)
    return m


FUSED = True


def kernel(**inputs):
    T = 2048
    n = 8
    consts = host_consts(T)
    x0 = _f32(inputs["L0_attn_x"])
    if FUSED:
        nc, p = build_program(T=T, blocks=("b0", "b1", "b2", "b3"), io_in="tm", io_out="tm")
        wmap = {}
        for blk in ("b0", "b1", "b2", "b3"):
            wmap.update(block_weight_map(blk, inputs))
        in_maps = []
        for c in range(n):
            m = dict(consts)
            m.update(wmap)
            m["x_in"] = np.ascontiguousarray(x0[c])
            in_maps.append(m)
        res = run_bass_kernel_spmd(nc, in_maps, core_ids=list(range(n)))
        return np.stack([np.ascontiguousarray(np.asarray(res.results[c]["y_out"], dtype=np.float32))
                         for c in range(n)], axis=0)
    cur = [np.ascontiguousarray(x0[c]) for c in range(n)]
    specs = [("b0", "tm", "fm"), ("b1", "fm", "fm"), ("b2", "fm", "fm"), ("b3", "fm", "tm")]
    for blk, ioi, ioo in specs:
        nc, p = build_program(T=T, blocks=(blk,), io_in=ioi, io_out=ioo)
        wmap = block_weight_map(blk, inputs)
        in_maps = []
        for c in range(n):
            m = dict(consts)
            m.update(wmap)
            m["x_in" if ioi == "tm" else "x_in_fm"] = cur[c]
            in_maps.append(m)
        res = run_bass_kernel_spmd(nc, in_maps, core_ids=list(range(n)))
        key = "y_out" if ioo == "tm" else "y_out_fm"
        cur = [np.ascontiguousarray(np.asarray(res.results[c][key], dtype=np.float32))
               for c in range(n)]
    return np.stack(cur, axis=0)


# version 11
# speedup vs baseline: 1.2339x; 1.0279x over previous
import numpy as np
import ml_dtypes
from contextlib import ExitStack

import concourse.bass as bass
import concourse.mybir as mybir
from concourse.bass_utils import run_bass_kernel_spmd
from concourse.alu_op_type import AluOpType as OP

F32 = mybir.dt.float32
BF16 = mybir.dt.bfloat16
AF = mybir.ActivationFunctionType
AX = mybir.AxisListType.X

D = 1024
KC = 8


class Buf:
    __slots__ = ("name", "writer", "readers")

    def __init__(self, name):
        self.name = name
        self.writer = None
        self.readers = {}


class KB:
    COMPUTE = {"t": 1, "v": 1, "s": 1, "g": 1}
    DMA_SEM = {"q": "dq", "s": "ds", "g": "dg", "v": "dv", "t": "dt"}

    def __init__(self, nc, T):
        self.nc = nc
        self.T = T
        self.TC = T // 128
        self.TT = T // 512
        self.st = ExitStack()
        self.eng = None
        self.sems = {}
        self.cnt = {}
        self.obs = {}
        self.bufs = {}
        self.q = {e: [] for e in ("t", "v", "s", "g", "q")}

    def sem(self, name):
        if name not in self.sems:
            self.sems[name] = self.st.enter_context(self.nc.semaphore(name))
            self.cnt[name] = 0
        return self.sems[name]

    def S(self, name, shape, dtype):
        h = self.st.enter_context(self.nc.sbuf_tensor(name, shape, dtype))
        self.bufs[name] = Buf(name)
        return h

    def PS(self, name, shape, dtype):
        h = self.st.enter_context(self.nc.psum_tensor(name, shape, dtype))
        self.bufs[name] = Buf(name)
        return h

    def buf(self, name):
        return self.bufs[name]

    def _wait(self, e, sem_name, val):
        if val <= 0:
            return
        key = (e, sem_name)
        if self.obs.get(key, 0) >= val:
            return
        self.q[e].append(("w", sem_name, val))
        self.obs[key] = val

    SELF_SYNC = True

    def _pre(self, e, outs, ins, is_compute):
        skip_own = is_compute and (e == "t" or not self.SELF_SYNC)
        own = e if skip_own else None
        for b in ins:
            if b.writer:
                ws, wc = b.writer
                if ws != own:
                    self._wait(e, ws, wc)
        for b in outs:
            if b.writer:
                ws, wc = b.writer
                if ws != own:
                    self._wait(e, ws, wc)
            for rs, rc in b.readers.items():
                if rs != own:
                    self._wait(e, rs, rc)

    def _post(self, sem_name, outs, ins):
        c = self.cnt[sem_name]
        for b in ins:
            if b.readers.get(sem_name, 0) < c:
                b.readers[sem_name] = c
        for b in outs:
            b.writer = (sem_name, c)
            b.readers = {}

    def op(self, e, emit, outs=(), ins=(), inc=True):
        self._pre(e, outs, ins, True)
        if inc:
            self.cnt[e] += 1
            self.q[e].append(("i", emit, e, 1))
            self._post(e, outs, ins)
        else:
            self.q[e].append(("i", emit, None, 0))

    def group(self, e, emits, outs=(), ins_per=None, inc=True):
        all_ins = []
        last = len(emits) - 1
        for i, emit in enumerate(emits):
            ins = ins_per[i] if ins_per else ()
            all_ins += list(ins)
            self._pre(e, outs if i == 0 else (), ins, True)
            if i < last:
                self.q[e].append(("i", emit, None, 0))
        self.cnt[e] += 1
        self.q[e].append(("i", emits[last], e, 1))
        self._post(e, outs, all_ins)

    def dma(self, q, out_ap, in_ap, outs=(), ins=(), **kw):
        assert outs, "dma needs a destination buf"
        sem_name = "D" + outs[0].name
        self.sem(sem_name)
        self._pre(q, outs, ins, False)
        self.cnt[sem_name] += 16
        self.q[q].append(("i", lambda e, o=out_ap, i_=in_ap, kw=kw: e.dma_start(out=o, in_=i_, **kw),
                          sem_name, 16))
        self._post(sem_name, outs, ins)

    def dmat(self, q, out_ap, in_ap, outs=(), ins=()):
        assert outs, "dmat needs a destination buf"
        sem_name = "D" + outs[0].name
        self.sem(sem_name)
        self._pre(q, outs, ins, False)
        self.cnt[sem_name] += 16
        self.q[q].append(("i", lambda e, o=out_ap, i_=in_ap: e.dma_start_transpose(o, i_),
                          sem_name, 16))
        self._post(sem_name, outs, ins)

    def split_buf(self, real, slots):
        for s in slots:
            s.writer = real.writer
            s.readers = dict(real.readers)

    def merge_buf(self, real, slots):
        rd = {}
        for s in slots:
            for k, v in s.readers.items():
                rd[k] = max(rd.get(k, 0), v)
            if s.writer:
                k, v = s.writer
                rd[k] = max(rd.get(k, 0), v)
        real.readers = rd
        real.writer = None

    def barrier_engine(self, e):
        for o, c in self.cnt.items():
            if o != e and c > 0:
                self._wait(e, o, c)


def ts(i, w=128):
    return slice(i * w, (i + 1) * w)


class Prog:

    def __init__(self, nc, T, blocks, io_in, io_out):
        self.nc = nc
        self.T = T
        self.TC = T // 128
        self.TT = T // 512
        self.blocks = blocks
        self.io_in = io_in
        self.io_out = io_out
        self.kb = KB(nc, T)

    def declare(self):
        nc, T = self.nc, self.T
        kb = self.kb
        bl = self.blocks
        self.dram = {}

        def din(name, shape, dtype=F32):
            self.dram[name] = nc.dram_tensor(name, shape, dtype, kind="ExternalInput").ap()

        def dscr(name, shape, dtype=BF16):
            self.dram[name] = nc.dram_tensor(name, shape, dtype, kind="Internal").ap()
            self.kb.bufs[name] = Buf(name)

        if self.io_in == "tm":
            din("x_in", [T, D])
        else:
            din("x_in_fm", [KC * 128, T])
        if self.io_out == "tm":
            self.dram["y_out"] = nc.dram_tensor("y_out", [T, D], F32, kind="ExternalOutput").ap()
            self.kb.bufs["y_out"] = Buf("y_out")
        else:
            self.dram["y_out_fm"] = nc.dram_tensor("y_out_fm", [KC * 128, T], F32, kind="ExternalOutput").ap()
            self.kb.bufs["y_out_fm"] = Buf("y_out_fm")

        din("ident_f", [128, 128])
        din("ident_b", [128, 128], BF16)
        din("ones_f", [128, 128])
        din("ones_b", [128, 128], BF16)
        din("ut_f", [128, 128])
        din("R2", [128, 128], BF16)
        din("cos_t", [128, T])
        din("sin_t", [128, T])
        din("t_bc", [128, T])
        din("constc", [128, 8])
        din("sel8", [8, 1024], BF16)
        if "b0" in bl:
            din("w_qkv", [3 * D, D])
            din("bqc", [128, 24])
            din("w_out", [D, D])
            din("boc", [128, 8])
            dscr("wqkv_bf", [3 * D, D])
            dscr("wout_bf", [D, D])
        if "b1" in bl:
            din("wq1", [D, D]); din("wk1", [D, D]); din("wv1", [D, D]); din("wo1", [D, D])
            din("w_router", [D, 8])
            din("wg", [8, D, 2752]); din("wu", [8, D, 2752]); din("wd", [8, 2752, D])
            din("g1c", [128, 8]); din("g2c", [128, 8])
            dscr("wd_bf", [8, 2752, D])
            dscr("wgu_t", [704 * 128, 1024])

            dscr("wq1_bf", [D, D]); dscr("wk1_bf", [D, D])
            dscr("wv1_bf", [D, D]); dscr("wo1_bf", [D, D])
        if "b2" in bl:
            din("Wr", [D, D]); din("Wk", [D, D]); din("Wv", [D, D]); din("Wo", [D, D])
            din("decc", [128, 8]); din("lnwc", [128, 8])
            dscr("Wr_bf", [D, D]); dscr("Wk_bf", [D, D]); dscr("Wv_bf", [D, D]); dscr("Wo_bf", [D, D])
        if "b3" in bl:
            din("w1w", [4096, D]); din("w2w", [4096, D])
            din("h1w", [4096, 4096]); din("h2w", [4096, 4096])
            din("w3w", [D, 4096])
            din("s1c", [128, 32]); din("s2c", [128, 32])
            din("hs1c", [128, 32]); din("hs2c", [128, 32]); din("s3c", [128, 8])
            dscr("w1_bf", [4096, D]); dscr("w2_bf", [4096, D])
            dscr("h1_bf", [4096, 4096]); dscr("h2_bf", [4096, 4096])
            dscr("w3_bf", [D, 4096])
        for n in ("s1_d", "xn2_d", "hin_d", "s2_d", "s3_d"):
            dscr(n, [KC * 128, T], F32)

        S = kb.S
        self.xs = [S(f"xs{k}", [128, T], F32) for k in range(KC)]
        NPOOL = 12
        self.pool = [S(f"pl{i}", [128, T], BF16) for i in range(NPOOL)]
        self.scr = [S(f"scr{i}", [128, T], F32) for i in range(4)]
        self.stg = [S(f"stg{i}", [128, max(2 * T, 4096)], BF16) for i in range(3)]
        self.vs2 = [S(f"vs{i}", [128, 2080 * T // 2048], BF16) for i in range(2)]
        self.cid = {}
        for n in ("ident_f", "ones_f", "ut_f"):
            self.cid[n] = S(n + "_s", [128, 128], F32)
        for n in ("ident_b", "ones_b", "R2"):
            self.cid[n] = S(n + "_s", [128, 128], BF16)
        self.sel_s = S("sel_s", [8, 1024], BF16)
        self.vecs = S("vecs", [128, 360], F32)
        self.vecmap = {}
        self.vec_used = 0
        self.rows = {n: S("row_" + n, [1, T], F32)
                     for n in ("mean", "rstd")}
        self.rr = {n: kb.buf("row_" + n) for n in self.rows}
        self.pb = [kb.PS(f"pb{i}", [128, 512], F32) for i in range(6)]
        self.pt = [kb.PS(f"pt{i}", [128, 1024], BF16) for i in range(2)]
        self.pbb = [kb.buf(f"pb{i}") for i in range(6)]
        self.ptb = [kb.buf(f"pt{i}") for i in range(2)]
        for i in range(4):
            kb.bufs[f"prepslot{i}"] = Buf(f"prepslot{i}")
        for e in range(8):
            kb.bufs[f"wgu_t_e{e}"] = Buf(f"wgu_t_e{e}")
        for i in range(4):
            kb.bufs[f"wdslot{i}"] = Buf(f"wdslot{i}")
        self.Ssf = S("Ssf", [128, 65], F32)
        self.Ssb = S("Ssb", [128, 65], BF16)
        self.small = [S(f"sm{i}", [128, 16], F32) for i in range(4)]
        self.pa = S("pa", [128, 256], BF16)
        if "b1" in bl:
            self.rt_s = S("rt_s", [128, 64], BF16)
            self.lg8 = S("lg8", [8, 512], F32)
            self.cfm = S("cfm", [8, T], BF16)
            self.ctm = S("ctm", [128, 128], F32)

    def vslot(self, name, ncols):
        if name not in self.vecmap:
            self.vecmap[name] = (self.vec_used, ncols)
            self.vec_used += ncols
            assert self.vec_used <= 360
        off, n = self.vecmap[name]
        return self.vecs[:, off:off + ncols]

    def B(self, name):
        return self.kb.buf(name)

    def load_vec(self, q, name, src_ap, ncols):
        ap = self.vslot(name, ncols)
        self.kb.dma(q, ap, src_ap, outs=(self.B("vecs"),))
        return ap

    def vec(self, name, col=None):
        off, n = self.vecmap[name]
        if col is None:
            return self.vecs[:, off:off + n]
        return self.vecs[:, off + col:off + col + 1]

    def ln_stats(self, tag, eps, rms=False, src=None, src_bufs=None, sq_eng="g"):
        kb, TT, T = self.kb, self.TT, self.T
        src = src or self.xs
        src_bufs = src_bufs or [self.B(f"xs{k}") for k in range(KC)]
        xc = self.vs2[0]
        xcB = self.B("vs0")
        sq = self.scr[3].bitcast(BF16)
        sqB = self.B("scr3")
        onesb = self.cid["ones_b"]
        onesbB = self.B("ones_b_s")
        per = 2 if TT >= 2 else 1
        nh = TT // per if TT >= 2 else 1
        for h in range(nh):
            for k in range(KC):
                c0 = (h * per) * 512
                w = per * 512
                sl = slice(c0, c0 + w)
                lsl = slice(0, w)
                kb.op("v", lambda e, k=k, sl=sl, lsl=lsl: e.tensor_copy(xc[:, lsl], src[k][:, sl]),
                      outs=(xcB,), ins=(src_bufs[k],))
                kb.op(sq_eng, lambda e, lsl=lsl: e.tensor_tensor(sq[:, lsl], xc[:, lsl], xc[:, lsl], OP.mult),
                      outs=(sqB,), ins=(xcB,))
                for j in range(per):
                    tt = h * per + j
                    if not rms:
                        kb.op("t", lambda e, k=k, j=j: e.matmul(
                            self.pb[j][0:1, :], onesb[:, 0:1], xc[:, ts(j, 512)],
                            start=(k == 0), stop=(k == KC - 1)),
                            outs=(self.pbb[j],), ins=(xcB, onesbB))
                    kb.op("t", lambda e, k=k, j=j: e.matmul(
                        self.pb[2 + j][0:1, :], onesb[:, 0:1], sq[:, ts(j, 512)],
                        start=(k == 0), stop=(k == KC - 1)),
                        outs=(self.pbb[2 + j],), ins=(sqB, onesbB))
            for j in range(per):
                tt = h * per + j
                R = self.rows
                RR = self.rr
                epscol = 0 if eps == 1e-5 else 1
                epsap = self.vec("cst", epscol)[0:1, :]
                bsum, bsq = self.pb[j], self.pb[2 + j]
                bsumB, bsqB = self.pbb[j], self.pbb[2 + j]
                if rms:
                    kb.op("v", lambda e, bsq=bsq: e.tensor_scalar(
                        bsq[0:1, :], bsq[0:1, :], 1.0 / D, None, OP.mult),
                        outs=(bsqB,), ins=(bsqB,))
                    kb.op("s", lambda e, bsq=bsq: e.activation(
                        bsq[0:1, :], bsq[0:1, :], AF.Ln, bias=epsap, scale=1.0),
                        outs=(bsqB,), ins=(bsqB, self.B("vecs")))
                    kb.op("s", lambda e, tt=tt, bsq=bsq: e.activation(
                        R["rstd"][:, ts(tt, 512)], bsq[0:1, :], AF.Exp, scale=-0.5),
                        outs=(RR["rstd"],), ins=(bsqB,))
                else:
                    kb.op("v", lambda e, tt=tt, bsum=bsum: e.tensor_scalar(
                        R["mean"][:, ts(tt, 512)], bsum[0:1, :], 1.0 / D, None, OP.mult),
                        outs=(RR["mean"],), ins=(bsumB,))
                    kb.op("v", lambda e, bsq=bsq: e.tensor_scalar(
                        bsq[0:1, :], bsq[0:1, :], 1.0 / D, None, OP.mult),
                        outs=(bsqB,), ins=(bsqB,))
                    m2 = self.scr[2]
                    m2B = self.B("scr2")
                    kb.op("v", lambda e, tt=tt, m2=m2: e.tensor_tensor(
                        m2[0:1, 0:512], R["mean"][:, ts(tt, 512)], R["mean"][:, ts(tt, 512)], OP.mult),
                        outs=(m2B,), ins=(RR["mean"],))
                    kb.op("v", lambda e, m2=m2, bsq=bsq: e.tensor_tensor(
                        bsq[0:1, :], bsq[0:1, :], m2[0:1, 0:512], OP.subtract),
                        outs=(bsqB,), ins=(bsqB, m2B))
                    kb.op("s", lambda e, bsq=bsq: e.activation(
                        bsq[0:1, :], bsq[0:1, :], AF.Ln, bias=epsap, scale=1.0),
                        outs=(bsqB,), ins=(bsqB, self.B("vecs")))
                    kb.op("s", lambda e, tt=tt, bsq=bsq: e.activation(
                        R["rstd"][:, ts(tt, 512)], bsq[0:1, :], AF.Exp, scale=-0.5),
                        outs=(RR["rstd"],), ins=(bsqB,))

    def bc_rows(self, tt, rows=("mean", "rstd"), banks=(4, 5)):
        kb = self.kb
        onesf = self.cid["ones_f"]
        onesfB = self.B("ones_f_s")
        out = []
        for rn, bi in zip(rows, banks):
            kb.op("t", lambda e, rn=rn, bi=bi: e.matmul(
                self.pb[bi][:, :], onesf[0:1, :], self.rows[rn][:, ts(tt, 512)],
                start=True, stop=True),
                outs=(self.pbb[bi],), ins=(self.rr[rn], onesfB))
            out.append(self.pb[bi])
        return out

    def emit_input(self):
        kb, T, TC = self.kb, self.T, self.TC
        d = self.dram
        for n in ("ident_f", "ones_f", "ut_f", "ident_b", "ones_b", "R2"):
            kb.dma("q", self.cid[n][:], d[n], outs=(self.B(n + "_s"),))
        kb.dma("q", self.sel_s[:], d["sel8"], outs=(self.B("sel_s"),))
        self.load_vec("q", "cst", d["constc"], 8)
        if self.io_in == "fm":
            for k in range(KC):
                kb.dma("q", self.xs[k][:], d["x_in_fm"][ts(k), :], outs=(self.B(f"xs{k}"),))
            return
        identf = self.cid["ident_f"]
        idB = self.B("ident_f_s")
        for tc in range(TC):
            stgv = self.stg[tc % 2].bitcast(F32)
            stgB = self.B(f"stg{tc % 2}")
            kb.dma("q", stgv[:, 0:D], d["x_in"][ts(tc), :], outs=(stgB,))
            for k in range(KC):
                b = 4 + (k % 2)
                kb.op("t", lambda e, k=k, b=b, stgv=stgv: e.transpose(
                    self.pb[b][:, 0:128], stgv[:, ts(k)], identf[:]),
                    outs=(self.pbb[b],), ins=(stgB, idB))
                kb.op("v", lambda e, k=k, tc=tc, b=b: e.tensor_copy(
                    self.xs[k][:, ts(tc)], self.pb[b][:, 0:128]),
                    outs=(self.B(f"xs{k}"),), ins=(self.pbb[b],))

    def emit_output(self):
        kb, T, TC = self.kb, self.T, self.TC
        d = self.dram
        if self.io_out == "fm":
            for k in range(KC):
                kb.dma("s", d["y_out_fm"][ts(k), :], self.xs[k][:],
                       outs=(self.B("y_out_fm"),), ins=(self.B(f"xs{k}"),))
            return
        identf = self.cid["ident_f"]
        idB = self.B("ident_f_s")
        for tc in range(TC):
            stgv = self.stg[tc % 2].bitcast(F32)
            stgB = self.B(f"stg{tc % 2}")
            for k in range(KC):
                b = 4 + (k % 2)
                kb.op("t", lambda e, k=k, tc=tc, b=b: e.transpose(
                    self.pb[b][:, 0:128], self.xs[k][:, ts(tc)], identf[:]),
                    outs=(self.pbb[b],), ins=(self.B(f"xs{k}"), idB))
                kb.op("v", lambda e, k=k, b=b, stgv=stgv: e.tensor_copy(
                    stgv[:, ts(k)], self.pb[b][:, 0:128]),
                    outs=(stgB,), ins=(self.pbb[b],))
            kb.dma("s", d["y_out"][ts(tc), :], stgv[:, 0:D],
                   outs=(self.B("y_out"),), ins=(stgB,))

    def emit_b0(self):
        kb, T, TC, TT = self.kb, self.T, self.TC, self.TT
        d = self.dram
        for i in range(3):
            kb.dma("g", d["wqkv_bf"][ts(i, D), :], d["w_qkv"][ts(i, D), :],
                   outs=(self.B("wqkv_bf"),))
        kb.dma("g", d["wout_bf"], d["w_out"], outs=(self.B("wout_bf"),))
        self.load_vec("q", "bqc", d["bqc"], 24)
        self.load_vec("q", "boc", d["boc"], 8)

        self.ln_stats("ln1", 1e-5)
        xn1 = self.pool[0:8]
        xn1B = [self.B(f"pl{i}") for i in range(8)]
        for tt in range(TT):
            self.bc_rows(tt)
            mb = self.scr[0]; mbB = self.B("scr0")
            rb = self.scr[1]; rbB = self.B("scr1")
            kb.op("v", lambda e, tt=tt: e.tensor_copy(mb[:, 0:512], self.pb[4][:, :]),
                  outs=(mbB,), ins=(self.pbb[4],))
            kb.op("v", lambda e, tt=tt: e.tensor_copy(rb[:, 0:512], self.pb[5][:, :]),
                  outs=(rbB,), ins=(self.pbb[5],))
            for k in range(KC):
                tmp = self.scr[2]
                tmpB = self.B("scr2")
                kb.op("g", lambda e, k=k, tt=tt: e.tensor_tensor(
                    tmp[:, ts(tt, 512)], self.xs[k][:, ts(tt, 512)], mb[:, 0:512], OP.subtract),
                    outs=(tmpB,), ins=(self.B(f"xs{k}"), mbB))
                kb.op("v", lambda e, k=k, tt=tt: e.tensor_tensor(
                    xn1[k][:, ts(tt, 512)], tmp[:, ts(tt, 512)], rb[:, 0:512], OP.mult),
                    outs=(xn1B[k],), ins=(tmpB, rbB))

        qp, kp, ktm, vfm = self.pool[8], self.pool[9], self.pool[10], self.pool[11]
        qpB, kpB, ktmB, vfmB = (self.B("pl8"), self.B("pl9"), self.B("pl10"), self.B("pl11"))
        atm = self.stg[2]
        atmB = self.B("stg2")
        identb = self.cid["ident_b"]
        idbB = self.B("ident_b_s")
        utf = self.cid["ut_f"]
        utfB = self.B("ut_f_s")
        SsfB, SsbB = self.B("Ssf"), self.B("Ssb")
        paB = self.B("pa")
        vsB = [self.B(f"vs{i}") for i in range(2)]

        for hp in range(8):
            stgw = self.stg[hp % 2]
            stgwB = self.B(f"stg{hp % 2}")
            for part in range(3):
                base = part * D + hp * 128
                for kc2 in range(KC):
                    col = part * D + kc2 * 128
                    kb.dmat("q", stgw[:, col:col + 128],
                            d["wqkv_bf"][base:base + 128, ts(kc2)],
                            outs=(stgwB,), ins=(self.B("wqkv_bf"),))
            for part, dst, dstB in ((0, qp, qpB), (1, kp, kpB), (2, vfm, vfmB)):
                bias = self.vec("bqc", part * 8 + hp)
                for tt in range(TT):
                    bank = tt % 4
                    emits = []
                    ins_per = []
                    for kc2 in range(KC):
                        emits.append(lambda e, kc2=kc2, tt=tt, bank=bank, part=part, stgw=stgw: e.matmul(
                            self.pb[bank][:, :],
                            stgw[:, slice(part * D + kc2 * 128, part * D + kc2 * 128 + 128)],
                            xn1[kc2][:, ts(tt, 512)],
                            start=(kc2 == 0), stop=(kc2 == KC - 1)))
                        ins_per.append((stgwB, xn1B[kc2]))
                    kb.group("t", emits, outs=(self.pbb[bank],), ins_per=ins_per)
                    if part < 2:
                        sc12 = self.stg[2].bitcast(F32)
                        sc12B = self.B("stg2")
                        co = self.T // 2
                        ee = sc12[:, co:co + 512]
                        rr2 = sc12[:, co + 512:co + 1024]
                        kb.op("s", lambda e, bank=bank, bias=bias, ee=ee: e.activation(
                            ee, self.pb[bank][:, :], AF.Exp, bias=bias, scale=1.0),
                            outs=(sc12B,), ins=(self.pbb[bank], self.B("vecs")))
                        kb.op("s", lambda e, bank=bank, bias=bias, rr2=rr2: e.activation(
                            rr2, self.pb[bank][:, :], AF.Relu, bias=bias, scale=1.0),
                            outs=(sc12B,), ins=(self.pbb[bank], self.B("vecs")))
                        kb.op("v", lambda e, tt=tt, dst=dst, ee=ee, rr2=rr2: e.scalar_tensor_tensor(
                            dst[:, ts(tt, 512)], ee, 1.0, rr2,
                            OP.min, OP.add),
                            outs=(dstB,), ins=(sc12B,))
                    else:
                        kb.op("v", lambda e, tt=tt, bank=bank, dst=dst, bias=bias: e.tensor_scalar(
                            dst[:, ts(tt, 512)], self.pb[bank][:, :], bias, None, OP.add),
                            outs=(dstB,), ins=(self.pbb[bank], self.B("vecs")))
            vs = self.vs2[hp % 2]
            vsb = vsB[hp % 2]
            kb.op("v", lambda e, vs=vs: e.memset(vs[:], 1.0), outs=(vsb,))
            for tc in range(TC):
                pt = self.pt[tc % 2]
                ptB = self.ptb[tc % 2]
                kb.op("t", lambda e, tc=tc, pt=pt: e.transpose(
                    pt[:, 0:128], kp[:, ts(tc)], identb[:]),
                    outs=(ptB,), ins=(kpB, idbB))
                kb.op("v", lambda e, tc=tc, pt=pt: e.tensor_copy(ktm[:, ts(tc)], pt[:, 0:128]),
                      outs=(ktmB,), ins=(ptB,))
                kb.op("t", lambda e, tc=tc, pt=pt: e.transpose(
                    pt[:, 128:256], vfm[:, ts(tc)], identb[:]),
                    outs=(ptB,), ins=(vfmB, idbB))
                kb.op("v", lambda e, tc=tc, pt=pt, vs=vs: e.tensor_copy(
                    vs[:, 130 * tc:130 * tc + 64], pt[:, 128:192]),
                    outs=(vsb,), ins=(ptB,))
                kb.op("v", lambda e, tc=tc, pt=pt, vs=vs: e.tensor_copy(
                    vs[:, 130 * tc + 65:130 * tc + 129], pt[:, 192:256]),
                    outs=(vsb,), ins=(ptB,))
            kb.op("v", lambda e: e.memset(self.Ssf[:], 0.0), outs=(SsfB,))
            kb.op("v", lambda e: e.memset(self.Ssb[:], 0.0), outs=(SsbB,))
            for tc in range(TC):
                for hl in range(2):
                    hs = slice(64 * hl, 64 * hl + 64)
                    bA = 0 if hl == 0 else 1
                    bO = 2 if hl == 0 else 3
                    kb.op("t", lambda e, tc=tc, hs=hs, bA=bA: e.matmul(
                        self.pb[bA][:, 0:128], kp[hs, ts(tc)], qp[hs, ts(tc)],
                        start=True, stop=True),
                        outs=(self.pbb[bA],), ins=(kpB, qpB))
                    pasl = slice(128 * hl, 128 * hl + 128)
                    kb.op("v", lambda e, bA=bA, pasl=pasl: e.tensor_tensor(
                        self.pa[:, pasl], self.pb[bA][:, 0:128], utf[:], OP.mult),
                        outs=(paB,), ins=(self.pbb[bA], utfB))
                    e1 = lambda e, tc=tc, hl=hl, bO=bO, pasl=pasl, vs=vs: e.matmul(
                        self.pb[bO][:, 0:65], self.pa[:, pasl],
                        vs[:, 130 * tc + 65 * hl: 130 * tc + 65 * hl + 65],
                        start=True, stop=False)
                    e2 = lambda e, tc=tc, hs=hs, bO=bO: e.matmul(
                        self.pb[bO][:, 0:65], qp[hs, ts(tc)], self.Ssb[hs, :],
                        start=False, stop=True)
                    kb.group("t", [e1, e2], outs=(self.pbb[bO],),
                             ins_per=[(paB, vsb), (qpB, SsbB)])
                    bS = 4 + hl
                    kb.op("t", lambda e, tc=tc, hl=hl, hs=hs, bS=bS, vs=vs: e.matmul(
                        self.pb[bS][hs, 0:65],
                        ktm[:, 128 * tc + 64 * hl:128 * tc + 64 * hl + 64],
                        vs[:, 130 * tc + 65 * hl: 130 * tc + 65 * hl + 65],
                        start=True, stop=True),
                        outs=(self.pbb[bS],), ins=(ktmB, vsb))
                    kb.op("v", lambda e, hs=hs, bS=bS: e.tensor_tensor(
                        self.Ssf[hs, :], self.Ssf[hs, :], self.pb[bS][hs, 0:65], OP.add),
                        outs=(SsfB,), ins=(SsfB, self.pbb[bS]))
                    kb.op("v", lambda e, hs=hs: e.tensor_copy(self.Ssb[hs, :], self.Ssf[hs, :]),
                          outs=(SsbB,), ins=(SsfB,))
                    nrm = self.small[0]
                    nrmB = self.B("sm0")
                    kb.op("v", lambda e, bO=bO: e.tensor_scalar(
                        nrm[:, 0:1], self.pb[bO][:, 64:65], 1e-6, None, OP.add),
                        outs=(nrmB,), ins=(self.pbb[bO],))
                    kb.op("v", lambda e: e.reciprocal(nrm[:, 1:2], nrm[:, 0:1]),
                          outs=(nrmB,), ins=(nrmB,))
                    kb.op("v", lambda e, tc=tc, hl=hl, bO=bO: e.tensor_scalar(
                        atm[:, 128 * tc + 64 * hl:128 * tc + 64 * hl + 64],
                        self.pb[bO][:, 0:64], nrm[:, 1:2], None, OP.mult),
                        outs=(atmB,), ins=(self.pbb[bO], nrmB))
            for tc in range(TC):
                pt = self.pt[tc % 2]
                ptB = self.ptb[tc % 2]
                kb.op("t", lambda e, tc=tc, pt=pt: e.transpose(
                    pt[:, 0:128], atm[:, ts(tc)], identb[:]),
                    outs=(ptB,), ins=(atmB, idbB))
                sc = self.scr[hp // 2].bitcast(BF16)
                scB = self.B(f"scr{hp // 2}")
                kb.op("v", lambda e, tc=tc, pt=pt, sc=sc, hp=hp: e.tensor_copy(
                    sc[:, (hp % 2) * T + 128 * tc:(hp % 2) * T + 128 * tc + 128], pt[:, 0:128]),
                    outs=(scB,), ins=(ptB,))

        for kc2 in range(KC):
            dst = self.stg[kc2 // 4]
            dstB = self.B(f"stg{kc2 // 4}")
            kb.dmat("q", dst[:, (kc2 % 4) * D:(kc2 % 4) * D + D],
                    d["wout_bf"][:, ts(kc2)], outs=(dstB,), ins=(self.B("wout_bf"),))
        for oc in range(KC):
            for tt in range(TT):
                bank = (oc * TT + tt) % 4
                emits, ins_per = [], []
                for kc2 in range(KC):
                    wv = self.stg[kc2 // 4]
                    wvB = self.B(f"stg{kc2 // 4}")
                    afm = self.scr[kc2 // 2].bitcast(BF16)
                    afmB = self.B(f"scr{kc2 // 2}")
                    emits.append(lambda e, kc2=kc2, oc=oc, tt=tt, bank=bank, wv=wv, afm=afm: e.matmul(
                        self.pb[bank][:, :],
                        wv[:, (kc2 % 4) * D + oc * 128:(kc2 % 4) * D + oc * 128 + 128],
                        afm[:, (kc2 % 2) * T + tt * 512:(kc2 % 2) * T + tt * 512 + 512],
                        start=(kc2 == 0), stop=(kc2 == KC - 1)))
                    ins_per.append((wvB, afmB))
                kb.group("t", emits, outs=(self.pbb[bank],), ins_per=ins_per)
                kb.op("v", lambda e, oc=oc, tt=tt, bank=bank: e.scalar_tensor_tensor(
                    self.xs[oc][:, ts(tt, 512)], self.pb[bank][:, :],
                    self.vec("boc", oc), self.xs[oc][:, ts(tt, 512)], OP.add, OP.add),
                    outs=(self.B(f"xs{oc}"),),
                    ins=(self.pbb[bank], self.B("vecs"), self.B(f"xs{oc}")))


    def prep_wgu_init(self):
        kb, T = self.kb, self.T
        d = self.dram
        wg_r = d["wg"].rearrange("x (c p) h -> p (x c) h", c=8)
        wu_r = d["wu"].rearrange("x (c p) h -> p (x c) h", c=8)
        wgt = d["wgu_t"]
        pslots = []
        for i in range(4):
            t = self.pool[9 + i // 2]
            ap = t[:, (i % 2) * 1024:(i % 2) * 1024 + 1024]
            pslots.append((ap, kb.buf(f"prepslot{i}"), self.B(f"pl{9 + i // 2}")))

        def gen():
            si = 0
            for e8 in range(8):
                for hz in range(22):
                    w = 128 if hz < 21 else 64
                    for gi, wr in ((0, wg_r), (1, wu_r)):
                        ap, sb, rb = pslots[si % 4]
                        si += 1
                        kb.dma("g", ap[:, 0:8 * w].rearrange("p (c h) -> p c h", c=8),
                               wr[:, 8 * e8:8 * e8 + 8, 128 * hz:128 * hz + w],
                               outs=(sb, rb), ins=())
                        row = ((e8 * 22 + hz) * 2 + gi) * 128
                        kb.dma("s", wgt[row:row + 128, 0:8 * w], ap[:, 0:8 * w],
                               outs=(kb.buf(f"wgu_t_e{e8}"),), ins=(sb, rb))
                        yield
        self._prep_gen = gen()

    def prep_pump(self, n):
        g = getattr(self, "_prep_gen", None)
        if g is None:
            return
        for _ in range(n):
            try:
                next(g)
            except StopIteration:
                self._prep_gen = None
                return

    def rms_rows_from_banks(self):
        kb = self.kb
        epsap = self.vec("cst", 1)[0:1, :]
        for tt in range(self.TT):
            b = self.pb[2 + tt]
            bB = self.pbb[2 + tt]
            kb.op("v", lambda e, b=b: e.tensor_scalar(b[0:1, :], b[0:1, :], 1.0 / D, None, OP.mult),
                  outs=(bB,), ins=(bB,))
            kb.op("s", lambda e, b=b: e.activation(b[0:1, :], b[0:1, :], AF.Ln, bias=epsap, scale=1.0),
                  outs=(bB,), ins=(bB, self.B("vecs")))
            kb.op("s", lambda e, tt=tt, b=b: e.activation(
                self.rows["rstd"][:, ts(tt, 512)], b[0:1, :], AF.Exp, scale=-0.5),
                outs=(self.rr["rstd"],), ins=(bB,))

    def emit_router(self, b2, poolB):
        kb, T, TC, TT = self.kb, self.T, self.TC, self.TT
        d = self.dram
        PB, PBB = self.pb, self.pbb
        smB = [self.B(f"sm{i}") for i in range(4)]
        rtB, lgB, ctmB, cfmB = self.B("rt_s"), self.B("lg8"), self.B("ctm"), self.B("cfm")
        sm0, sm1, sm2, sm3 = self.small
        identf = self.cid["ident_f"]
        idB = self.B("ident_f_s")
        for kc2 in range(KC):
            kb.dma("g", self.rt_s[:, 8 * kc2:8 * kc2 + 8], d["w_router"][ts(kc2), :], outs=(rtB,))
        for tt in range(TT):
            emits, ins_per = [], []
            for kc2 in range(KC):
                emits.append(lambda e, kc2=kc2, tt=tt: e.matmul(
                    PB[0][0:8, :], self.rt_s[:, 8 * kc2:8 * kc2 + 8], b2[kc2][:, ts(tt, 512)],
                    start=(kc2 == 0), stop=(kc2 == KC - 1)))
                ins_per.append((rtB, poolB[kc2]))
            kb.group("t", emits, outs=(PBB[0],), ins_per=ins_per)
            kb.op("v", lambda e: e.tensor_copy(self.lg8[:, :], PB[0][0:8, :]),
                  outs=(lgB,), ins=(PBB[0],))
            for tcl in range(4 if TT > 0 else 0):
                tc = tt * 4 + tcl
                if tc >= TC:
                    break
                kb.op("t", lambda e, tcl=tcl: e.transpose(
                    PB[1][:, 0:8], self.lg8[:, ts(tcl)], identf[0:8, 0:8]),
                    outs=(PBB[1],), ins=(lgB, idB))
                kb.op("v", lambda e: e.reduce_max(sm0[:, 0:1], PB[1][:, 0:8], axis=AX, negate=True),
                      outs=(smB[0],), ins=(PBB[1],))
                kb.op("s", lambda e: e.activation(sm1[:, 0:8], PB[1][:, 0:8], AF.Exp,
                                                  bias=sm0[:, 0:1], scale=1.0),
                      outs=(smB[1],), ins=(PBB[1], smB[0]))
                kb.op("v", lambda e: e.reduce_sum(sm0[:, 1:2], sm1[:, 0:8], axis=AX),
                      outs=(smB[0],), ins=(smB[1],))
                kb.op("v", lambda e: e.reciprocal(sm0[:, 2:3], sm0[:, 1:2]),
                      outs=(smB[0],), ins=(smB[0],))
                kb.op("v", lambda e: e.tensor_scalar(sm1[:, 8:16], sm1[:, 0:8], sm0[:, 2:3], None, OP.mult),
                      outs=(smB[1],), ins=(smB[1], smB[0]))
                kb.op("v", lambda e: e.reduce_max(sm0[:, 3:4], sm1[:, 8:16], axis=AX),
                      outs=(smB[0],), ins=(smB[1],))
                kb.op("v", lambda e: e.tensor_scalar(sm2[:, 0:8], sm1[:, 8:16], sm0[:, 3:4], None, OP.is_ge),
                      outs=(smB[2],), ins=(smB[1], smB[0]))
                kb.op("v", lambda e: e.scalar_tensor_tensor(sm2[:, 8:16], sm2[:, 0:8], -1e30,
                                                            sm1[:, 8:16], OP.mult, OP.add),
                      outs=(smB[2],), ins=(smB[2], smB[1]))
                kb.op("v", lambda e: e.reduce_max(sm0[:, 4:5], sm2[:, 8:16], axis=AX),
                      outs=(smB[0],), ins=(smB[2],))
                kb.op("v", lambda e: e.tensor_scalar(sm2[:, 0:8], sm1[:, 8:16], sm0[:, 4:5], None, OP.is_ge),
                      outs=(smB[2],), ins=(smB[1], smB[0]))
                kb.op("v", lambda e: e.tensor_tensor(sm0[:, 5:6], sm0[:, 3:4], sm0[:, 4:5], OP.add),
                      outs=(smB[0],), ins=(smB[0],))
                kb.op("v", lambda e: e.reciprocal(sm0[:, 6:7], sm0[:, 5:6]),
                      outs=(smB[0],), ins=(smB[0],))
                kb.op("v", lambda e: e.tensor_tensor(sm3[:, 0:8], sm1[:, 8:16], sm2[:, 0:8], OP.mult),
                      outs=(smB[3],), ins=(smB[1], smB[2]))
                kb.op("v", lambda e, tc=tc: e.tensor_scalar(
                    self.ctm[:, 8 * tc:8 * tc + 8], sm3[:, 0:8], sm0[:, 6:7], None, OP.mult),
                    outs=(ctmB,), ins=(smB[3], smB[0]))
        for tc in range(TC):
            kb.op("t", lambda e, tc=tc: e.transpose(
                PB[1][0:8, 0:128], self.ctm[:, 8 * tc:8 * tc + 8], identf[:]),
                outs=(PBB[1],), ins=(ctmB, idB))
            kb.op("v", lambda e, tc=tc: e.tensor_copy(
                self.cfm[0:8, ts(tc)], PB[1][0:8, 0:128]),
                outs=(cfmB,), ins=(PBB[1],))

    def emit_moe(self, b2, poolB):
        kb, T, TC, TT = self.kb, self.T, self.TC, self.TT
        assert T == 2048, "MoE gu layout assumes T=2048"
        d = self.dram
        PB, PBB = self.pb, self.pbb
        xsB = [self.B(f"xs{k}") for k in range(KC)]
        scrB = [self.B(f"scr{i}") for i in range(4)]
        stgB = [self.B(f"stg{i}") for i in range(3)]
        vsB = [self.B("vs0"), self.B("vs1")]
        cfmB = self.B("cfm")
        selB = self.B("sel_s")
        wd_r = d["wd_bf"].rearrange("x h o -> (x h) o")
        wgt = d["wgu_t"]
        wgtB = self.B("wgu_t")
        NHZ = 22

        def hzw(hz):
            return 128 if hz < 21 else 64

        def gu_tile(t):
            if t < 16:
                xb = self.xs[t // 2].bitcast(BF16)
                c0 = T + (t % 2) * 1024
                return xb[:, c0:c0 + 1024], xsB[t // 2]
            u = t - 16
            return self.pool[8 + u // 2][:, (u % 2) * 1024:(u % 2) * 1024 + 1024], self.B(f"pl{8 + u // 2}")

        for half in range(2):
            h0 = half * 1024
            for e8 in range(8):
                for tt in range(2):
                    kb.op("t", lambda e, e8=e8, tt=tt, h0=h0: e.matmul(
                        PB[0][:, :], self.sel_s[0:8, ts(e8)], self.cfm[0:8, h0 + tt * 512:h0 + tt * 512 + 512],
                        start=True, stop=True),
                        outs=(PBB[0],), ins=(selB, cfmB))
                    kb.op("v", lambda e, tt=tt: e.tensor_copy(
                        self.vs2[1][:, ts(tt, 512)], PB[0][:, :]),
                        outs=(vsB[1],), ins=(PBB[0],))
                for hz in range(NHZ):
                    w = hzw(hz)
                    stg = self.stg[hz % 2]
                    sB = stgB[hz % 2]
                    row = (e8 * 22 + hz) * 2 * 128
                    eb = kb.buf(f"wgu_t_e{e8}")
                    kb.dma("q", stg[:, 0:8 * w], wgt[row:row + 128, 0:8 * w],
                           outs=(sB,), ins=(eb,))
                    kb.dma("s", stg[:, 1024:1024 + 8 * w], wgt[row + 128:row + 256, 0:8 * w],
                           outs=(sB,), ins=(eb,))
                    gt, gtB = gu_tile(hz)
                    for tt in range(2):
                        bk = 0 if (hz * 2 + tt) % 2 == 0 else 2
                        for which, bank, cb in ((0, bk, 0), (1, bk + 1, 1024)):
                            emits, ins_per = [], []
                            for kc2 in range(KC):
                                emits.append(lambda e, kc2=kc2, tt=tt, bank=bank, cb=cb, stg=stg, w=w, h0=h0: e.matmul(
                                    PB[bank][0:w, :], stg[:, cb + w * kc2:cb + w * kc2 + w],
                                    b2[kc2][:, h0 + tt * 512:h0 + tt * 512 + 512],
                                    start=(kc2 == 0), stop=(kc2 == KC - 1)))
                                ins_per.append((sB, poolB[kc2]))
                            kb.group("t", emits, outs=(PBB[bank],), ins_per=ins_per)
                        sg = self.vs2[0]
                        sgsl = slice(0, 512) if bk == 0 else slice(2080 - 512, 2080)
                        kb.op("s", lambda e, sg=sg, w=w, bk=bk, sgsl=sgsl: e.activation(
                            sg[0:w, sgsl], PB[bk][0:w, :], AF.Silu),
                            outs=(vsB[0],), ins=(PBB[bk],))
                        kb.op("v", lambda e, tt=tt, gt=gt, sg=sg, w=w, bk=bk, sgsl=sgsl: e.tensor_tensor(
                            gt[0:w, ts(tt, 512)], sg[0:w, sgsl], PB[bk + 1][0:w, :], OP.mult),
                            outs=(gtB,), ins=(vsB[0], PBB[bk + 1]))
                first_e = (e8 == 0)
                wdslots = [kb.buf(f"wdslot{i}") for i in range(4)]
                kb.split_buf(stgB[2], wdslots)
                for ocg in range(2):
                    for tt in range(2):
                        for hz in range(NHZ):
                            w = hzw(hz)
                            wdt = self.stg[2]
                            wdtB = wdslots[ocg * 2 + hz % 2]
                            po = 512 * (hz % 2)
                            kb.dma("q" if hz % 2 == 0 else "s",
                                   wdt[0:w, po + 1024 * ocg:po + 1024 * ocg + 512],
                                   wd_r[2752 * e8 + 128 * hz:2752 * e8 + 128 * hz + w,
                                        512 * ocg:512 * ocg + 512],
                                   outs=(wdtB,), ins=(self.B("wd_bf"),))
                            gt, gtB = gu_tile(hz)
                            for oci in range(4):
                                kb.op("t", lambda e, hz=hz, oci=oci, tt=tt, po=po, ocg=ocg, gt=gt, wdt=wdt, w=w: e.matmul(
                                    PB[2 + oci][:, :],
                                    wdt[0:w, po + 1024 * ocg + 128 * oci:po + 1024 * ocg + 128 * oci + 128],
                                    gt[0:w, ts(tt, 512)],
                                    start=(hz == 0), stop=(hz == NHZ - 1)),
                                    outs=(PBB[2 + oci],), ins=(wdtB, gtB))
                        t1 = self.vs2[0].bitcast(F32)
                        for oci in range(4):
                            oc = ocg * 4 + oci
                            ysl = self.scr[oc // 2][:, (oc % 2) * 1024 + 512 * tt:(oc % 2) * 1024 + 512 * tt + 512]
                            kb.op("v", lambda e, oci=oci, tt=tt, t1=t1: e.tensor_tensor(
                                t1[:, 256:768], PB[2 + oci][:, :], self.vs2[1][:, ts(tt, 512)], OP.mult),
                                outs=(vsB[0],), ins=(PBB[2 + oci], vsB[1]))
                            if first_e:
                                kb.op("v", lambda e, ysl=ysl, t1=t1: e.tensor_copy(ysl, t1[:, 256:768]),
                                      outs=(scrB[oc // 2],), ins=(vsB[0],))
                            else:
                                kb.op("v", lambda e, ysl=ysl, t1=t1: e.tensor_tensor(ysl, ysl, t1[:, 256:768], OP.add),
                                      outs=(scrB[oc // 2],),
                                      ins=(scrB[oc // 2], vsB[0]))
                kb.merge_buf(stgB[2], wdslots)
            p8 = self.pool[8].bitcast(F32)
            p9 = self.pool[9].bitcast(F32)
            p10 = self.pool[10].bitcast(F32)
            for k in range(KC):
                kb.dma("q", p8[:, 0:1024], d["s1_d"][ts(k), h0:h0 + 1024],
                       outs=(self.B("pl8"),), ins=(self.B("s1_d"),))
                kb.dma("q", p9[:, 0:1024], d["hin_d"][ts(k), h0:h0 + 1024],
                       outs=(self.B("pl9"),), ins=(self.B("hin_d"),))
                kb.op("v", lambda e: e.tensor_tensor(p10[:, 0:1024], p8[:, 0:1024], p9[:, 0:1024], OP.add),
                      outs=(self.B("pl10"),), ins=(self.B("pl8"), self.B("pl9")))
                ysl = self.scr[k // 2][:, (k % 2) * 1024:(k % 2) * 1024 + 1024]
                kb.op("v", lambda e, k=k, ysl=ysl, h0=h0: e.tensor_tensor(
                    self.xs[k][:, h0:h0 + 1024], p10[:, 0:1024], ysl, OP.add),
                    outs=(xsB[k],), ins=(self.B("pl10"), scrB[k // 2]))

    def emit_b1(self):
        kb, T, TC, TT = self.kb, self.T, self.TC, self.TT
        d = self.dram
        for wn in ("wq1", "wk1", "wv1", "wo1"):
            kb.dma("g", d[wn + "_bf"], d[wn], outs=(self.B(wn + "_bf"),))
        kb.dma("g", d["wd_bf"].rearrange("x h o -> (x h) o"),
               d["wd"].rearrange("x h o -> (x h) o"), outs=(self.B("wd_bf"),))
        self.prep_wgu_init()
        self.load_vec("q", "g1c", d["g1c"], 8)
        self.load_vec("q", "g2c", d["g2c"], 8)
        xsB = [self.B(f"xs{k}") for k in range(KC)]
        scrB = [self.B(f"scr{i}") for i in range(4)]
        stgB = [self.B(f"stg{i}") for i in range(3)]
        poolB = [self.B(f"pl{i}") for i in range(12)]
        vsB = [self.B("vs0"), self.B("vs1")]
        onesfB = self.B("ones_f_s")
        onesbB = self.B("ones_b_s")
        smB = [self.B(f"sm{i}") for i in range(4)]

        self.ln_stats("ln2", 1e-5, sq_eng="v")
        for k in range(KC):
            kb.dma("s", d["s1_d"][ts(k), :], self.xs[k][:], outs=(self.B("s1_d"),), ins=(xsB[k],))
        b1 = self.pool[0:8]
        for tt in range(TT):
            self.bc_rows(tt)
            mb, rb = self.scr[0], self.scr[1]
            kb.op("v", lambda e: e.tensor_copy(mb[:, 0:512], self.pb[4][:, :]),
                  outs=(scrB[0],), ins=(self.pbb[4],))
            kb.op("v", lambda e: e.tensor_copy(rb[:, 0:512], self.pb[5][:, :]),
                  outs=(scrB[1],), ins=(self.pbb[5],))
            for k in range(KC):
                tmp = self.scr[2]
                kb.op("v", lambda e, k=k, tt=tt: e.tensor_tensor(
                    tmp[:, ts(tt, 512)], self.xs[k][:, ts(tt, 512)], mb[:, 0:512], OP.subtract),
                    outs=(scrB[2],), ins=(xsB[k], scrB[0]))
                kb.op("v", lambda e, k=k, tt=tt: e.tensor_tensor(
                    self.xs[k][:, ts(tt, 512)], tmp[:, ts(tt, 512)], rb[:, 0:512], OP.mult),
                    outs=(xsB[k],), ins=(scrB[2], scrB[1]))
        for k in range(KC):
            kb.dma("s", d["xn2_d"][ts(k), :], self.xs[k][:], outs=(self.B("xn2_d"),), ins=(xsB[k],))
            kb.op("v", lambda e, k=k: e.tensor_scalar(
                b1[k][:], self.xs[k][:], self.vec("g1c", k), None, OP.mult),
                outs=(poolB[k],), ins=(xsB[k], self.B("vecs")))
        self.ln_stats("rms1", 1e-6, rms=True, src=self.xs, src_bufs=xsB, sq_eng="v")

        qx = [self.xs[h].bitcast(BF16) for h in range(KC)]
        PB, PBB = self.pb, self.pbb

        def load_strips(wname):
            for kc2 in range(KC):
                dst = self.stg[kc2 // 4]
                kb.dma("q", dst[:, (kc2 % 4) * D:(kc2 % 4) * D + D], d[wname + "_bf"][ts(kc2), :],
                       outs=(stgB[kc2 // 4],), ins=(self.B(wname + "_bf"),))

        def gemm_1024(dst_fn, rhs_list, rhs_bufs, epi, n_oc=8, banks=(0, 1, 2, 3)):
            for oc in range(n_oc):
                for tt in range(TT):
                    bank = banks[(oc * TT + tt) % len(banks)]
                    emits, ins_per = [], []
                    for kc2 in range(KC):
                        wv = self.stg[kc2 // 4]
                        emits.append(lambda e, kc2=kc2, oc=oc, tt=tt, bank=bank, wv=wv: e.matmul(
                            PB[bank][:, :],
                            wv[:, (kc2 % 4) * D + oc * 128:(kc2 % 4) * D + oc * 128 + 128],
                            rhs_list[kc2][:, ts(tt, 512)],
                            start=(kc2 == 0), stop=(kc2 == KC - 1)))
                        ins_per.append((stgB[kc2 // 4], rhs_bufs[kc2]))
                    kb.group("t", emits, outs=(PBB[bank],), ins_per=ins_per)
                    epi(oc, tt, bank)

        b1B = poolB[0:8]
        for proj, wname, colbase in (("q", "wq1", 0), ("k", "wk1", T)):
            load_strips(wname)
            self.prep_pump(16)

            def epi(oc, tt, bank, colbase=colbase):
                kb.op("v", lambda e, oc=oc, tt=tt, bank=bank: e.tensor_copy(
                    qx[oc][:, colbase + tt * 512: colbase + tt * 512 + 512], PB[bank][:, :]),
                    outs=(xsB[oc],), ins=(PBB[bank],))
                self.prep_pump(1)
            gemm_1024(None, b1, b1B, epi)

        cqs, sqs = self.scr[0], self.scr[1]
        for tt in range(TT):
            self.bc_rows(tt, rows=("rstd",), banks=(4,))
            ct = self.scr[2]
            kb.dma("q", ct[:, 0:512], d["cos_t"][:, ts(tt, 512)], outs=(scrB[2],))
            kb.dma("q", ct[:, 512:1024], d["sin_t"][:, ts(tt, 512)], outs=(scrB[2],))
            kb.op("v", lambda e, tt=tt: e.tensor_tensor(
                cqs[:, ts(tt, 512)], ct[:, 0:512], self.pb[4][:, :], OP.mult),
                outs=(scrB[0],), ins=(scrB[2], self.pbb[4]))
            kb.op("v", lambda e, tt=tt: e.tensor_tensor(
                sqs[:, ts(tt, 512)], ct[:, 512:1024], self.pb[4][:, :], OP.mult),
                outs=(scrB[1],), ins=(scrB[2], self.pbb[4]))
        R2 = self.cid["R2"]
        R2B = self.B("R2_s")
        kb.op("v", lambda e: e.memset(self.small[1][:], 0.0), outs=(smB[1],))
        for colbase in (0, T):
            for hp in range(KC):
                self.prep_pump(2)
                for tt in range(TT):
                    sl = slice(colbase + tt * 512, colbase + tt * 512 + 512)
                    bank = tt % 2
                    kb.op("t", lambda e, hp=hp, sl=sl, bank=bank: e.matmul(
                        PB[bank][:, :], R2[:], qx[hp][:, sl], start=True, stop=True),
                        outs=(PBB[bank],), ins=(xsB[hp], R2B))
                    t1 = self.scr[3]
                    kb.op("v", lambda e, hp=hp, sl=sl, tt=tt: e.tensor_tensor(
                        t1[:, 0:512], qx[hp][:, sl], cqs[:, ts(tt, 512)], OP.mult),
                        outs=(scrB[3],), ins=(xsB[hp], scrB[0]))
                    kb.op("v", lambda e, tt=tt, bank=bank: e.tensor_tensor(
                        t1[:, 512:1024], PB[bank][:, :], sqs[:, ts(tt, 512)], OP.mult),
                        outs=(scrB[3],), ins=(PBB[bank], scrB[1]))
                    kb.op("v", lambda e, hp=hp, sl=sl: e.tensor_tensor(
                        qx[hp][:, sl], t1[:, 0:512], t1[:, 512:1024], OP.add),
                        outs=(xsB[hp],), ins=(scrB[3],))
                    sqt = self.vs2[1]
                    kb.op("v", lambda e, hp=hp, sl=sl: e.tensor_tensor(
                        sqt[:, 0:512], qx[hp][:, sl], qx[hp][:, sl], OP.mult),
                        outs=(vsB[1],), ins=(xsB[hp],))
                    kb.op("t", lambda e: e.matmul(
                        PB[5][0:1, 0:512], self.cid["ones_b"][:, 0:1], sqt[:, 0:512],
                        start=True, stop=True),
                        outs=(PBB[5],), ins=(vsB[1], onesbB))
                    gi = 0 if colbase == 0 else 1
                    kb.op("v", lambda e: e.reduce_max(
                        self.small[1][0:1, 4:5], PB[5][0:1, 0:512], axis=AX),
                        outs=(smB[1],), ins=(PBB[5], smB[1]))
                    kb.op("v", lambda e, gi=gi: e.tensor_tensor(
                        self.small[1][0:1, gi:gi + 1], self.small[1][0:1, gi:gi + 1],
                        self.small[1][0:1, 4:5], OP.max),
                        outs=(smB[1],), ins=(smB[1],))
        kb.op("s", lambda e: e.activation(self.small[1][0:1, 2:4], self.small[1][0:1, 0:2], AF.Sqrt),
              outs=(smB[1],), ins=(smB[1],))
        kb.op("v", lambda e: e.tensor_tensor(
            self.small[1][0:1, 4:5], self.small[1][0:1, 2:3], self.small[1][0:1, 3:4], OP.mult),
            outs=(smB[1],), ins=(smB[1],))
        kb.op("v", lambda e: e.tensor_scalar(
            self.small[1][0:1, 5:6], self.small[1][0:1, 4:5], -0.125, None, OP.mult),
            outs=(smB[1],), ins=(smB[1],))
        kb.op("t", lambda e: e.matmul(
            PB[5][:, 0:1], self.cid["ones_f"][0:1, :], self.small[1][0:1, 5:6],
            start=True, stop=True),
            outs=(PBB[5],), ins=(smB[1], onesfB))
        kb.op("v", lambda e: e.tensor_copy(self.small[2][:, 0:1], PB[5][:, 0:1]),
              outs=(smB[2],), ins=(PBB[5],))
        negm = self.small[2][:, 0:1]

        for tt in range(TT):
            self.bc_rows(tt, rows=("rstd",), banks=(4,))
            kb.op("v", lambda e, tt=tt: e.tensor_copy(self.scr[0][:, ts(tt, 512)], self.pb[4][:, :]),
                  outs=(scrB[0],), ins=(self.pbb[4],))

        load_strips("wv1")
        identb = self.cid["ident_b"]
        idbB = self.B("ident_b_s")
        utf = self.cid["ut_f"]
        utfB = self.B("ut_f_s")
        recrow = self.rows["mean"]
        recrowB = self.rr["mean"]
        for hp in range(KC):
            vfm = self.pool[8]
            for tt in range(TT):
                bank = tt % 2
                emits, ins_per = [], []
                for kc2 in range(KC):
                    wv = self.stg[kc2 // 4]
                    emits.append(lambda e, kc2=kc2, hp=hp, tt=tt, bank=bank, wv=wv: e.matmul(
                        PB[bank][:, :],
                        wv[:, (kc2 % 4) * D + hp * 128:(kc2 % 4) * D + hp * 128 + 128],
                        b1[kc2][:, ts(tt, 512)],
                        start=(kc2 == 0), stop=(kc2 == KC - 1)))
                    ins_per.append((stgB[kc2 // 4], b1B[kc2]))
                kb.group("t", emits, outs=(PBB[bank],), ins_per=ins_per)
                kb.op("v", lambda e, tt=tt, bank=bank: e.tensor_tensor(
                    vfm[:, ts(tt, 512)], PB[bank][:, :], self.scr[0][:, ts(tt, 512)], OP.mult),
                    outs=(poolB[8],), ins=(PBB[bank], scrB[0]))
            vs = self.vs2[0]
            vsb = vsB[0]
            kb.op("v", lambda e: e.memset(vs[:], 1.0), outs=(vsb,))
            for tc in range(TC):
                pt = self.pt[tc % 2]
                ptB = self.ptb[tc % 2]
                kb.op("t", lambda e, tc=tc, pt=pt: e.transpose(
                    pt[:, 0:128], vfm[:, ts(tc)], identb[:]),
                    outs=(ptB,), ins=(poolB[8], idbB))
                kb.op("v", lambda e, tc=tc, pt=pt: e.tensor_copy(
                    vs[:, 130 * tc:130 * tc + 64], pt[:, 0:64]),
                    outs=(vsb,), ins=(ptB,))
                kb.op("v", lambda e, tc=tc, pt=pt: e.tensor_copy(
                    vs[:, 130 * tc + 65:130 * tc + 129], pt[:, 64:128]),
                    outs=(vsb,), ins=(ptB,))
            for hl in range(2):
                hs = slice(64 * hl, 64 * hl + 64)
                for j in range(TT):
                    self.prep_pump(3)
                    bO = 2 + (j % 2)
                    n_ck = min(4 * j + 4, TC)
                    for ck in range(n_ck):
                        bA = ck % 2
                        kb.op("t", lambda e, hp=hp, hs=hs, j=j, ck=ck, bA=bA: e.matmul(
                            PB[bA][:, :],
                            qx[hp][hs, slice(T + 128 * ck, T + 128 * ck + 128)],
                            qx[hp][hs, ts(j, 512)],
                            start=True, stop=True),
                            outs=(PBB[bA],), ins=(xsB[hp],))
                        pts = self.vs2[1]
                        po = 512 * (ck % 2)
                        kb.op("s", lambda e, bA=bA, po=po, pts=pts: e.activation(
                            pts[:, po:po + 512], PB[bA][:, :], AF.Exp, bias=negm, scale=0.125),
                            outs=(vsB[1],), ins=(PBB[bA], smB[2]))
                        dz = (ck - 4 * j) * 128
                        if dz > 0:
                            kb.op("v", lambda e, po=po, dz=dz, pts=pts: e.memset(
                                pts[:, po:po + dz], 0.0),
                                outs=(vsB[1],), ins=())
                        if 0 <= dz < 512:
                            kb.op("v", lambda e, po=po, dz=dz, pts=pts: e.tensor_tensor(
                                pts[:, po + dz:po + dz + 128], pts[:, po + dz:po + dz + 128],
                                utf[:], OP.mult),
                                outs=(vsB[1],), ins=(vsB[1], utfB))
                        kb.op("t", lambda e, ck=ck, hl=hl, bO=bO, po=po, pts=pts, n_ck=n_ck: e.matmul(
                            PB[bO][0:65, :], self.vs2[0][:, 130 * ck + 65 * hl:130 * ck + 65 * hl + 65],
                            pts[:, po:po + 512],
                            start=(ck == 0), stop=(ck == n_ck - 1)),
                            outs=(PBB[bO],), ins=(vsB[0], vsB[1]))
                    kb.op("v", lambda e, bO=bO: e.reciprocal(recrow[:, 0:512], PB[bO][64:65, :]),
                          outs=(recrowB,), ins=(PBB[bO],))
                    kb.op("t", lambda e, bO=bO: e.matmul(
                        PB[4][0:64, :], self.cid["ones_f"][0:1, 0:64], recrow[:, 0:512],
                        start=True, stop=True),
                        outs=(PBB[4],), ins=(recrowB, onesfB))
                    kb.op("v", lambda e: e.tensor_copy(self.scr[3][:, 0:512], PB[4][0:64, :][0:64, :])
                          if False else e.tensor_copy(self.scr[3][0:64, 0:512], PB[4][0:64, :]),
                          outs=(scrB[3],), ins=(PBB[4],))
                    kb.op("v", lambda e, hp=hp, hs=hs, j=j, bO=bO: e.tensor_tensor(
                        qx[hp][hs, ts(j, 512)], PB[bO][0:64, :], self.scr[3][0:64, 0:512], OP.mult),
                        outs=(xsB[hp],), ins=(PBB[bO], scrB[3]))

        load_strips("wo1")
        attn2 = qx
        hb = self.vs2[1]
        for oc in range(KC):
            hin = self.scr[1 + (oc % 2)]
            hinB = scrB[1 + (oc % 2)]
            kb.dma("q", hin[:, 0:T], d["xn2_d"][ts(oc), :],
                   outs=(hinB,), ins=(self.B("xn2_d"),))
            for tt in range(TT):
                bank = tt % 2
                emits, ins_per = [], []
                for kc2 in range(KC):
                    wv = self.stg[kc2 // 4]
                    emits.append(lambda e, kc2=kc2, oc=oc, tt=tt, bank=bank, wv=wv: e.matmul(
                        PB[bank][:, :],
                        wv[:, (kc2 % 4) * D + oc * 128:(kc2 % 4) * D + oc * 128 + 128],
                        attn2[kc2][:, ts(tt, 512)],
                        start=(kc2 == 0), stop=(kc2 == KC - 1)))
                    ins_per.append((stgB[kc2 // 4], xsB[kc2]))
                kb.group("t", emits, outs=(PBB[bank],), ins_per=ins_per)
                kb.op("v", lambda e, tt=tt, bank=bank, hin=hin: e.tensor_tensor(
                    hin[:, ts(tt, 512)], hin[:, ts(tt, 512)], PB[bank][:, :], OP.add),
                    outs=(hinB,), ins=(hinB, PBB[bank]))
                kb.op("v", lambda e, tt=tt, hin=hin: e.tensor_copy(
                    hb[:, 0:512], hin[:, ts(tt, 512)]),
                    outs=(vsB[1],), ins=(hinB,))
                kb.op("v", lambda e: e.tensor_tensor(hb[:, 512:1024], hb[:, 0:512], hb[:, 0:512], OP.mult),
                      outs=(vsB[1],), ins=(vsB[1],))
                kb.op("t", lambda e, oc=oc, tt=tt: e.matmul(
                    self.pb[2 + tt][0:1, :],
                    self.cid["ones_b"][:, 0:1], hb[:, 512:1024],
                    start=(oc == 0), stop=(oc == KC - 1)),
                    outs=(self.pbb[2 + tt],), ins=(vsB[1], onesbB))
            kb.dma("s", d["hin_d"][ts(oc), :], hin[:, 0:T], outs=(self.B("hin_d"),), ins=(hinB,))
        self.rms_rows_from_banks()

        b2 = self.pool[0:8]
        for oc in range(KC):
            hin = self.scr[1 + (oc % 2)]
            hinB = scrB[1 + (oc % 2)]
            kb.dma("q", hin[:, 0:T], d["hin_d"][ts(oc), :], outs=(hinB,), ins=(self.B("hin_d"),))
            for tt in range(TT):
                self.bc_rows(tt, rows=("rstd",), banks=(4,))
                kb.op("v", lambda e, oc=oc, tt=tt, hin=hin: e.tensor_tensor(
                    self.scr[3][:, 0:512], hin[:, ts(tt, 512)], self.pb[4][:, :], OP.mult),
                    outs=(scrB[3],), ins=(hinB, self.pbb[4]))
                kb.op("v", lambda e, oc=oc, tt=tt: e.tensor_scalar(
                    b2[oc][:, ts(tt, 512)], self.scr[3][:, 0:512], self.vec("g2c", oc), None, OP.mult),
                    outs=(poolB[oc],), ins=(scrB[3], self.B("vecs")))

        self.prep_pump(10000)
        self.emit_router(b2, poolB)

        self.emit_moe(b2, poolB)

    def emit_b2(self):
        kb, T, TC, TT = self.kb, self.T, self.TC, self.TT
        d = self.dram
        xsB = [self.B(f"xs{k}") for k in range(KC)]
        scrB = [self.B(f"scr{i}") for i in range(4)]
        stgB = [self.B(f"stg{i}") for i in range(3)]
        poolB = [self.B(f"pl{i}") for i in range(12)]
        vsB = [self.B("vs0"), self.B("vs1")]
        smB = [self.B(f"sm{i}") for i in range(4)]
        PB, PBB = self.pb, self.pbb
        sm3 = self.small[3]
        for wn in ("Wr", "Wk", "Wv", "Wo"):
            kb.dma("g", d[wn + "_bf"], d[wn], outs=(self.B(wn + "_bf"),))
        self.load_vec("q", "decc", d["decc"], 8)
        self.load_vec("q", "lnwc", d["lnwc"], 8)
        kb.op("s", lambda e: e.activation(sm3[:, 0:8], self.vec("decc"), AF.Sigmoid),
              outs=(smB[3],), ins=(self.B("vecs"),))
        kb.op("v", lambda e: e.tensor_scalar(sm3[:, 0:8], sm3[:, 0:8], 1e-7, None, OP.max),
              outs=(smB[3],), ins=(smB[3],))
        kb.op("s", lambda e: e.activation(sm3[:, 8:16], sm3[:, 0:8], AF.Ln),
              outs=(smB[3],), ins=(smB[3],))

        self.ln_stats("ln3", 1e-5)
        xn3 = self.pool[0:8]
        for k in range(KC):
            kb.dma("s", d["s2_d"][ts(k), :], self.xs[k][:], outs=(self.B("s2_d"),), ins=(xsB[k],))
        for tt in range(TT):
            self.bc_rows(tt)
            mb, rb = self.scr[0], self.scr[1]
            kb.op("v", lambda e: e.tensor_copy(mb[:, 0:512], self.pb[4][:, :]),
                  outs=(scrB[0],), ins=(self.pbb[4],))
            kb.op("v", lambda e: e.tensor_copy(rb[:, 0:512], self.pb[5][:, :]),
                  outs=(scrB[1],), ins=(self.pbb[5],))
            for k in range(KC):
                tmp = self.scr[2]
                kb.op("g", lambda e, k=k, tt=tt: e.tensor_tensor(
                    tmp[:, ts(tt, 512)], self.xs[k][:, ts(tt, 512)], mb[:, 0:512], OP.subtract),
                    outs=(scrB[2],), ins=(xsB[k], scrB[0]))
                kb.op("v", lambda e, k=k, tt=tt: e.tensor_tensor(
                    xn3[k][:, ts(tt, 512)], tmp[:, ts(tt, 512)], rb[:, 0:512], OP.mult),
                    outs=(poolB[k],), ins=(scrB[2], scrB[1]))

        Wr_bf, Wk_bf, Wv_bf = d["Wr_bf"], d["Wk_bf"], d["Wv_bf"]
        for oc in range(KC):
            stg = self.stg[oc % 2]
            sB = stgB[oc % 2]
            for pi, wbf, wB in ((0, Wr_bf, self.B("Wr_bf")), (1, Wk_bf, self.B("Wk_bf")),
                                (2, Wv_bf, self.B("Wv_bf"))):
                for kc2 in range(KC):
                    kb.dmat("q",
                            stg[:, pi * 1024 + 128 * kc2:pi * 1024 + 128 * kc2 + 128],
                            wbf[oc * 128:oc * 128 + 128, ts(kc2)],
                            outs=(sB,), ins=(wB,))
            for tt in range(TT):
                for pi, bank in ((0, 0), (1, 1), (2, 2)):
                    emits, ins_per = [], []
                    for kc2 in range(KC):
                        emits.append(lambda e, kc2=kc2, tt=tt, pi=pi, bank=bank, stg=stg: e.matmul(
                            PB[bank][:, :], stg[:, pi * 1024 + 128 * kc2:pi * 1024 + 128 * kc2 + 128],
                            xn3[kc2][:, ts(tt, 512)],
                            start=(kc2 == 0), stop=(kc2 == KC - 1)))
                        ins_per.append((sB, poolB[kc2]))
                    kb.group("t", emits, outs=(PBB[bank],), ins_per=ins_per)
                kb.op("s", lambda e, tt=tt: e.activation(
                    self.scr[3][:, ts(tt, 512)], PB[0][:, :], AF.Sigmoid),
                    outs=(scrB[3],), ins=(PBB[0],))
                kb.op("v", lambda e, tt=tt: e.tensor_copy(self.scr[2][:, ts(tt, 512)], PB[1][:, :]),
                      outs=(scrB[2],), ins=(PBB[1],))
                kb.op("v", lambda e, tt=tt: e.tensor_tensor(
                    self.scr[2][:, ts(tt, 512)], self.scr[2][:, ts(tt, 512)], PB[2][:, :], OP.mult),
                    outs=(scrB[2],), ins=(scrB[2], PBB[2]))
            kb.dma("q", self.scr[0][:], d["t_bc"], outs=(scrB[0],))
            ldap = sm3[:, 8 + oc:9 + oc]
            kb.op("v", lambda e, ldap=ldap: e.tensor_scalar(
                self.scr[0][:], self.scr[0][:], ldap, None, OP.mult),
                outs=(scrB[0],), ins=(scrB[0], smB[3]))
            kb.op("v", lambda e: e.tensor_scalar(self.scr[1][:], self.scr[0][:], -1.0, 23.1,
                                                 OP.mult, OP.min),
                  outs=(scrB[1],), ins=(scrB[0],))
            kb.op("s", lambda e: e.activation(self.scr[1][:], self.scr[1][:], AF.Exp),
                  outs=(scrB[1],), ins=(scrB[1],))
            kb.op("v", lambda e: e.tensor_scalar(self.scr[1][:], self.scr[1][:], 1e10, None, OP.min),
                  outs=(scrB[1],), ins=(scrB[1],))
            kb.op("s", lambda e: e.activation(self.scr[0][:], self.scr[0][:], AF.Exp),
                  outs=(scrB[0],), ins=(scrB[0],))
            kb.op("v", lambda e: e.tensor_tensor(
                self.scr[2][:], self.scr[2][:], self.scr[1][:], OP.mult),
                outs=(scrB[2],), ins=(scrB[2], scrB[1]))
            kb.op("v", lambda e: e.tensor_tensor_scan(
                self.scr[2][:], self.scr[2][:], self.scr[2][:], 0.0, OP.add, OP.bypass),
                outs=(scrB[2],), ins=(scrB[2],))
            kb.op("v", lambda e: e.tensor_tensor(
                self.scr[2][:], self.scr[2][:], self.scr[0][:], OP.mult),
                outs=(scrB[2],), ins=(scrB[2], scrB[0]))
            kb.op("v", lambda e, oc=oc: e.tensor_tensor(
                self.xs[oc][:], self.scr[2][:], self.scr[3][:], OP.mult),
                outs=(xsB[oc],), ins=(scrB[2], scrB[3]))

        if getattr(self, "b2_stop", None) == "rs":
            return
        self.ln_stats("rms_l1", 1e-6, rms=True, src=self.xs, src_bufs=xsB)
        ybf = self.pool[0:8]
        for oc in range(KC):
            for tt in range(TT):
                self.bc_rows(tt, rows=("rstd",), banks=(4,))
                kb.op("v", lambda e, oc=oc, tt=tt: e.tensor_tensor(
                    self.scr[2][:, 0:512], self.xs[oc][:, ts(tt, 512)], self.pb[4][:, :], OP.mult),
                    outs=(scrB[2],), ins=(xsB[oc], self.pbb[4]))
                kb.op("v", lambda e, oc=oc, tt=tt: e.tensor_scalar(
                    ybf[oc][:, ts(tt, 512)], self.scr[2][:, 0:512], self.vec("lnwc", oc), None, OP.mult),
                    outs=(poolB[oc],), ins=(scrB[2], self.B("vecs")))

        for kc2 in range(KC):
            dst = self.stg[kc2 // 4]
            kb.dmat("q", dst[:, (kc2 % 4) * D:(kc2 % 4) * D + D],
                    d["Wo_bf"][:, ts(kc2)], outs=(stgB[kc2 // 4],), ins=(self.B("Wo_bf"),))
        for oc in range(KC):
            s2t = self.scr[oc % 2]
            s2tB = scrB[oc % 2]
            kb.dma("q", s2t[:], d["s2_d"][ts(oc), :], outs=(s2tB,), ins=(self.B("s2_d"),))
            for tt in range(TT):
                bank = tt % 4
                emits, ins_per = [], []
                for kc2 in range(KC):
                    wv = self.stg[kc2 // 4]
                    emits.append(lambda e, kc2=kc2, oc=oc, tt=tt, bank=bank, wv=wv: e.matmul(
                        PB[bank][:, :],
                        wv[:, (kc2 % 4) * D + oc * 128:(kc2 % 4) * D + oc * 128 + 128],
                        ybf[kc2][:, ts(tt, 512)],
                        start=(kc2 == 0), stop=(kc2 == KC - 1)))
                    ins_per.append((stgB[kc2 // 4], poolB[kc2]))
                kb.group("t", emits, outs=(PBB[bank],), ins_per=ins_per)
                kb.op("v", lambda e, oc=oc, tt=tt, bank=bank, s2t=s2t: e.tensor_tensor(
                    self.xs[oc][:, ts(tt, 512)], s2t[:, ts(tt, 512)], PB[bank][:, :], OP.add),
                    outs=(xsB[oc],), ins=(s2tB, PBB[bank]))

    def emit_b3(self):
        kb, T, TC, TT = self.kb, self.T, self.TC, self.TT
        d = self.dram
        xsB = [self.B(f"xs{k}") for k in range(KC)]
        scrB = [self.B(f"scr{i}") for i in range(4)]
        stgB = [self.B(f"stg{i}") for i in range(3)]
        poolB = [self.B(f"pl{i}") for i in range(12)]
        PB, PBB = self.pb, self.pbb
        for wn, src_n in (("w1_bf", "w1w"), ("w2_bf", "w2w"), ("w3_bf", "w3w")):
            kb.dma("g", d[wn], d[src_n], outs=(self.B(wn),))
        for wn, src_n in (("h1_bf", "h1w"), ("h2_bf", "h2w")):
            for i in range(4):
                kb.dma("g", d[wn][ts(i, 1024), :], d[src_n][ts(i, 1024), :], outs=(self.B(wn),))
        self.load_vec("q", "s1c", d["s1c"], 32)
        self.load_vec("q", "s2c", d["s2c"], 32)
        self.load_vec("q", "hs1c", d["hs1c"], 32)
        self.load_vec("q", "hs2c", d["hs2c"], 32)
        self.load_vec("q", "s3c", d["s3c"], 8)

        self.ln_stats("ln4", 1e-5)
        for k in range(KC):
            kb.dma("s", d["s3_d"][ts(k), :], self.xs[k][:], outs=(self.B("s3_d"),), ins=(xsB[k],))

        slots = []
        for i in range(3):
            for j in range(8):
                slots.append((self.stg[i][:, 512 * j:512 * j + 512], stgB[i]))
        for pi in (10, 11):
            for j in range(4):
                slots.append((self.pool[pi][:, 512 * j:512 * j + 512], poolB[pi]))
        self.slot_i = 0

        def next_slot():
            s = slots[self.slot_i % len(slots)]
            self.slot_i += 1
            return s

        def hx(i):
            return self.xs[i // 8].bitcast(BF16)[:, (i % 8) * 512:(i % 8) * 512 + 512], xsB[i // 8]

        def gx(i):
            return self.xs[4 + i // 8].bitcast(BF16)[:, (i % 8) * 512:(i % 8) * 512 + 512], xsB[4 + i // 8]

        def big_gemm(wbf_name, n_out, n_k, rhs_fn, epi, tq):
            wbf = d[wbf_name]
            wB = self.B(wbf_name)
            for ocg in range((n_out + 3) // 4):
                ocs = [ocg * 4 + i for i in range(4) if ocg * 4 + i < n_out]
                burst_slots = {}
                for kc0 in range(0, n_k, 8):
                    cur = []
                    for kc2 in range(kc0, min(kc0 + 8, n_k)):
                        sl, slB = next_slot()
                        kb.dmat("q", sl,
                                wbf[ocg * 512:ocg * 512 + 512, ts(kc2)],
                                outs=(slB,), ins=(wB,))
                        cur.append((kc2, sl, slB))
                    for oci, oc in enumerate(ocs):
                        for bi, (kc2, sl, slB) in enumerate(cur):
                            rhs, rhsB = rhs_fn(kc2)
                            kb.op("t", lambda e, sl=sl, oci=oci, oc=oc, rhs=rhs, kc2=kc2, n_k=n_k: e.matmul(
                                PB[oci][:, :], sl[:, 128 * (oc % 4):128 * (oc % 4) + 128], rhs,
                                start=(kc2 == 0), stop=(kc2 == n_k - 1)),
                                outs=(PBB[oci],), ins=(slB, rhsB))
                for oci, oc in enumerate(ocs):
                    epi(oc, oci)

        identf = self.cid["ident_f"]
        idB = self.B("ident_f_s")
        for tt in range(TT):
            self.bc_rows(tt)
            mb, rb = self.scr[0], self.scr[1]
            kb.op("v", lambda e: e.tensor_copy(mb[:, 0:512], self.pb[4][:, :]),
                  outs=(scrB[0],), ins=(self.pbb[4],))
            kb.op("v", lambda e: e.tensor_copy(rb[:, 0:512], self.pb[5][:, :]),
                  outs=(scrB[1],), ins=(self.pbb[5],))
            for kc2 in range(KC):
                tmp = self.scr[2]
                kb.op("g", lambda e, kc2=kc2, tt=tt: e.tensor_tensor(
                    tmp[:, ts(tt, 512)], self.xs[kc2][:, ts(tt, 512)], mb[:, 0:512], OP.subtract),
                    outs=(scrB[2],), ins=(xsB[kc2], scrB[0]))
                kb.op("v", lambda e, kc2=kc2, tt=tt: e.tensor_tensor(
                    self.pool[kc2][:, ts(tt, 512)], tmp[:, ts(tt, 512)], rb[:, 0:512], OP.mult),
                    outs=(poolB[kc2],), ins=(scrB[2], scrB[1]))
        for tq in range(TT):
            q0 = tq * 512

            def xn4(kc2, q0=q0):
                return (self.pool[kc2][:, q0:q0 + 512], poolB[kc2])
            if getattr(self, "b2_stop", None) == "xn4":
                return
            def epi_w1(oc, bank):
                sgt, sgtB = gx(oc)
                kb.op("s", lambda e, oc=oc, bank=bank, sgt=sgt: e.activation(
                    sgt, PB[bank][:, :], AF.Silu, scale=self.vec("s1c", oc)),
                    outs=(sgtB,), ins=(PBB[bank], self.B("vecs")))
            def epi_w2(oc, bank):
                sgt, sgtB = gx(oc)
                ht, htB = hx(oc)
                kb.op("v", lambda e, oc=oc, bank=bank: e.tensor_scalar(
                    self.scr[2][:, 0:512], PB[bank][:, :], self.vec("s2c", oc), None, OP.mult),
                    outs=(scrB[2],), ins=(PBB[bank], self.B("vecs")))
                kb.op("v", lambda e, ht=ht, sgt=sgt: e.tensor_tensor(
                    ht, sgt, self.scr[2][:, 0:512], OP.mult),
                    outs=(htB,), ins=(sgtB, scrB[2]))
            big_gemm("w1_bf", 32, 8, xn4, epi_w1, tq)
            big_gemm("w2_bf", 32, 8, xn4, epi_w2, tq)
            if getattr(self, "b2_stop", None) == "h":
                return

            def epi_h1(oc, bank):
                gt, gtB = gx(oc)
                kb.op("s", lambda e, oc=oc, bank=bank, gt=gt: e.activation(
                    gt, PB[bank][:, :], AF.Gelu, scale=self.vec("hs1c", oc)),
                    outs=(gtB,), ins=(PBB[bank], self.B("vecs")))
            big_gemm("h1_bf", 32, 32, hx, epi_h1, tq)
            if getattr(self, "b2_stop", None) == "g":
                return

            def epi_h2(oc, bank):
                ht, htB = hx(oc)
                kb.op("s", lambda e, oc=oc, bank=bank, ht=ht: e.activation(
                    ht, PB[bank][:, :], AF.Gelu, scale=self.vec("hs2c", oc)),
                    outs=(htB,), ins=(PBB[bank], self.B("vecs")))
            big_gemm("h2_bf", 32, 32, gx, epi_h2, tq)

            def epi_w3(oc, bank):
                s3t = self.scr[1 + (oc % 2)]
                s3tB = scrB[1 + (oc % 2)]
                kb.dma("q", s3t[:, 0:512], d["s3_d"][ts(oc), q0:q0 + 512],
                       outs=(s3tB,), ins=(self.B("s3_d"),))
                kb.op("v", lambda e, oc=oc, bank=bank, s3t=s3t: e.scalar_tensor_tensor(
                    self.scr[3][:, 0:512], PB[bank][:, :], self.vec("s3c", oc), s3t[:, 0:512],
                    OP.mult, OP.add),
                    outs=(scrB[3],), ins=(PBB[bank], self.B("vecs"), s3tB))
                if self.io_out == "fm":
                    kb.dma("s", d["y_out_fm"][ts(oc), q0:q0 + 512], self.scr[3][:, 0:512],
                           outs=(self.B("y_out_fm"),), ins=(scrB[3],))
                else:
                    def stage(tb):
                        if tb < 2:
                            return (self.pool[8 + tb].bitcast(F32)[:, 0:1024],
                                    poolB[8 + tb])
                        return self.scr[0][:, 1024 * (tb - 2):1024 * (tb - 2) + 1024], scrB[0]
                    for tb in range(4):
                        bb = 4 + tb % 2
                        stg_ap, stg_b = stage(tb)
                        kb.op("t", lambda e, tb=tb, bb=bb: e.transpose(
                            PB[bb][:, 0:128], self.scr[3][:, 128 * tb:128 * tb + 128], identf[:]),
                            outs=(PBB[bb],), ins=(scrB[3], idB))
                        kb.op("v", lambda e, oc=oc, bb=bb, stg_ap=stg_ap: e.tensor_copy(
                            stg_ap[:, 128 * oc:128 * oc + 128], PB[bb][:, 0:128]),
                            outs=(stg_b,), ins=(PBB[bb],))
                        if oc == KC - 1:
                            kb.dma("s", d["y_out"][q0 + 128 * tb:q0 + 128 * tb + 128, :],
                                   stg_ap[:, 0:1024],
                                   outs=(self.B("y_out"),), ins=(stg_b,))
            big_gemm("w3_bf", 8, 32, hx, epi_w3, tq)

    def finalize(self):
        kb = self.kb
        for sn, c in list(kb.cnt.items()):
            if sn.startswith("D") and c > 0:
                kb._wait("s", sn, c)


def build_program(T=2048, blocks=("b0",), io_in="tm", io_out="fm", b2_stop=None):
    nc = bass.Bass("TRN2", target_bir_lowering=False)
    p = Prog(nc, T, blocks, io_in, io_out)
    p.b2_stop = b2_stop
    p.declare()
    for e in ("t", "v", "s", "g"):
        p.kb.sem(e)
    p.emit_input()
    if "b0" in blocks:
        p.emit_b0()
    if "b1" in blocks:
        p.emit_b1()
    if "b2" in blocks:
        p.emit_b2()
    if "b3" in blocks:
        p.emit_b3()
    if "b3" not in blocks:
        p.emit_output()
    p.finalize()

    kb = p.kb
    def replay(eng, name):
        for entry in kb.q[name]:
            if entry[0] == "w":
                eng.wait_ge(kb.sems[entry[1]], entry[2])
            else:
                inst = entry[1](eng)
                if entry[2] is not None:
                    inst.then_inc(kb.sems[entry[2]], entry[3])

    with nc.Block() as block:
        @block.sync
        def _(sync):
            replay(sync, "q")

        @block.tensor
        def _(tensor):
            replay(tensor, "t")

        @block.vector
        def _(vector):
            replay(vector, "v")

        @block.scalar
        def _(scalar):
            replay(scalar, "s")

        @block.gpsimd
        def _(gpsimd):
            replay(gpsimd, "g")
    kb.st.close()
    return nc, p


def host_consts(T):
    c = {}
    c["ident_f"] = np.eye(128, dtype=np.float32)
    c["ident_b"] = np.eye(128, dtype=np.float32).astype(ml_dtypes.bfloat16)
    c["ones_f"] = np.ones((128, 128), np.float32)
    c["ones_b"] = np.ones((128, 128), np.float32).astype(ml_dtypes.bfloat16)
    c["ut_f"] = np.triu(np.ones((128, 128), np.float32))
    R = np.zeros((64, 64), np.float32)
    for m in range(32):
        R[m + 32, m] = -1.0
    for m in range(32, 64):
        R[m - 32, m] = 1.0
    R2 = np.zeros((128, 128), np.float32)
    R2[0:64, 0:64] = R
    R2[64:128, 64:128] = R
    c["R2"] = R2.astype(ml_dtypes.bfloat16)
    dim = 64
    inv_freq = 1.0 / (10000.0 ** (np.arange(0, dim, 2, dtype=np.float32) / dim))
    tpos = np.arange(T, dtype=np.float32)
    freqs = np.einsum("i,j->ij", tpos, inv_freq)
    emb = np.concatenate([freqs, freqs], axis=-1)
    cos = np.cos(emb).T.astype(np.float32)
    sin = np.sin(emb).T.astype(np.float32)
    c["cos_t"] = np.concatenate([cos, cos], axis=0)
    c["sin_t"] = np.concatenate([sin, sin], axis=0)
    c["t_bc"] = np.broadcast_to(np.arange(T, dtype=np.float32), (128, T)).copy()
    sel = np.zeros((8, 1024), np.float32)
    for e in range(8):
        sel[e, 128 * e:128 * (e + 1)] = 1.0
    c["sel8"] = sel.astype(ml_dtypes.bfloat16)
    cc = np.zeros((128, 8), np.float32)
    cc[:, 0] = 1e-5
    cc[:, 1] = 1e-6
    cc[:, 3] = 1.0
    c["constc"] = cc
    return c


def vec_cols(v, groups, width=128):
    return np.ascontiguousarray(v.reshape(groups, width).T.astype(np.float32))


def _f32(a):
    return np.ascontiguousarray(np.asarray(a, dtype=np.float32))


def block_weight_map(blk, inputs):
    m = {}
    if blk == "b0":
        m["w_qkv"] = _f32(inputs["L0_attn_w_qkv"])
        m["bqc"] = vec_cols(_f32(inputs["L0_attn_b_qkv"]), 24)
        m["w_out"] = _f32(inputs["L0_attn_w_out"])
        m["boc"] = vec_cols(_f32(inputs["L0_attn_b_out"]), 8)
    elif blk == "b1":
        m["wq1"] = _f32(inputs["L0_ffn_wq"])
        m["wk1"] = _f32(inputs["L0_ffn_wk"])
        m["wv1"] = _f32(inputs["L0_ffn_wv"])
        m["wo1"] = _f32(inputs["L0_ffn_wo"])
        m["w_router"] = _f32(inputs["L0_ffn_w_router"])
        m["wg"] = _f32(inputs["L0_ffn_wg"])
        m["wu"] = _f32(inputs["L0_ffn_wu"])
        m["wd"] = _f32(inputs["L0_ffn_wd"])
        m["g1c"] = vec_cols(_f32(inputs["L0_ffn_g1"]), 8)
        m["g2c"] = vec_cols(_f32(inputs["L0_ffn_g2"]), 8)
    elif blk == "b2":
        m["Wr"] = _f32(inputs["L1_attn_Wr"])
        m["Wk"] = _f32(inputs["L1_attn_Wk"])
        m["Wv"] = _f32(inputs["L1_attn_Wv"])
        m["Wo"] = _f32(inputs["L1_attn_Wo"])
        m["decc"] = vec_cols(_f32(inputs["L1_attn_decay"]), 8)
        m["lnwc"] = vec_cols(_f32(inputs["L1_attn_ln_w"]), 8)
    elif blk == "b3":
        m["w1w"] = _f32(inputs["L1_ffn_w1_w"])
        m["w2w"] = _f32(inputs["L1_ffn_w2_w"])
        m["h1w"] = _f32(inputs["L1_ffn_h1_w"])
        m["h2w"] = _f32(inputs["L1_ffn_h2_w"])
        m["w3w"] = _f32(inputs["L1_ffn_w3_w"])
        m["s1c"] = vec_cols(_f32(inputs["L1_ffn_w1_s"]), 32)
        m["s2c"] = vec_cols(_f32(inputs["L1_ffn_w2_s"]), 32)
        m["hs1c"] = vec_cols(_f32(inputs["L1_ffn_h1_s"]), 32)
        m["hs2c"] = vec_cols(_f32(inputs["L1_ffn_h2_s"]), 32)
        m["s3c"] = vec_cols(_f32(inputs["L1_ffn_w3_s"]), 8)
    return m


FUSED = True


def kernel(**inputs):
    T = 2048
    n = 8
    consts = host_consts(T)
    x0 = _f32(inputs["L0_attn_x"])
    if FUSED:
        nc, p = build_program(T=T, blocks=("b0", "b1", "b2", "b3"), io_in="tm", io_out="tm")
        wmap = {}
        for blk in ("b0", "b1", "b2", "b3"):
            wmap.update(block_weight_map(blk, inputs))
        in_maps = []
        for c in range(n):
            m = dict(consts)
            m.update(wmap)
            m["x_in"] = np.ascontiguousarray(x0[c])
            in_maps.append(m)
        res = run_bass_kernel_spmd(nc, in_maps, core_ids=list(range(n)))
        return np.stack([np.ascontiguousarray(np.asarray(res.results[c]["y_out"], dtype=np.float32))
                         for c in range(n)], axis=0)
    cur = [np.ascontiguousarray(x0[c]) for c in range(n)]
    specs = [("b0", "tm", "fm"), ("b1", "fm", "fm"), ("b2", "fm", "fm"), ("b3", "fm", "tm")]
    for blk, ioi, ioo in specs:
        nc, p = build_program(T=T, blocks=(blk,), io_in=ioi, io_out=ioo)
        wmap = block_weight_map(blk, inputs)
        in_maps = []
        for c in range(n):
            m = dict(consts)
            m.update(wmap)
            m["x_in" if ioi == "tm" else "x_in_fm"] = cur[c]
            in_maps.append(m)
        res = run_bass_kernel_spmd(nc, in_maps, core_ids=list(range(n)))
        key = "y_out" if ioo == "tm" else "y_out_fm"
        cur = [np.ascontiguousarray(np.asarray(res.results[c][key], dtype=np.float32))
               for c in range(n)]
    return np.stack(cur, axis=0)
